# version 08
# baseline (speedup 1.0000x reference)
import os
from contextlib import ExitStack

import numpy as np
import ml_dtypes
import concourse.bass as bass
import concourse.mybir as mybir
from concourse.bass_utils import run_bass_kernel_spmd

F32 = mybir.dt.float32
BF16 = mybir.dt.bfloat16
AF = mybir.ActivationFunctionType
ALU = mybir.AluOpType
AX = mybir.AxisListType

NCORES = 8
T = 2048
D = 2048
NE = 8
NTT = 16
NDC = 16
NHC = 16
NFL = 16
EPS = 1e-5

MAGIC = float(np.float32(1.5 * 2 ** 23))
INV2PI = float(np.float32(1.0 / (2.0 * np.pi)))
TWOPI = float(np.float32(2.0 * np.pi))
HALFPI = float(np.float32(np.pi / 2.0))
NEG = -1.0e30

ENGINES = ("sync", "scalar", "vector", "tensor", "gpsimd")


class Sem:
    __slots__ = ("h", "n")

    def __init__(self, h):
        self.h = h
        self.n = 0


class Sched:
    def __init__(self):
        self.ops = {e: [] for e in ENGINES}

    def emit(self, eng, fn, waits=(), incs=()):
        for s, v in waits:
            assert v <= s.n, f"deadlock: wait {v} > emitted {s.n} on {s.h.name}"
        lowered = (fn, [(s.h, v) for s, v in waits if v > 0], [])
        for s, k in incs:
            s.n += k
            lowered[2].append((s.h, k))
        self.ops[eng].append(lowered)

    def build(self, block):
        def make(name):
            ops = self.ops[name]

            def body(eng):
                for fn, waits, incs in ops:
                    for h, v in waits:
                        eng.wait_ge(h, v)
                    instr = fn(eng)
                    for h, k in incs:
                        if instr is None:
                            instr = eng.nop()
                        instr = instr.then_inc(h, k)
            return body

        for name in ENGINES:
            if self.ops[name]:
                getattr(block, name)(make(name))

    def counts(self):
        return {e: len(self.ops[e]) for e in ENGINES}


def build_nc():
    nc = bass.Bass("TRN2", target_bir_lowering=False)

    x0_in = nc.dram_tensor("x0_in", [T, D], F32, kind="ExternalInput").ap()
    wp_in = nc.dram_tensor("wp_in", [4, 128, 16 * 1024], F32, kind="ExternalInput").ap()
    b_in = nc.dram_tensor("b_in", [1, 4096], F32, kind="ExternalInput").ap()
    gw_in = nc.dram_tensor("gw_in", [128, 128], F32, kind="ExternalInput").ap()
    w1_in = nc.dram_tensor("w1_in", [NE, 32, 128, 2048], F32, kind="ExternalInput").ap()
    w3_in = nc.dram_tensor("w3_in", [NE, 32, 128, 2048], F32, kind="ExternalInput").ap()
    w2_in = nc.dram_tensor("w2_in", [NE, 2, NHC, 128, 2048], F32, kind="ExternalInput").ap()
    cs_in = nc.dram_tensor("cs_in", [2, NTT, 128, 256], F32, kind="ExternalInput").ap()
    sn_in = nc.dram_tensor("sn_in", [2, NTT, 128, 256], F32, kind="ExternalInput").ap()
    idb_in = nc.dram_tensor("idb_in", [128, 128], BF16, kind="ExternalInput").ap()
    idf_in = nc.dram_tensor("idf_in", [128, 128], F32, kind="ExternalInput").ap()
    onc_in = nc.dram_tensor("onc_in", [128, 1], F32, kind="ExternalInput").ap()
    onr_in = nc.dram_tensor("onr_in", [1, 128], F32, kind="ExternalInput").ap()
    onb_in = nc.dram_tensor("onb_in", [1, 128], BF16, kind="ExternalInput").ap()
    oh_in = nc.dram_tensor("oh_in", [8, 8 * 128], BF16, kind="ExternalInput").ap()
    out_d = nc.dram_tensor("out", [T, D], F32, kind="ExternalOutput").ap()
    x1_d = nc.dram_tensor("x1_d", [NTT, 128, D], F32, kind="Internal").ap()
    l2t_d = nc.dram_tensor("l2t_d", [NDC, NTT, 128, 128], BF16, kind="Internal").ap()
    ar_src_h = nc.dram_tensor("ar_src", [128, 16], F32, kind="Internal")
    ar_dst_h = nc.dram_tensor("ar_dst", [128, 16], F32, kind="Internal")
    ar_src_flat = ar_src_h.ap().rearrange("p g -> (p g)").unsqueeze(0)
    ar_dst_flat = ar_dst_h.ap().rearrange("p g -> (p g)").unsqueeze(0)

    dbg = bool(os.environ.get("KDBG"))
    dbgt = {}
    if dbg:
        dbgt["afin"] = nc.dram_tensor("dbg_afin", [1, 2048], F32, kind="ExternalOutput").ap()
        dbgt["stats"] = nc.dram_tensor("dbg_stats", [128, 192], F32, kind="ExternalOutput").ap()
        dbgt["scores"] = nc.dram_tensor("dbg_scores", [128, 128], F32, kind="ExternalOutput").ap()
        dbgt["coef"] = nc.dram_tensor("dbg_coef", [128, 128], F32, kind="ExternalOutput").ap()
        dbgt["coefT"] = nc.dram_tensor("dbg_coefT", [8, 2048], BF16, kind="ExternalOutput").ap()
        dbgt["ln1T"] = nc.dram_tensor("dbg_ln1T", [128, NDC * T], BF16, kind="ExternalOutput").ap()
        dbgt["x1"] = nc.dram_tensor("dbg_x1", [NTT, 128, D], F32, kind="ExternalOutput").ap()
        dbgt["l2t"] = nc.dram_tensor("dbg_l2t", [NDC, NTT, 128, 128], BF16,
                                     kind="ExternalOutput").ap()
        dbgt["yac"] = nc.dram_tensor("dbg_yac", [128, NHC * 1024], BF16,
                                     kind="ExternalOutput").ap()
        dbgt["yach0"] = nc.dram_tensor("dbg_yach0", [128, NHC * 1024], BF16,
                                       kind="ExternalOutput").ap()
        dbgt["cb0"] = nc.dram_tensor("dbg_cb0", [128, 1024], F32, kind="ExternalOutput").ap()
        dbgt["hp"] = nc.dram_tensor("dbg_hp", [128, NFL * 1024], BF16,
                                    kind="ExternalOutput").ap()

    sch = Sched()
    E = sch.emit

    with ExitStack() as stk:
        en = stk.enter_context

        def sem(name):
            return Sem(en(nc.semaphore(name)))

        SX = [sem("sx0"), sem("sx1")]
        STB = [sem("stb0"), sem("stb1")]
        SWB = [sem("swb0"), sem("swb1")]
        SW1 = [sem("sw1a"), sem("sw1b")]
        SW3 = [sem("sw3a"), sem("sw3b")]
        SW2 = [sem("sw2a"), sem("sw2b")]
        ST2 = [sem("st2a"), sem("st2b")]
        SLN = sem("sln")
        SCQ = sem("scq")
        SST = sem("sst")
        SARL = sem("sarl")
        SCS = sem("scs")
        SCG = sem("scg")
        CO = sem("co")
        M = sem("m")
        V = sem("v")
        A_ = sem("a")

        P = [en(nc.psum_tensor(f"ps{i}", [128, 512], F32)) for i in range(6)]
        PT = [en(nc.psum_tensor("pt0", [128, 1024], BF16)),
              en(nc.psum_tensor("pt1", [128, 1024], BF16))]
        psA = P[4][0:1, 0:512]
        psS = P[5][:, 0:8]
        psC = P[5][0:8, 128:256]
        psfree = {}

        def ps_wait(key):
            w = psfree.get(key)
            return [w] if w else []

        stats = en(nc.sbuf_tensor("stats", [128, 16 * 12], F32))

        def stc(q, tt=None):
            if tt is None:
                return stats[:, q * 16:(q + 1) * 16]
            return stats[:, q * 16 + tt:q * 16 + tt + 1]
        scores_sb = en(nc.sbuf_tensor("scores", [128, 128], F32))
        coefT = en(nc.sbuf_tensor("coefT", [8, 2048], BF16))
        xsl = [en(nc.sbuf_tensor("xs0", [128, D], F32)),
               en(nc.sbuf_tensor("xs1", [128, D], F32))]
        idb = en(nc.sbuf_tensor("idb", [128, 128], BF16))
        oh = en(nc.sbuf_tensor("oh", [8, 8 * 128], BF16))
        cpi2 = en(nc.sbuf_tensor("cpi2", [128, 1], F32))
        ceps = en(nc.sbuf_tensor("ceps", [128, 1], F32))

        sA = ExitStack()
        enA = sA.enter_context
        ln1T = enA(nc.sbuf_tensor("ln1T", [128, NDC * T], BF16))
        WbT = enA(nc.sbuf_tensor("Wb", [128, 2 * NDC * 1024], BF16))
        Wbs = [WbT[:, 0:NDC * 1024], WbT[:, NDC * 1024:2 * NDC * 1024]]
        ATMP = enA(nc.sbuf_tensor("ATMP", [128, 4096], F32))
        KH = ATMP[:, 0:512]
        G1 = ATMP[:, 512:1024]
        G2 = ATMP[:, 1024:1536]
        RK = ATMP[:, 1536:2048]
        RB = ATMP[:, 2048:2560]
        CKb = ATMP[:, 2560:3072]
        CVb = ATMP[:, 3072:3584]
        PSb = ATMP[:, 3584:4096]
        PSbs = [PSb, RB]
        psb_free = {}
        A_sb = enA(nc.sbuf_tensor("A_sb", [1, 2048], F32))
        A_fin = ATMP[0:1, 0:2048]
        A_bc = enA(nc.sbuf_tensor("A_bc", [128, 2048], F32))
        tabs = enA(nc.sbuf_tensor("tabs", [128, 2 * 256], F32))

        def tab(kind, slot):
            return tabs[:, kind * 256:(kind + 1) * 256]
        ln_f32 = enA(nc.sbuf_tensor("ln_f32", [128, D], F32))
        u_buf = ATMP[:, 0:2048]
        ln_bf = enA(nc.sbuf_tensor("ln_bf", [128, D], BF16))
        b_sb = enA(nc.sbuf_tensor("b_sb", [1, 4096], BF16))
        gwT = enA(nc.sbuf_tensor("gwT", [128, 128], BF16))
        onc = enA(nc.sbuf_tensor("onc", [128, 1], F32))
        onr = enA(nc.sbuf_tensor("onr", [1, 128], F32))
        onb = enA(nc.sbuf_tensor("onb", [1, 128], BF16))
        idf = enA(nc.sbuf_tensor("idf", [128, 128], F32))
        t2t = [enA(nc.sbuf_tensor("t2t0", [128, 128], BF16)),
               enA(nc.sbuf_tensor("t2t1", [128, 128], BF16))]
        gT1 = ATMP[:, 0:128]
        gE1 = ATMP[:, 128:256]
        gAB = ATMP[:, 256:384]
        gU1 = ATMP[:, 384:512]
        gS2 = ATMP[:, 512:640]
        gTO = ATMP[:, 640:768]
        gE2 = ATMP[:, 768:896]
        gCM = ATMP[:, 896:1024]
        gst = ATMP[:, 1024:1216]

        def gss(q):
            return gst[:, q * 16:(q + 1) * 16]

        E("sync", lambda e: e.dma_start(idb[:], idb_in), incs=[(SCS, 16)])
        E("sync", lambda e: e.dma_start(idf[:], idf_in), incs=[(SCS, 16)])
        E("sync", lambda e: e.dma_start(onc[:], onc_in), incs=[(SCS, 16)])
        E("sync", lambda e: e.dma_start(onr[:], onr_in), incs=[(SCS, 16)])
        E("sync", lambda e: e.dma_start(onb[:], onb_in), incs=[(SCS, 16)])
        E("sync", lambda e: e.dma_start(oh[:], oh_in), incs=[(SCS, 16)])
        consts_s = SCS.n
        E("gpsimd", lambda e: e.dma_start(gwT[:], gw_in), incs=[(SCG, 16)])
        E("gpsimd", lambda e: e.dma_start(b_sb[:], b_in), incs=[(SCG, 16)])
        consts_g = SCG.n
        E("vector", lambda e: e.memset(cpi2[:], HALFPI), incs=[(V, 1)])
        E("vector", lambda e: e.memset(ceps[:], EPS), incs=[(V, 1)])

        p2_lnb = {}
        p_rste = {}
        m_lastT = 0
        psq = {}
        for tt in range(NTT):
            sl = tt % 2
            w = []
            if tt >= 2:
                w = [(V, p2_lnb[tt - 2])]
            E("sync",
              lambda e, tt=tt, sl=sl: e.dma_start(xsl[sl][:], x0_in[tt * 128:(tt + 1) * 128, :]),
              waits=w, incs=[(SX[sl], 16)])
            lw = [(SX[sl], SX[sl].n)]
            E("vector",
              lambda e, tt=tt, sl=sl: e.reduce_sum(stc(0, tt), xsl[sl][:], AX.X),
              waits=lw, incs=[(V, 1)])
            E("scalar",
              lambda e, tt=tt, sl=sl: e.activation(ln_f32[:], xsl[sl][:], AF.Square,
                                                   accum_out=stc(1, tt)),
              waits=lw + ([(A_, psq[tt - 1])] if tt > 0 else []), incs=[(A_, 1)])
            a_sq = A_.n
            psq[tt] = a_sq
            E("vector", lambda e, tt=tt: e.tensor_scalar(stc(2, tt), stc(0, tt), 1.0 / D,
                                                         None, ALU.mult),
              waits=[(A_, a_sq), (V, V.n)], incs=[(V, 1)])
            E("vector", lambda e, tt=tt: e.tensor_scalar(stc(9, tt), stc(1, tt), 1.0 / D,
                                                         None, ALU.mult),
              waits=[(V, V.n)], incs=[(V, 1)])
            E("vector", lambda e, tt=tt: e.tensor_tensor(stc(10, tt), stc(2, tt), stc(2, tt),
                                                         ALU.mult),
              waits=[(V, V.n)], incs=[(V, 1)])
            E("vector", lambda e, tt=tt: e.tensor_tensor(stc(11, tt), stc(9, tt), stc(10, tt),
                                                         ALU.subtract),
              waits=[(V, V.n)], incs=[(V, 1)])
            v_var = V.n
            E("scalar", lambda e, tt=tt: e.activation(stc(9, tt), stc(11, tt), AF.Ln,
                                                      bias=ceps[:]),
              waits=[(V, v_var)], incs=[(A_, 1)])
            E("vector", lambda e, tt=tt: e.tensor_scalar(stc(10, tt), stc(9, tt), -0.5,
                                                         None, ALU.mult),
              waits=[(A_, A_.n), (V, V.n)], incs=[(V, 1)])
            E("scalar", lambda e, tt=tt: e.activation(stc(3, tt), stc(10, tt), AF.Exp),
              waits=[(V, V.n)], incs=[(A_, 1)])
            a_rstd1 = A_.n
            p_rste[tt] = a_rstd1
            E("vector",
              lambda e, tt=tt, sl=sl: e.tensor_scalar(ln_bf[:], xsl[sl][:], stc(2, tt),
                                                      stc(3, tt), ALU.subtract, ALU.mult),
              waits=[(A_, a_rstd1), (V, V.n), (M, m_lastT)], incs=[(V, 1)])
            v_lnb = V.n
            p2_lnb[tt] = v_lnb
            for dc in range(NDC):
                pslot = dc % 2
                w = [(V, v_lnb), (SCS, consts_s)] + ps_wait(("T", pslot))
                E("tensor",
                  lambda e, dc=dc, pslot=pslot: e.transpose(
                      PT[pslot][:, 0:128], ln_bf[:, dc * 128:(dc + 1) * 128], idb[:]),
                  waits=w, incs=[(M, 1)])
                mt = M.n
                E("vector",
                  lambda e, tt=tt, dc=dc, pslot=pslot: e.tensor_copy(
                      ln1T[:, dc * T + tt * 128: dc * T + (tt + 1) * 128], PT[pslot][:, 0:128]),
                  waits=[(M, mt)], incs=[(V, 1)])
                psfree[("T", pslot)] = (V, V.n)
            m_lastT = M.n
        v_p2_done = V.n
        v_m1s = V.n
        a_rstd1 = A_.n

        m_wb_free = {}
        v_Acp = 0
        v_tabfree = {}
        a_sin_prev = 0
        m_A_prev = 0
        v_P_prev = 0
        pend_A = None
        for np_ in range(4):
            rot = np_ < 2
            wsl = np_ % 2
            Wb = Wbs[wsl]
            E("gpsimd",
              lambda e, np_=np_, wsl=wsl: e.dma_start(Wbs[wsl][:], wp_in[np_]),
              waits=([(M, m_wb_free[wsl])] if wsl in m_wb_free else []) + [(SCG, consts_g)],
              incs=[(SWB[wsl], 16)])
            swb_n = SWB[wsl].n
            for tt in range(NTT):
                pr = tt % 2
                psK, psV = P[2 * pr], P[2 * pr + 1]
                kK, kV = 2 * pr, 2 * pr + 1
                if rot:
                    tsl = 0
                    wtb = ([v_tabfree["last"]] if "last" in v_tabfree else [])
                    E("sync",
                      lambda e, np_=np_, tt=tt: e.dma_start(tab(0, 0), cs_in[np_, tt]),
                      waits=wtb, incs=[(STB[0], 16)])
                    E("sync",
                      lambda e, np_=np_, tt=tt: e.dma_start(tab(1, 0), sn_in[np_, tt]),
                      incs=[(STB[0], 16)])
                    stb_n = STB[0].n
                wk = [(SWB[wsl], swb_n)] + ps_wait(kK)
                if np_ == 0 and tt == 0:
                    wk.append((V, v_p2_done))
                for dc in range(NDC):
                    E("tensor",
                      lambda e, tt=tt, dc=dc, psK=psK, Wb=Wb: e.matmul(
                          psK[:], ln1T[:, dc * T + tt * 128: dc * T + (tt + 1) * 128],
                          Wb[:, dc * 1024: dc * 1024 + 512], start=(dc == 0), stop=False),
                      waits=(wk if dc == 0 else []))
                E("tensor",
                  lambda e, np_=np_, psK=psK: e.matmul(
                      psK[:], onb[:], b_sb[0:1, np_ * 1024: np_ * 1024 + 512],
                      start=False, stop=True),
                  incs=[(M, 1)])
                m_k = M.n
                wv = ps_wait(kV)
                for dc in range(NDC):
                    E("tensor",
                      lambda e, tt=tt, dc=dc, psV=psV, Wb=Wb: e.matmul(
                          psV[:], ln1T[:, dc * T + tt * 128: dc * T + (tt + 1) * 128],
                          Wb[:, dc * 1024 + 512: dc * 1024 + 1024], start=(dc == 0), stop=False),
                      waits=(wv if dc == 0 else []))
                E("tensor",
                  lambda e, np_=np_, psV=psV: e.matmul(
                      psV[:], onb[:], b_sb[0:1, np_ * 1024 + 512: np_ * 1024 + 1024],
                      start=False, stop=True),
                  incs=[(M, 1)])
                m_v = M.n

                def rotary(ps, m_ps, stb_val, tslot):
                    ev, od = ps[:, 0:512:2], ps[:, 1:512:2]
                    ct, st = tab(0, tslot), tab(1, tslot)
                    E("vector", lambda e, ev=ev, ct=ct: e.tensor_tensor(
                        G1[:, 0:256], ev, ct, ALU.mult),
                      waits=[(M, m_ps), (STB[tslot], stb_val), (V, V.n)], incs=[(V, 1)])
                    E("vector", lambda e, od=od, st=st: e.tensor_tensor(
                        G2[:, 0:256], od, st, ALU.mult), waits=[(V, V.n)], incs=[(V, 1)])
                    E("vector", lambda e: e.tensor_tensor(
                        KH[:, 0:512:2], G1[:, 0:256], G2[:, 0:256], ALU.subtract),
                      waits=[(V, V.n)], incs=[(V, 1)])
                    E("vector", lambda e, od=od, ct=ct: e.tensor_tensor(
                        G1[:, 0:256], od, ct, ALU.mult), waits=[(V, V.n)], incs=[(V, 1)])
                    E("vector", lambda e, ev=ev, st=st: e.tensor_tensor(
                        G2[:, 0:256], ev, st, ALU.mult), waits=[(V, V.n)], incs=[(V, 1)])
                    E("vector", lambda e: e.tensor_tensor(
                        KH[:, 1:512:2], G1[:, 0:256], G2[:, 0:256], ALU.add),
                      waits=[(V, V.n)], incs=[(V, 1)])
                    return V.n

                def cos_chain(src_ap, out_sb, first_w, rk_w, sin_w):
                    E("vector", lambda e, src_ap=src_ap: e.tensor_scalar(
                        G1[:], src_ap, INV2PI, 0.25, ALU.mult, ALU.add),
                      waits=list(first_w) + [(V, V.n)], incs=[(V, 1)])
                    E("vector", lambda e: e.tensor_scalar(
                        G2[:], G1[:], MAGIC, MAGIC, ALU.add, ALU.subtract), incs=[(V, 1)])
                    E("vector", lambda e: e.tensor_scalar(
                        G1[:], G2[:], TWOPI, None, ALU.mult), incs=[(V, 1)])
                    E("vector", lambda e, src_ap=src_ap: e.tensor_tensor(
                        RK[:], src_ap, G1[:], ALU.subtract), waits=rk_w, incs=[(V, 1)])
                    vr = V.n
                    E("scalar", lambda e, out_sb=out_sb: e.activation(
                        out_sb[:], RK[:], AF.Sin, bias=cpi2[:]),
                      waits=[(V, vr)] + sin_w, incs=[(A_, 1)])
                    return A_.n

                rk_guard = ([(A_, a_sin_prev)] if a_sin_prev else []) + \
                           ([(M, m_A_prev)] if m_A_prev else [])
                sin_guard = [(V, v_P_prev)] if v_P_prev else []
                if rot:
                    rotary(psK, m_k, stb_n, tsl)
                    v_k_done = V.n
                    aK = cos_chain(KH[:], CKb, [], rk_guard, sin_guard)
                    rotary(psV, m_v, stb_n, tsl)
                    v_v_done = V.n
                    aV = cos_chain(KH[:], CVb, [], [(A_, aK)], sin_guard)
                    v_tabfree["last"] = (V, v_v_done)
                else:
                    aK = cos_chain(psK[:], CKb, [(M, m_k)], rk_guard, sin_guard)
                    v_k_done = V.n
                    aV = cos_chain(psV[:], CVb, [(M, m_v)], [(A_, aK)], sin_guard)
                    v_v_done = V.n
                psfree[kK] = (V, v_k_done)
                psfree[kV] = (V, v_v_done)
                psb = PSbs[tt % 2]
                E("vector", lambda e, psb=psb: e.tensor_tensor(psb[:], CKb[:], CVb[:], ALU.mult),
                  waits=[(A_, aK), (A_, aV)] +
                        ([(M, psb_free[tt % 2])] if (tt % 2) in psb_free else []),
                  incs=[(V, 1)])
                v_P = V.n
                if pend_A is not None:
                    ptt, pv, pfirst, ppsb = pend_A
                    wA = [(V, pv)]
                    if pfirst:
                        wA += [(SCS, consts_s)] + ([(V, v_Acp)] if v_Acp else [])
                    E("tensor",
                      lambda e, ppsb=ppsb, pfirst=pfirst: e.matmul(
                          psA, onc[:], ppsb[:], start=pfirst, stop=False),
                      waits=wA, incs=[(M, 1)])
                    psb_free[ptt % 2] = M.n
                pend_A = (tt, v_P, (tt == 0), psb)
                a_sin_prev = aV
                v_P_prev = v_P
            ptt, pv, pfirst, ppsb = pend_A
            E("tensor",
              lambda e, ppsb=ppsb, pfirst=pfirst: e.matmul(
                  psA, onc[:], ppsb[:], start=pfirst, stop=True),
              waits=[(V, pv)] + ([(SCS, consts_s)] if pfirst else []), incs=[(M, 1)])
            psb_free[ptt % 2] = M.n
            pend_A = None
            m_A_prev = M.n
            m_endKV = M.n
            E("vector",
              lambda e, np_=np_: e.tensor_copy(A_sb[0:1, np_ * 512:(np_ + 1) * 512], psA),
              waits=[(M, m_A_prev)], incs=[(V, 1)])
            v_Acp = V.n
            m_wb_free[wsl] = M.n
        psfree[4] = (V, v_Acp)

        E("scalar", lambda e: e.dma_start(ar_src_flat, A_sb[:]),
          waits=[(V, v_Acp)], incs=[(SCQ, 16)])
        E("gpsimd", lambda e: e.collective_compute(
            "AllReduce", ALU.add, replica_groups=[list(range(NCORES))],
            ins=[ar_src_h.ap().opt()], outs=[ar_dst_h.ap().opt()]),
          waits=[(SCQ, SCQ.n)], incs=[(CO, 1)])
        pre_sx = {}
        for tt in range(2):
            E("sync",
              lambda e, tt=tt: e.dma_start(xsl[tt][:], x0_in[tt * 128:(tt + 1) * 128, :]),
              waits=[(V, p2_lnb[NTT - 2 + tt])], incs=[(SX[tt], 16)])
            pre_sx[tt] = SX[tt].n
        E("sync", lambda e: e.dma_start(A_sb[:], ar_dst_flat),
          waits=[(CO, 1)], incs=[(SARL, 16)])
        E("vector", lambda e: e.tensor_scalar(A_fin[:], A_sb[:], 0.5, 0.5, ALU.mult, ALU.add),
          waits=[(SARL, 16)], incs=[(V, 1)])
        v_afin = V.n
        for i in range(4):
            E("tensor",
              lambda e, i=i: e.matmul(P[i][:], onr[:], A_fin[0:1, i * 512:(i + 1) * 512],
                                      start=True, stop=True),
              waits=[(V, v_afin)] + ps_wait(i), incs=[(M, 1)])
            mi = M.n
            E("vector",
              lambda e, i=i: e.tensor_copy(A_bc[:, i * 512:(i + 1) * 512], P[i][:]),
              waits=[(M, mi)], incs=[(V, 1)])
            psfree[i] = (V, V.n)
        v_abc = V.n

        a2 = {}
        x1w = {}
        a2sq = {}
        a2c_lnb = {}
        m_lastT2 = m_lastT
        for tt in range(NTT):
            sl = tt % 2
            if tt < 2:
                lw = (SX[sl], pre_sx[tt])
            else:
                E("sync",
                  lambda e, tt=tt, sl=sl: e.dma_start(xsl[sl][:],
                                                      x0_in[tt * 128:(tt + 1) * 128, :]),
                  waits=[(V, a2[tt - 2])], incs=[(SX[sl], 16)])
                lw = (SX[sl], SX[sl].n)
            lnw = [lw, (V, v_abc), (V, v_m1s), (A_, a_rstd1)]
            if tt > 0:
                lnw += [(A_, a2sq[tt - 1]), (SCQ, x1w[tt - 1])]
            E("vector",
              lambda e, tt=tt, sl=sl: e.tensor_scalar(ln_f32[:], xsl[sl][:], stc(2, tt),
                                                      stc(3, tt), ALU.subtract, ALU.mult),
              waits=lnw, incs=[(V, 1)])
            v_ln1 = V.n
            E("gpsimd", lambda e: e.tensor_tensor(u_buf[:], A_bc[:], ln_f32[:], ALU.mult),
              waits=[(V, v_ln1)], incs=[(CO, 1)])
            co_u = CO.n
            E("vector", lambda e, sl=sl: e.tensor_tensor(ln_f32[:], xsl[sl][:], u_buf[:],
                                                         ALU.add),
              waits=[(CO, co_u)], incs=[(V, 1)])
            v_x1 = V.n
            a2[tt] = v_x1
            E("scalar", lambda e, tt=tt: e.dma_start(x1_d[tt], ln_f32[:]),
              waits=[(V, v_x1)], incs=[(SCQ, 16)])
            x1w[tt] = SCQ.n
            E("vector", lambda e, tt=tt: e.reduce_sum(stc(5, tt), ln_f32[:], AX.X),
              incs=[(V, 1)])
            E("scalar", lambda e, tt=tt: e.activation(u_buf[:], ln_f32[:], AF.Square,
                                                      accum_out=stc(6, tt)),
              waits=[(V, v_x1)], incs=[(A_, 1)])
            a_sq = A_.n
            a2sq[tt] = a_sq
            E("vector", lambda e, tt=tt: e.tensor_scalar(stc(7, tt), stc(5, tt), 1.0 / D,
                                                         None, ALU.mult),
              waits=[(A_, a_sq), (V, V.n)], incs=[(V, 1)])
            E("vector", lambda e, tt=tt: e.tensor_scalar(stc(9, tt), stc(6, tt), 1.0 / D,
                                                         None, ALU.mult),
              waits=[(V, V.n)], incs=[(V, 1)])
            E("vector", lambda e, tt=tt: e.tensor_tensor(stc(10, tt), stc(7, tt), stc(7, tt),
                                                         ALU.mult),
              waits=[(V, V.n)], incs=[(V, 1)])
            E("vector", lambda e, tt=tt: e.tensor_tensor(stc(11, tt), stc(9, tt), stc(10, tt),
                                                         ALU.subtract),
              waits=[(V, V.n)], incs=[(V, 1)])
            v_var = V.n
            E("scalar", lambda e, tt=tt: e.activation(stc(9, tt), stc(11, tt), AF.Ln,
                                                      bias=ceps[:]),
              waits=[(V, v_var)] + ([(A_, a2rste[tt - 1])] if tt > 0 else []),
              incs=[(A_, 1)])
            E("vector", lambda e, tt=tt: e.tensor_scalar(stc(10, tt), stc(9, tt), -0.5,
                                                         None, ALU.mult),
              waits=[(A_, A_.n), (V, V.n)], incs=[(V, 1)])
            E("scalar", lambda e, tt=tt: e.activation(stc(8, tt), stc(10, tt), AF.Exp),
              waits=[(V, V.n)], incs=[(A_, 1)])
            a_rstd2 = A_.n
            if tt == 0:
                a2rste = {}
            a2rste[tt] = a_rstd2
            E("vector",
              lambda e, tt=tt: e.tensor_scalar(ln_bf[:], ln_f32[:], stc(7, tt),
                                               stc(8, tt), ALU.subtract, ALU.mult),
              waits=[(A_, a_rstd2), (V, V.n), (M, m_lastT2)], incs=[(V, 1)])
            v_lnb = V.n
            a2c_lnb[tt] = v_lnb
            for dc in range(NDC):
                pslot = dc % 2
                w = [(V, v_lnb)] + ps_wait(("T", pslot))
                E("tensor",
                  lambda e, dc=dc, pslot=pslot: e.transpose(
                      PT[pslot][:, 0:128], ln_bf[:, dc * 128:(dc + 1) * 128], idb[:]),
                  waits=w, incs=[(M, 1)])
                mt = M.n
                wv = [(M, mt)]
                if ST2[pslot].n > 0:
                    wv.append((ST2[pslot], ST2[pslot].n))
                E("scalar",
                  lambda e, pslot=pslot: e.copy(t2t[pslot][:], PT[pslot][:, 0:128]),
                  waits=wv, incs=[(A_, 1)])
                psfree[("T", pslot)] = (A_, A_.n)
                vev = A_.n
                E("scalar",
                  lambda e, dc=dc, tt=tt, pslot=pslot: e.dma_start(l2t_d[dc, tt],
                                                                   t2t[pslot][:]),
                  incs=[(ST2[pslot], 16)])
                wmm = [(A_, vev)]
                if dc == 0:
                    wmm += [(SCG, consts_g)] + ps_wait(5)
                E("tensor",
                  lambda e, dc=dc, pslot=pslot: e.matmul(
                      psS, t2t[pslot][:], gwT[:, dc * 8:(dc + 1) * 8],
                      start=(dc == 0), stop=(dc == NDC - 1)),
                  waits=wmm, incs=([(M, 1)] if dc == NDC - 1 else []))
            m_lastT2 = M.n
            E("vector",
              lambda e, tt=tt: e.tensor_copy(scores_sb[:, tt * 8:(tt + 1) * 8], psS),
              waits=[(M, m_lastT2)], incs=[(V, 1)])
            psfree[5] = (V, V.n)
            if tt == 7:
                scq_l2t_half0 = [ST2[0].n, ST2[1].n]
        scq_x1 = SCQ.n
        scq_l2t = [ST2[0].n, ST2[1].n]
        scq_l2t_h0 = list(scq_l2t_half0)

        def g3(t):
            a = t if isinstance(t, bass.AP) else t.ap()
            return a.rearrange("p (g k) -> p g k", k=8)

        def bc16(t16):
            return t16.unsqueeze(2).broadcast_to([128, 16, 8])

        v_scores = V.n

        def gv(fn, waits=()):
            E("vector", fn, waits=[(V, V.n)] + list(waits), incs=[(V, 1)])
            return V.n

        def ga(fn, wv):
            E("scalar", fn, waits=[(V, wv)], incs=[(A_, 1)])
            return A_.n

        def gw(fn, wa):
            E("vector", fn, waits=[(A_, wa), (V, V.n)], incs=[(V, 1)])
            return V.n

        BIG = 1.0e30
        v_nv1 = gv(lambda e: e.tensor_reduce(gss(0), g3(scores_sb), AX.X, ALU.max, negate=True))
        gv(lambda e: e.tensor_tensor(g3(gT1), g3(scores_sb), bc16(gss(0)), ALU.add),
           waits=[(V, v_nv1)])
        gv(lambda e: e.tensor_scalar(gS2[:], gT1[:], -BIG, None, ALU.mult))
        gv(lambda e: e.tensor_scalar(gCM[:], gS2[:], 1.0, None, ALU.min))
        gv(lambda e: e.tensor_scalar(gE1[:], gCM[:], -1.0, 1.0, ALU.mult, ALU.add))
        ax = ga(lambda e: e.activation(gAB[:], scores_sb[:], AF.Abs), v_scores)
        v_nm1 = gw(lambda e: e.tensor_scalar(gss(1), gss(0), -1.0, None, ALU.mult), ax)
        gv(lambda e: e.tensor_tensor(g3(gU1), g3(gAB), bc16(gss(1)), ALU.max),
           waits=[(V, v_nm1)])
        gv(lambda e: e.tensor_scalar(gCM[:], gU1[:], 0.02, None, ALU.mult))
        gv(lambda e: e.tensor_tensor(gS2[:], gT1[:], gCM[:], ALU.add))
        gv(lambda e: e.tensor_scalar(gCM[:], gS2[:], -BIG, None, ALU.mult))
        gv(lambda e: e.tensor_scalar(gU1[:], gCM[:], 1.0, None, ALU.min))
        gv(lambda e: e.tensor_scalar(gCM[:], gU1[:], 0.0, None, ALU.max))
        gv(lambda e: e.tensor_scalar(gS2[:], gCM[:], NEG, None, ALU.mult))
        v_sm1 = gv(lambda e: e.tensor_tensor(gU1[:], gT1[:], gS2[:], ALU.add))
        a_e1 = ga(lambda e: e.activation(gTO[:], gU1[:], AF.Exp), v_sm1)
        v_d1 = gw(lambda e: e.tensor_reduce(gss(2), g3(gTO), AX.X, ALU.add), a_e1)
        a_l1 = ga(lambda e: e.activation(gss(3), gss(2), AF.Ln), v_d1)
        v_l1 = gw(lambda e: e.tensor_scalar(gss(9), gss(3), -1.0, None, ALU.mult), a_l1)
        a_r1 = ga(lambda e: e.activation(gss(3), gss(9), AF.Exp), v_l1)
        gv(lambda e: e.tensor_tensor(gU1[:], gTO[:], gE1[:], ALU.mult))
        gv(lambda e: e.tensor_reduce(gss(4), g3(gU1), AX.X, ALU.add))
        v_m1v = gw(lambda e: e.tensor_tensor(gss(5), gss(4), gss(3), ALU.mult), a_r1)
        gv(lambda e: e.tensor_scalar(gS2[:], gE1[:], NEG, None, ALU.mult))
        gv(lambda e: e.tensor_tensor(gU1[:], scores_sb[:], gS2[:], ALU.add))
        v_nv2 = gv(lambda e: e.tensor_reduce(gss(6), g3(gU1), AX.X, ALU.max, negate=True))
        gv(lambda e: e.tensor_tensor(g3(gTO), g3(scores_sb), bc16(gss(6)), ALU.add),
           waits=[(V, v_nv2)])
        gv(lambda e: e.tensor_tensor(g3(gS2), g3(gU1), bc16(gss(6)), ALU.add))
        gv(lambda e: e.tensor_scalar(gCM[:], gS2[:], -BIG, None, ALU.mult))
        gv(lambda e: e.tensor_scalar(gU1[:], gCM[:], 1.0, None, ALU.min))
        gv(lambda e: e.tensor_scalar(gE2[:], gU1[:], -1.0, 1.0, ALU.mult, ALU.add))
        v_nm2 = gv(lambda e: e.tensor_scalar(gss(7), gss(6), -1.0, None, ALU.mult))
        gv(lambda e: e.tensor_tensor(g3(gS2), g3(gAB), bc16(gss(7)), ALU.max),
           waits=[(V, v_nm2)])
        gv(lambda e: e.tensor_scalar(gU1[:], gS2[:], 0.02, None, ALU.mult))
        gv(lambda e: e.tensor_tensor(gS2[:], gTO[:], gU1[:], ALU.add))
        gv(lambda e: e.tensor_scalar(gCM[:], gS2[:], -BIG, None, ALU.mult))
        gv(lambda e: e.tensor_scalar(gU1[:], gCM[:], 1.0, None, ALU.min))
        gv(lambda e: e.tensor_scalar(gCM[:], gU1[:], 0.0, None, ALU.max))
        gv(lambda e: e.tensor_scalar(gU1[:], gCM[:], NEG, None, ALU.mult))
        gv(lambda e: e.tensor_scalar(gCM[:], gE1[:], NEG, None, ALU.mult))
        gv(lambda e: e.tensor_tensor(gS2[:], gU1[:], gCM[:], ALU.add))
        v_sm2 = gv(lambda e: e.tensor_tensor(gU1[:], gTO[:], gS2[:], ALU.add))
        a_e2 = ga(lambda e: e.activation(gCM[:], gU1[:], AF.Exp), v_sm2)
        v_d2 = gw(lambda e: e.tensor_reduce(gss(8), g3(gCM), AX.X, ALU.add), a_e2)
        a_l2 = ga(lambda e: e.activation(gss(9), gss(8), AF.Ln), v_d2)
        v_l2 = gw(lambda e: e.tensor_scalar(gss(10), gss(9), -1.0, None, ALU.mult), a_l2)
        a_r2 = ga(lambda e: e.activation(gss(9), gss(10), AF.Exp), v_l2)
        gv(lambda e: e.tensor_tensor(gU1[:], gCM[:], gE2[:], ALU.mult))
        gv(lambda e: e.tensor_reduce(gss(10), g3(gU1), AX.X, ALU.add))
        v_m2v = gw(lambda e: e.tensor_tensor(gss(11), gss(10), gss(9), ALU.mult), a_r2)
        gv(lambda e: e.tensor_tensor(g3(gT1), g3(gE1), bc16(gss(5)), ALU.mult),
           waits=[(V, v_m1v)])
        gv(lambda e: e.tensor_tensor(g3(gS2), g3(gE2), bc16(gss(11)), ALU.mult),
           waits=[(V, v_m2v)])
        v_coef = gv(lambda e: e.tensor_tensor(gU1[:], gT1[:], gS2[:], ALU.add))
        for tt in range(NTT):
            w = [(V, v_coef), (SCS, consts_s)] + ps_wait(5)
            E("tensor",
              lambda e, tt=tt: e.transpose(psC, gU1[:, tt * 8:(tt + 1) * 8], idf[:]),
              waits=w, incs=[(M, 1)])
            mt = M.n
            E("vector",
              lambda e, tt=tt: e.tensor_copy(coefT[0:8, tt * 128:(tt + 1) * 128], psC),
              waits=[(M, mt)], incs=[(V, 1)])
            psfree[5] = (V, V.n)
        v_coefT = V.n
        m_endA = M.n
        v_endA = V.n
        a_endA = A_.n

        SDB = sem("sdb")
        if dbg:
            E("sync", lambda e: e.dma_start(dbgt["afin"], A_fin[:]),
              waits=[(V, v_afin)], incs=[(SDB, 16)])
            E("sync", lambda e: e.dma_start(dbgt["stats"], stats[:]),
              waits=[(A_, a_rstd2)], incs=[(SDB, 16)])
            E("sync", lambda e: e.dma_start(dbgt["scores"], scores_sb[:]),
              waits=[(V, v_scores)], incs=[(SDB, 16)])
            E("sync", lambda e: e.dma_start(dbgt["coef"], gU1[:]),
              waits=[(V, v_coef)], incs=[(SDB, 16)])
            E("sync", lambda e: e.dma_start(dbgt["coefT"], coefT[:]),
              waits=[(V, v_coefT)], incs=[(SDB, 16)])
            E("sync", lambda e: e.dma_start(dbgt["ln1T"], ln1T[:]),
              waits=[(V, v_p2_done)], incs=[(SDB, 16)])

        sA.close()

        sB = ExitStack()
        enB = sB.enter_context
        hP = enB(nc.sbuf_tensor("hP", [128, NFL * 1024], BF16))
        ln2T = enB(nc.sbuf_tensor("ln2T", [128, NDC * 1024], BF16))
        yac = [enB(nc.sbuf_tensor("yac0", [128, NHC * 1024], BF16)),
               enB(nc.sbuf_tensor("yac1", [128, NHC * 1024], BF16))]
        w1b = [enB(nc.sbuf_tensor("w1b0", [128, NDC * 128], BF16)),
               enB(nc.sbuf_tensor("w1b1", [128, NDC * 128], BF16))]
        w3b = [enB(nc.sbuf_tensor("w3b0", [128, NDC * 128], BF16)),
               enB(nc.sbuf_tensor("w3b1", [128, NDC * 128], BF16))]
        w2b = [enB(nc.sbuf_tensor("w2b0", [128, NFL * 128], BF16)),
               enB(nc.sbuf_tensor("w2b1", [128, NFL * 128], BF16))]
        c_b = enB(nc.sbuf_tensor("c_b", [128, 1024], F32))
        s_tmp = [enB(nc.sbuf_tensor("stmp0", [128, 512], BF16)),
                 enB(nc.sbuf_tensor("stmp1", [128, 512], BF16))]
        ye_t = [enB(nc.sbuf_tensor("ye0", [128, 512], BF16)),
                enB(nc.sbuf_tensor("ye1", [128, 512], BF16))]
        out_t = enB(nc.sbuf_tensor("out_t", [128, D], F32))

        m_h_free = {}
        m_w2_free = {}
        a_silu = {}
        v_hprime = {}
        v_yev = {}
        m_h_last = 0
        m_endhalf = 0
        sst_last = 0
        eh_out = {}
        dbg_cbw = None
        dbg_h0w = None

        for half in range(2):
            if half == 0:
                wl = [(M, m_endKV), (ST2[0], scq_l2t_h0[0]), (ST2[1], scq_l2t_h0[1])]
            else:
                wl = [(M, m_h_last), (ST2[0], scq_l2t[0]), (ST2[1], scq_l2t[1])]
            wl += ([(SDB, SDB.n)] if (dbg and half == 0) else []) + \
                  ([dbg_h0w] if (dbg and half == 1 and dbg_h0w) else [])
            for dc in range(NDC):
                E("sync",
                  lambda e, dc=dc, half=half: e.dma_start(
                      ln2T.ap()[:, dc * 1024:(dc + 1) * 1024].rearrange(
                          "p (tl t) -> p tl t", t=128),
                      l2t_d[dc, half * 8:(half + 1) * 8].transpose([1, 0, 2])),
                  waits=(wl if dc == 0 else []), incs=[(SLN, 16)])
            sln_n = SLN.n
            E("vector", lambda e: e.memset(yac[0][:], 0.0),
              waits=(([(V, v_endA)] if half == 0 else [(M, m_endhalf)]) +
                     ([dbg_h0w] if (dbg and half == 1 and dbg_h0w) else [])),
              incs=[(V, 1)])

            for ee in range(NE):
                for i in range(2):
                    w = [(V, v_coefT)] + ps_wait(4 + i)
                    E("tensor",
                      lambda e, ee=ee, half=half, i=i: e.matmul(
                          P[4 + i][:], oh[0:8, ee * 128:(ee + 1) * 128],
                          coefT[0:8, half * 1024 + i * 512: half * 1024 + (i + 1) * 512],
                          start=True, stop=True),
                      waits=w, incs=[(M, 1)])
                    mi = M.n
                    E("vector",
                      lambda e, i=i: e.tensor_copy(c_b[:, i * 512:(i + 1) * 512], P[4 + i][:]),
                      waits=[(M, mi)] + ([dbg_cbw] if dbg_cbw else []), incs=[(V, 1)])
                    dbg_cbw = None
                    psfree[4 + i] = (V, V.n)
                v_cb_e = V.n
                if dbg and half == 0 and ee == 0:
                    E("sync", lambda e: e.dma_start(dbgt["cb0"], c_b[:]),
                      waits=[(V, v_cb_e)], incs=[(SDB, 16)])
                    dbg_cbw = (SDB, SDB.n)

                for fh in range(2):
                    step = ee * 2 + fh
                    ysrc, ydst = yac[step % 2], yac[(step + 1) % 2]
                    for fl in range(NFL):
                        fc = fh * NFL + fl
                        sl = fl % 2
                        w1w = [(SCG, consts_g)]
                        if half == 0 and ee == 0 and fc == 0:
                            w1w += [(V, v_endA), (A_, a_endA), (SCQ, SCQ.n)]
                        if ("w1", sl) in m_h_free:
                            w1w.append((M, m_h_free[("w1", sl)]))
                        E("gpsimd",
                          lambda e, ee=ee, fc=fc, sl=sl: e.dma_start(w1b[sl][:], w1_in[ee, fc]),
                          waits=w1w, incs=[(SW1[sl], 16)])
                        sw1_n = SW1[sl].n
                        w3w = []
                        if ("w3", sl) in m_h_free:
                            w3w.append((M, m_h_free[("w3", sl)]))
                        E("gpsimd",
                          lambda e, ee=ee, fc=fc, sl=sl: e.dma_start(w3b[sl][:], w3_in[ee, fc]),
                          waits=w3w, incs=[(SW3[sl], 16)])
                        sw3_n = SW3[sl].n
                        for tau in range(2):
                            PA, PB = P[2 * tau], P[2 * tau + 1]
                            wh = [(SW1[sl], sw1_n), (SLN, sln_n)]
                            if tau in a_silu:
                                wh.append((A_, a_silu[tau]))
                            else:
                                wh += ps_wait(2 * tau)
                            for dc in range(NDC):
                                E("tensor",
                                  lambda e, tau=tau, dc=dc, sl=sl, PA=PA: e.matmul(
                                      PA[:], w1b[sl][:, dc * 128:(dc + 1) * 128],
                                      ln2T[:, dc * 1024 + tau * 512:
                                           dc * 1024 + (tau + 1) * 512],
                                      start=(dc == 0), stop=(dc == NDC - 1)),
                                  waits=(wh if dc == 0 else []),
                                  incs=([(M, 1)] if dc == NDC - 1 else []))
                            m_h1 = M.n
                            if tau == 1:
                                m_h_free[("w1", sl)] = m_h1
                            ws = [(M, m_h1)]
                            if tau in v_hprime:
                                ws.append((V, v_hprime[tau]))
                            E("scalar",
                              lambda e, tau=tau, PA=PA: e.activation(s_tmp[tau][:], PA[:],
                                                                     AF.Silu),
                              waits=ws, incs=[(A_, 1)])
                            a_silu[tau] = A_.n
                            wh3 = [(SW3[sl], sw3_n)]
                            if tau in v_hprime:
                                wh3.append((V, v_hprime[tau]))
                            else:
                                wh3 += ps_wait(2 * tau + 1)
                            for dc in range(NDC):
                                E("tensor",
                                  lambda e, tau=tau, dc=dc, sl=sl, PB=PB: e.matmul(
                                      PB[:], w3b[sl][:, dc * 128:(dc + 1) * 128],
                                      ln2T[:, dc * 1024 + tau * 512:
                                           dc * 1024 + (tau + 1) * 512],
                                      start=(dc == 0), stop=(dc == NDC - 1)),
                                  waits=(wh3 if dc == 0 else []),
                                  incs=([(M, 1)] if dc == NDC - 1 else []))
                            m_h3 = M.n
                            if tau == 1:
                                m_h_free[("w3", sl)] = m_h3
                            E("vector",
                              lambda e, fl=fl, tau=tau, PB=PB: e.tensor_tensor(
                                  hP[:, fl * 1024 + tau * 512: fl * 1024 + (tau + 1) * 512],
                                  s_tmp[tau][:], PB[:], ALU.mult),
                              waits=[(A_, a_silu[tau]), (M, m_h3)], incs=[(V, 1)])
                            v_hprime[tau] = V.n
                    m_h_last = M.n
                    v_hp_all = V.n

                    for hc in range(NHC):
                        sl = hc % 2
                        w2w = []
                        if ("w2", sl) in m_w2_free:
                            w2w.append((M, m_w2_free[("w2", sl)]))
                        E("gpsimd",
                          lambda e, ee=ee, fh=fh, hc=hc, sl=sl: e.dma_start(
                              w2b[sl][:], w2_in[ee, fh, hc]),
                          waits=w2w, incs=[(SW2[sl], 16)])
                        sw2_n = SW2[sl].n
                        for tau in range(2):
                            PY = P[4 + tau]
                            wy = [(SW2[sl], sw2_n)]
                            if hc == 0:
                                wy.append((V, v_hp_all))
                            if tau in v_yev:
                                wy.append((V, v_yev[tau]))
                            else:
                                wy += ps_wait(4 + tau)
                            for fl in range(NFL):
                                E("tensor",
                                  lambda e, tau=tau, fl=fl, sl=sl, PY=PY: e.matmul(
                                      PY[:], w2b[sl][:, fl * 128:(fl + 1) * 128],
                                      hP[:, fl * 1024 + tau * 512:
                                         fl * 1024 + (tau + 1) * 512],
                                      start=(fl == 0), stop=(fl == NFL - 1)),
                                  waits=(wy if fl == 0 else []),
                                  incs=([(M, 1)] if fl == NFL - 1 else []))
                            m_y = M.n
                            if tau == 1:
                                m_w2_free[("w2", sl)] = m_y
                            E("vector",
                              lambda e, tau=tau, PY=PY: e.tensor_tensor(
                                  ye_t[tau][:], PY[:], c_b[:, tau * 512:(tau + 1) * 512],
                                  ALU.mult),
                              waits=[(M, m_y)], incs=[(V, 1)])
                            E("vector",
                              lambda e, hc=hc, tau=tau, ysrc=ysrc, ydst=ydst: e.tensor_tensor(
                                  ydst[:, hc * 1024 + tau * 512: hc * 1024 + (tau + 1) * 512],
                                  ysrc[:, hc * 1024 + tau * 512: hc * 1024 + (tau + 1) * 512],
                                  ye_t[tau][:], ALU.add),
                              incs=[(V, 1)])
                            v_yev[tau] = V.n
                    psfree[4] = (V, v_yev[0])
                    psfree[5] = (V, v_yev[1])
            v_yacc_done = V.n
            if dbg and half == 0:
                E("sync", lambda e: e.dma_start(dbgt["yach0"], yac[0][:]),
                  waits=[(V, v_yacc_done)], incs=[(SDB, 16)])
                E("sync", lambda e: e.dma_start(dbgt["hp"], hP[:]),
                  waits=[(V, v_yacc_done)], incs=[(SDB, 16)])
                dbg_h0w = (SDB, SDB.n)

            for tl in range(8):
                tt = half * 8 + tl
                sl = tl % 2
                if tl < 2:
                    if half == 0:
                        wx = [(V, a2c_lnb[NTT - 2 + tl])]
                    else:
                        wx = [(V, eh_out[(0, 6 + tl)])]
                else:
                    wx = [(V, eh_out[(half, tl - 2)])]
                E("sync",
                  lambda e, tt=tt, sl=sl: e.dma_start(xsl[sl][:], x1_d[tt]),
                  waits=wx, incs=[(SX[sl], 16)])
                sx_n = SX[sl].n
                wout = ([(SST, sst_last)] if sst_last else [])
                for hc in range(NHC):
                    pslot = hc % 2
                    w = [(V, v_yacc_done)] + ps_wait(("T", pslot))
                    E("tensor",
                      lambda e, tl=tl, hc=hc, pslot=pslot: e.transpose(
                          PT[pslot][:, 0:128],
                          yac[0][:, hc * 1024 + tl * 128: hc * 1024 + (tl + 1) * 128], idb[:]),
                      waits=w, incs=[(M, 1)])
                    mt = M.n
                    E("vector",
                      lambda e, hc=hc, sl=sl, pslot=pslot: e.tensor_tensor(
                          out_t[:, hc * 128:(hc + 1) * 128],
                          xsl[sl][:, hc * 128:(hc + 1) * 128],
                          PT[pslot][:, 0:128], ALU.add),
                      waits=[(M, mt), (SX[sl], sx_n)] + (wout if hc == 0 else []),
                      incs=[(V, 1)])
                    psfree[("T", pslot)] = (V, V.n)
                v_out = V.n
                eh_out[(half, tl)] = v_out
                E("sync",
                  lambda e, tt=tt: e.dma_start(out_d[tt * 128:(tt + 1) * 128, :], out_t[:]),
                  waits=[(V, v_out)], incs=[(SST, 16)])
                sst_last = SST.n
            m_endhalf = M.n

        if dbg:
            E("sync", lambda e: e.dma_start(dbgt["x1"], x1_d), waits=[(SCQ, scq_x1)],
              incs=[(SDB, 16)])
            E("sync", lambda e: e.dma_start(dbgt["l2t"], l2t_d),
              waits=[(ST2[0], scq_l2t[0]), (ST2[1], scq_l2t[1])], incs=[(SDB, 16)])
            E("sync", lambda e: e.dma_start(dbgt["yac"], yac[0][:]),
              waits=[(V, v_yacc_done)], incs=[(SDB, 16)])
        E("sync", lambda e: None, waits=[(SST, SST.n), (SDB, SDB.n)])
        sB.close()

        block = stk.enter_context(nc.Block())
        sch.build(block)

    if os.environ.get("KDBG"):
        import sys
        print("engine op counts:", sch.counts(), file=sys.stderr)
    return nc


_CACHE = {}


def _prep(W, b, gate_w, w1, w2, w3):
    W = np.asarray(W, dtype=np.float32)
    b = np.asarray(b, dtype=np.float32)
    gate_w = np.asarray(gate_w, dtype=np.float32)
    w1 = np.asarray(w1, dtype=np.float32)
    w2 = np.asarray(w2, dtype=np.float32)
    w3 = np.asarray(w3, dtype=np.float32)
    wp = np.stack([np.concatenate([W[:, i * 512:(i + 1) * 512],
                                   W[:, 2048 + i * 512: 2048 + (i + 1) * 512]], axis=1)
                   for i in range(4)])
    wp = np.ascontiguousarray(
        wp.reshape(4, 16, 128, 1024).transpose(0, 2, 1, 3).reshape(4, 128, 16 * 1024))
    bp = np.concatenate([np.concatenate([b[i * 512:(i + 1) * 512],
                                         b[2048 + i * 512: 2048 + (i + 1) * 512]])
                         for i in range(4)]).reshape(1, 4096)
    gwT = np.ascontiguousarray(
        gate_w.T.reshape(16, 128, 8).transpose(1, 0, 2).reshape(128, 128))

    def prep_w13(w):
        t = w.transpose(0, 2, 1)
        t = t.reshape(NE, 16, 128, 32, 128).transpose(0, 3, 2, 1, 4)
        return np.ascontiguousarray(t.reshape(NE, 32, 128, 2048))
    w1p = prep_w13(w1)
    w3p = prep_w13(w3)
    t = w2.transpose(0, 2, 1)
    t = t.reshape(NE, 2, 16, 128, 16, 128)
    t = t.transpose(0, 1, 4, 3, 2, 5)
    w2p = np.ascontiguousarray(t.reshape(NE, 2, NHC, 128, 2048))
    pos = np.arange(16384, dtype=np.float32)
    inv_freq = (1.0 / (10000.0 ** (np.arange(0, 1024, 2, dtype=np.float32) / 1024.0))
                ).astype(np.float32)
    freqs = (pos[:, None] * inv_freq[None, :]).astype(np.float32)
    cosf = np.cos(freqs).astype(np.float32)
    sinf = np.sin(freqs).astype(np.float32)
    return wp, bp, gwT, w1p, w2p, w3p, cosf, sinf


def kernel(L0_attn_X, L0_attn_W, L0_attn_b, L0_ffn_gate_w, L0_ffn_w1, L0_ffn_w2, L0_ffn_w3):
    X = np.ascontiguousarray(np.asarray(L0_attn_X, dtype=np.float32).reshape(16384, 2048))
    wp, bp, gwT, w1p, w2p, w3p, cosf, sinf = _prep(
        L0_attn_W, L0_attn_b, L0_ffn_gate_w, L0_ffn_w1, L0_ffn_w2, L0_ffn_w3)

    idb = np.eye(128).astype(ml_dtypes.bfloat16)
    idf = np.eye(128, dtype=np.float32)
    onc = np.ones((128, 1), dtype=np.float32)
    onr = np.ones((1, 128), dtype=np.float32)
    onb = np.ones((1, 128), dtype=ml_dtypes.bfloat16)
    oh = np.zeros((8, 8 * 128), dtype=np.float32)
    for e in range(8):
        oh[e, e * 128:(e + 1) * 128] = 1.0
    oh = oh.astype(ml_dtypes.bfloat16)

    if "nc" not in _CACHE:
        _CACHE["nc"] = build_nc()
    nc = _CACHE["nc"]

    in_maps = []
    for c in range(NCORES):
        rows = slice(c * T, (c + 1) * T)
        cs_c = np.ascontiguousarray(
            cosf[rows].reshape(NTT, 128, 2, 256).transpose(2, 0, 1, 3))
        sn_c = np.ascontiguousarray(
            sinf[rows].reshape(NTT, 128, 2, 256).transpose(2, 0, 1, 3))
        in_maps.append({
            "x0_in": np.ascontiguousarray(X[rows]), "wp_in": wp, "b_in": bp, "gw_in": gwT,
            "w1_in": w1p, "w3_in": w3p, "w2_in": w2p,
            "cs_in": cs_c, "sn_in": sn_c,
            "idb_in": idb, "idf_in": idf, "onc_in": onc, "onr_in": onr,
            "onb_in": onb, "oh_in": oh,
        })
    res = run_bass_kernel_spmd(nc, in_maps, core_ids=list(range(NCORES)))
    _CACHE["res"] = res
    out = np.concatenate([np.asarray(res.results[c]["out"]) for c in range(NCORES)], axis=0)
    return out.reshape(1, 16384, 2048).astype(np.float32)


# version 09
# speedup vs baseline: 1.0027x; 1.0027x over previous
import os
from contextlib import ExitStack

import numpy as np
import ml_dtypes
import concourse.bass as bass
import concourse.mybir as mybir
from concourse.bass_utils import run_bass_kernel_spmd

F32 = mybir.dt.float32
BF16 = mybir.dt.bfloat16
AF = mybir.ActivationFunctionType
ALU = mybir.AluOpType
AX = mybir.AxisListType

NCORES = 8
T = 2048
D = 2048
NE = 8
NTT = 16
NDC = 16
NHC = 16
NFL = 16
EPS = 1e-5

MAGIC = float(np.float32(1.5 * 2 ** 23))
INV2PI = float(np.float32(1.0 / (2.0 * np.pi)))
TWOPI = float(np.float32(2.0 * np.pi))
HALFPI = float(np.float32(np.pi / 2.0))
NEG = -1.0e30

ENGINES = ("sync", "scalar", "vector", "tensor", "gpsimd")


class Sem:
    __slots__ = ("h", "n")

    def __init__(self, h):
        self.h = h
        self.n = 0


class Sched:
    def __init__(self):
        self.ops = {e: [] for e in ENGINES}

    def emit(self, eng, fn, waits=(), incs=()):
        for s, v in waits:
            assert v <= s.n, f"deadlock: wait {v} > emitted {s.n} on {s.h.name}"
        lowered = (fn, [(s.h, v) for s, v in waits if v > 0], [])
        for s, k in incs:
            s.n += k
            lowered[2].append((s.h, k))
        self.ops[eng].append(lowered)

    def build(self, block):
        def make(name):
            ops = self.ops[name]

            def body(eng):
                for fn, waits, incs in ops:
                    for h, v in waits:
                        eng.wait_ge(h, v)
                    instr = fn(eng)
                    for h, k in incs:
                        if instr is None:
                            instr = eng.nop()
                        instr = instr.then_inc(h, k)
            return body

        for name in ENGINES:
            if self.ops[name]:
                getattr(block, name)(make(name))

    def counts(self):
        return {e: len(self.ops[e]) for e in ENGINES}


def build_nc():
    nc = bass.Bass("TRN2", target_bir_lowering=False)

    x0_in = nc.dram_tensor("x0_in", [T, D], F32, kind="ExternalInput").ap()
    wp_in = nc.dram_tensor("wp_in", [4, 128, 16 * 1024], F32, kind="ExternalInput").ap()
    b_in = nc.dram_tensor("b_in", [1, 4096], F32, kind="ExternalInput").ap()
    gw_in = nc.dram_tensor("gw_in", [128, 128], F32, kind="ExternalInput").ap()
    w1_in = nc.dram_tensor("w1_in", [NE, 32, 128, 2048], F32, kind="ExternalInput").ap()
    w3_in = nc.dram_tensor("w3_in", [NE, 32, 128, 2048], F32, kind="ExternalInput").ap()
    w2_in = nc.dram_tensor("w2_in", [NE, 2, NHC, 128, 2048], F32, kind="ExternalInput").ap()
    cs_in = nc.dram_tensor("cs_in", [2, NTT, 128, 256], F32, kind="ExternalInput").ap()
    sn_in = nc.dram_tensor("sn_in", [2, NTT, 128, 256], F32, kind="ExternalInput").ap()
    idb_in = nc.dram_tensor("idb_in", [128, 128], BF16, kind="ExternalInput").ap()
    idf_in = nc.dram_tensor("idf_in", [128, 128], F32, kind="ExternalInput").ap()
    onc_in = nc.dram_tensor("onc_in", [128, 1], F32, kind="ExternalInput").ap()
    onr_in = nc.dram_tensor("onr_in", [1, 128], F32, kind="ExternalInput").ap()
    onb_in = nc.dram_tensor("onb_in", [1, 128], BF16, kind="ExternalInput").ap()
    oh_in = nc.dram_tensor("oh_in", [8, 8 * 128], BF16, kind="ExternalInput").ap()
    out_d = nc.dram_tensor("out", [T, D], F32, kind="ExternalOutput").ap()
    x1_d = nc.dram_tensor("x1_d", [NTT, 128, D], F32, kind="Internal").ap()
    l2t_d = nc.dram_tensor("l2t_d", [NDC, NTT, 128, 128], BF16, kind="Internal").ap()
    ar_src_h = nc.dram_tensor("ar_src", [128, 16], F32, kind="Internal")
    ar_dst_h = nc.dram_tensor("ar_dst", [128, 16], F32, kind="Internal")
    ar_src_flat = ar_src_h.ap().rearrange("p g -> (p g)").unsqueeze(0)
    ar_dst_flat = ar_dst_h.ap().rearrange("p g -> (p g)").unsqueeze(0)

    dbg = bool(os.environ.get("KDBG"))
    dbgt = {}
    if dbg:
        dbgt["afin"] = nc.dram_tensor("dbg_afin", [1, 2048], F32, kind="ExternalOutput").ap()
        dbgt["stats"] = nc.dram_tensor("dbg_stats", [128, 192], F32, kind="ExternalOutput").ap()
        dbgt["scores"] = nc.dram_tensor("dbg_scores", [128, 128], F32, kind="ExternalOutput").ap()
        dbgt["coef"] = nc.dram_tensor("dbg_coef", [128, 128], F32, kind="ExternalOutput").ap()
        dbgt["coefT"] = nc.dram_tensor("dbg_coefT", [8, 2048], BF16, kind="ExternalOutput").ap()
        dbgt["ln1T"] = nc.dram_tensor("dbg_ln1T", [128, NDC * T], BF16, kind="ExternalOutput").ap()
        dbgt["x1"] = nc.dram_tensor("dbg_x1", [NTT, 128, D], F32, kind="ExternalOutput").ap()
        dbgt["l2t"] = nc.dram_tensor("dbg_l2t", [NDC, NTT, 128, 128], BF16,
                                     kind="ExternalOutput").ap()
        dbgt["yac"] = nc.dram_tensor("dbg_yac", [128, NHC * 1024], BF16,
                                     kind="ExternalOutput").ap()
        dbgt["yach0"] = nc.dram_tensor("dbg_yach0", [128, NHC * 1024], BF16,
                                       kind="ExternalOutput").ap()
        dbgt["cb0"] = nc.dram_tensor("dbg_cb0", [128, 1024], F32, kind="ExternalOutput").ap()
        dbgt["hp"] = nc.dram_tensor("dbg_hp", [128, NFL * 1024], BF16,
                                    kind="ExternalOutput").ap()

    sch = Sched()
    E = sch.emit

    with ExitStack() as stk:
        en = stk.enter_context

        def sem(name):
            return Sem(en(nc.semaphore(name)))

        SX = [sem("sx0"), sem("sx1")]
        STB = [sem("stb0"), sem("stb1")]
        SWB = [sem("swb0"), sem("swb1")]
        SW1 = [sem("sw1a"), sem("sw1b")]
        SW3 = [sem("sw3a"), sem("sw3b")]
        SW2 = [sem("sw2a"), sem("sw2b")]
        ST2 = [sem("st2a"), sem("st2b")]
        SLN = sem("sln")
        SCQ = sem("scq")
        SST = sem("sst")
        SARL = sem("sarl")
        SCS = sem("scs")
        SCG = sem("scg")
        CO = sem("co")
        M = sem("m")
        V = sem("v")
        A_ = sem("a")

        P = [en(nc.psum_tensor(f"ps{i}", [128, 512], F32)) for i in range(6)]
        PT = [en(nc.psum_tensor("pt0", [128, 1024], BF16)),
              en(nc.psum_tensor("pt1", [128, 1024], BF16))]
        psA = P[4][0:1, 0:512]
        psS = P[5][:, 0:8]
        psC = P[5][0:8, 128:256]
        psfree = {}

        def ps_wait(key):
            w = psfree.get(key)
            return [w] if w else []

        stats = en(nc.sbuf_tensor("stats", [128, 16 * 12], F32))

        def stc(q, tt=None):
            if tt is None:
                return stats[:, q * 16:(q + 1) * 16]
            return stats[:, q * 16 + tt:q * 16 + tt + 1]
        scores_sb = en(nc.sbuf_tensor("scores", [128, 128], F32))
        coefT = en(nc.sbuf_tensor("coefT", [8, 2048], BF16))
        xsl = [en(nc.sbuf_tensor("xs0", [128, D], F32)),
               en(nc.sbuf_tensor("xs1", [128, D], F32))]
        idb = en(nc.sbuf_tensor("idb", [128, 128], BF16))
        oh = en(nc.sbuf_tensor("oh", [8, 8 * 128], BF16))
        cpi2 = en(nc.sbuf_tensor("cpi2", [128, 1], F32))
        ceps = en(nc.sbuf_tensor("ceps", [128, 1], F32))

        sA = ExitStack()
        enA = sA.enter_context
        ln1T = enA(nc.sbuf_tensor("ln1T", [128, NDC * T], BF16))
        WbT = enA(nc.sbuf_tensor("Wb", [128, 2 * NDC * 1024], BF16))
        Wbs = [WbT[:, 0:NDC * 1024], WbT[:, NDC * 1024:2 * NDC * 1024]]
        ATMP = enA(nc.sbuf_tensor("ATMP", [128, 4096], F32))
        KH = ATMP[:, 0:512]
        G1 = ATMP[:, 512:1024]
        G2 = ATMP[:, 1024:1536]
        RK = ATMP[:, 1536:2048]
        RB = ATMP[:, 2048:2560]
        CKb = ATMP[:, 2560:3072]
        CVb = ATMP[:, 3072:3584]
        PSb = ATMP[:, 3584:4096]
        PSbs = [PSb, RB]
        psb_free = {}
        A_sb = enA(nc.sbuf_tensor("A_sb", [1, 2048], F32))
        A_fin = ATMP[0:1, 0:2048]
        A_bc = enA(nc.sbuf_tensor("A_bc", [128, 2048], F32))
        tabs = enA(nc.sbuf_tensor("tabs", [128, 2 * 256], F32))

        def tab(kind, slot):
            return tabs[:, kind * 256:(kind + 1) * 256]
        ln_f32 = enA(nc.sbuf_tensor("ln_f32", [128, D], F32))
        u_buf = ATMP[:, 0:2048]
        ln_bf = enA(nc.sbuf_tensor("ln_bf", [128, D], BF16))
        b_sb = enA(nc.sbuf_tensor("b_sb", [1, 4096], BF16))
        gwT = enA(nc.sbuf_tensor("gwT", [128, 128], BF16))
        onc = enA(nc.sbuf_tensor("onc", [128, 1], F32))
        onr = enA(nc.sbuf_tensor("onr", [1, 128], F32))
        onb = enA(nc.sbuf_tensor("onb", [1, 128], BF16))
        idf = enA(nc.sbuf_tensor("idf", [128, 128], F32))
        t2t = [enA(nc.sbuf_tensor("t2t0", [128, 128], BF16)),
               enA(nc.sbuf_tensor("t2t1", [128, 128], BF16))]
        gT1 = ATMP[:, 0:128]
        gE1 = ATMP[:, 128:256]
        gAB = ATMP[:, 256:384]
        gU1 = ATMP[:, 384:512]
        gS2 = ATMP[:, 512:640]
        gTO = ATMP[:, 640:768]
        gE2 = ATMP[:, 768:896]
        gCM = ATMP[:, 896:1024]
        gst = ATMP[:, 1024:1216]

        def gss(q):
            return gst[:, q * 16:(q + 1) * 16]

        E("sync", lambda e: e.dma_start(idb[:], idb_in), incs=[(SCS, 16)])
        E("sync", lambda e: e.dma_start(idf[:], idf_in), incs=[(SCS, 16)])
        E("sync", lambda e: e.dma_start(onc[:], onc_in), incs=[(SCS, 16)])
        E("sync", lambda e: e.dma_start(onr[:], onr_in), incs=[(SCS, 16)])
        E("sync", lambda e: e.dma_start(onb[:], onb_in), incs=[(SCS, 16)])
        E("sync", lambda e: e.dma_start(oh[:], oh_in), incs=[(SCS, 16)])
        consts_s = SCS.n
        E("gpsimd", lambda e: e.dma_start(gwT[:], gw_in), incs=[(SCG, 16)])
        E("gpsimd", lambda e: e.dma_start(b_sb[:], b_in), incs=[(SCG, 16)])
        consts_g = SCG.n
        E("vector", lambda e: e.memset(cpi2[:], HALFPI), incs=[(V, 1)])
        E("vector", lambda e: e.memset(ceps[:], EPS), incs=[(V, 1)])

        p2_lnb = {}
        p_rste = {}
        m_lastT = 0
        psq = {}
        for tt in range(NTT):
            sl = tt % 2
            w = []
            if tt >= 2:
                w = [(V, p2_lnb[tt - 2])]
            E("sync",
              lambda e, tt=tt, sl=sl: e.dma_start(xsl[sl][:], x0_in[tt * 128:(tt + 1) * 128, :]),
              waits=w, incs=[(SX[sl], 16)])
            lw = [(SX[sl], SX[sl].n)]
            E("vector",
              lambda e, tt=tt, sl=sl: e.reduce_sum(stc(0, tt), xsl[sl][:], AX.X),
              waits=lw, incs=[(V, 1)])
            E("scalar",
              lambda e, tt=tt, sl=sl: e.activation(ln_f32[:], xsl[sl][:], AF.Square,
                                                   accum_out=stc(1, tt)),
              waits=lw + ([(A_, psq[tt - 1])] if tt > 0 else []), incs=[(A_, 1)])
            a_sq = A_.n
            psq[tt] = a_sq
            E("vector", lambda e, tt=tt: e.tensor_scalar(stc(2, tt), stc(0, tt), 1.0 / D,
                                                         None, ALU.mult),
              waits=[(A_, a_sq), (V, V.n)], incs=[(V, 1)])
            E("vector", lambda e, tt=tt: e.tensor_scalar(stc(9, tt), stc(1, tt), 1.0 / D,
                                                         None, ALU.mult),
              waits=[(V, V.n)], incs=[(V, 1)])
            E("vector", lambda e, tt=tt: e.tensor_tensor(stc(10, tt), stc(2, tt), stc(2, tt),
                                                         ALU.mult),
              waits=[(V, V.n)], incs=[(V, 1)])
            E("vector", lambda e, tt=tt: e.tensor_tensor(stc(11, tt), stc(9, tt), stc(10, tt),
                                                         ALU.subtract),
              waits=[(V, V.n)], incs=[(V, 1)])
            v_var = V.n
            E("scalar", lambda e, tt=tt: e.activation(stc(9, tt), stc(11, tt), AF.Ln,
                                                      bias=ceps[:]),
              waits=[(V, v_var)], incs=[(A_, 1)])
            E("vector", lambda e, tt=tt: e.tensor_scalar(stc(10, tt), stc(9, tt), -0.5,
                                                         None, ALU.mult),
              waits=[(A_, A_.n), (V, V.n)], incs=[(V, 1)])
            E("scalar", lambda e, tt=tt: e.activation(stc(3, tt), stc(10, tt), AF.Exp),
              waits=[(V, V.n)], incs=[(A_, 1)])
            a_rstd1 = A_.n
            p_rste[tt] = a_rstd1
            E("vector",
              lambda e, tt=tt, sl=sl: e.tensor_scalar(ln_bf[:], xsl[sl][:], stc(2, tt),
                                                      stc(3, tt), ALU.subtract, ALU.mult),
              waits=[(A_, a_rstd1), (V, V.n), (M, m_lastT)], incs=[(V, 1)])
            v_lnb = V.n
            p2_lnb[tt] = v_lnb
            for dc in range(NDC):
                pslot = dc % 2
                w = [(V, v_lnb), (SCS, consts_s)] + ps_wait(("T", pslot))
                E("tensor",
                  lambda e, dc=dc, pslot=pslot: e.transpose(
                      PT[pslot][:, 0:128], ln_bf[:, dc * 128:(dc + 1) * 128], idb[:]),
                  waits=w, incs=[(M, 1)])
                mt = M.n
                E("vector",
                  lambda e, tt=tt, dc=dc, pslot=pslot: e.tensor_copy(
                      ln1T[:, dc * T + tt * 128: dc * T + (tt + 1) * 128], PT[pslot][:, 0:128]),
                  waits=[(M, mt)], incs=[(V, 1)])
                psfree[("T", pslot)] = (V, V.n)
            m_lastT = M.n
        v_p2_done = V.n
        v_m1s = V.n
        a_rstd1 = A_.n

        m_wb_free = {}
        v_Acp = 0
        v_tabfree = {}
        a_sin_prev = 0
        m_A_prev = 0
        v_P_prev = 0
        pend_A = None
        for np_ in range(4):
            rot = np_ < 2
            wsl = np_ % 2
            Wb = Wbs[wsl]
            E("gpsimd",
              lambda e, np_=np_, wsl=wsl: e.dma_start(Wbs[wsl][:], wp_in[np_]),
              waits=([(M, m_wb_free[wsl])] if wsl in m_wb_free else []) + [(SCG, consts_g)],
              incs=[(SWB[wsl], 16)])
            swb_n = SWB[wsl].n
            for tt in range(NTT):
                pr = tt % 2
                psK, psV = P[2 * pr], P[2 * pr + 1]
                kK, kV = 2 * pr, 2 * pr + 1
                if rot:
                    tsl = 0
                    wtb = ([v_tabfree["last"]] if "last" in v_tabfree else [])
                    E("sync",
                      lambda e, np_=np_, tt=tt: e.dma_start(tab(0, 0), cs_in[np_, tt]),
                      waits=wtb, incs=[(STB[0], 16)])
                    E("sync",
                      lambda e, np_=np_, tt=tt: e.dma_start(tab(1, 0), sn_in[np_, tt]),
                      incs=[(STB[0], 16)])
                    stb_n = STB[0].n
                wk = [(SWB[wsl], swb_n)] + ps_wait(kK)
                if np_ == 0 and tt == 0:
                    wk.append((V, v_p2_done))
                for dc in range(NDC):
                    E("tensor",
                      lambda e, tt=tt, dc=dc, psK=psK, Wb=Wb: e.matmul(
                          psK[:], ln1T[:, dc * T + tt * 128: dc * T + (tt + 1) * 128],
                          Wb[:, dc * 1024: dc * 1024 + 512], start=(dc == 0), stop=False),
                      waits=(wk if dc == 0 else []))
                E("tensor",
                  lambda e, np_=np_, psK=psK: e.matmul(
                      psK[:], onb[:], b_sb[0:1, np_ * 1024: np_ * 1024 + 512],
                      start=False, stop=True),
                  incs=[(M, 1)])
                m_k = M.n
                wv = ps_wait(kV)
                for dc in range(NDC):
                    E("tensor",
                      lambda e, tt=tt, dc=dc, psV=psV, Wb=Wb: e.matmul(
                          psV[:], ln1T[:, dc * T + tt * 128: dc * T + (tt + 1) * 128],
                          Wb[:, dc * 1024 + 512: dc * 1024 + 1024], start=(dc == 0), stop=False),
                      waits=(wv if dc == 0 else []))
                E("tensor",
                  lambda e, np_=np_, psV=psV: e.matmul(
                      psV[:], onb[:], b_sb[0:1, np_ * 1024 + 512: np_ * 1024 + 1024],
                      start=False, stop=True),
                  incs=[(M, 1)])
                m_v = M.n

                def rotary(ps, m_ps, stb_val, tslot):
                    ev, od = ps[:, 0:512:2], ps[:, 1:512:2]
                    ct, st = tab(0, tslot), tab(1, tslot)
                    E("vector", lambda e, ev=ev, ct=ct: e.tensor_tensor(
                        G1[:, 0:256], ev, ct, ALU.mult),
                      waits=[(M, m_ps), (STB[tslot], stb_val), (V, V.n)], incs=[(V, 1)])
                    E("vector", lambda e, od=od, st=st: e.tensor_tensor(
                        G2[:, 0:256], od, st, ALU.mult), waits=[(V, V.n)], incs=[(V, 1)])
                    E("vector", lambda e: e.tensor_tensor(
                        KH[:, 0:512:2], G1[:, 0:256], G2[:, 0:256], ALU.subtract),
                      waits=[(V, V.n)], incs=[(V, 1)])
                    E("vector", lambda e, od=od, ct=ct: e.tensor_tensor(
                        G1[:, 0:256], od, ct, ALU.mult), waits=[(V, V.n)], incs=[(V, 1)])
                    E("vector", lambda e, ev=ev, st=st: e.tensor_tensor(
                        G2[:, 0:256], ev, st, ALU.mult), waits=[(V, V.n)], incs=[(V, 1)])
                    E("vector", lambda e: e.tensor_tensor(
                        KH[:, 1:512:2], G1[:, 0:256], G2[:, 0:256], ALU.add),
                      waits=[(V, V.n)], incs=[(V, 1)])
                    return V.n

                def cos_chain(src_ap, out_sb, first_w, rk_w, sin_w):
                    E("vector", lambda e, src_ap=src_ap: e.tensor_scalar(
                        G1[:], src_ap, INV2PI, 0.25, ALU.mult, ALU.add),
                      waits=list(first_w) + [(V, V.n)], incs=[(V, 1)])
                    E("vector", lambda e: e.tensor_scalar(
                        G2[:], G1[:], MAGIC, MAGIC, ALU.add, ALU.subtract), incs=[(V, 1)])
                    E("vector", lambda e: e.tensor_scalar(
                        G1[:], G2[:], TWOPI, None, ALU.mult), incs=[(V, 1)])
                    E("vector", lambda e, src_ap=src_ap: e.tensor_tensor(
                        RK[:], src_ap, G1[:], ALU.subtract), waits=rk_w, incs=[(V, 1)])
                    vr = V.n
                    E("scalar", lambda e, out_sb=out_sb: e.activation(
                        out_sb[:], RK[:], AF.Sin, bias=cpi2[:]),
                      waits=[(V, vr)] + sin_w, incs=[(A_, 1)])
                    return A_.n

                rk_guard = ([(A_, a_sin_prev)] if a_sin_prev else []) + \
                           ([(M, m_A_prev)] if m_A_prev else [])
                sin_guard = [(V, v_P_prev)] if v_P_prev else []
                if rot:
                    rotary(psK, m_k, stb_n, tsl)
                    v_k_done = V.n
                    aK = cos_chain(KH[:], CKb, [], rk_guard, sin_guard)
                    rotary(psV, m_v, stb_n, tsl)
                    v_v_done = V.n
                    aV = cos_chain(KH[:], CVb, [], [(A_, aK)], sin_guard)
                    v_tabfree["last"] = (V, v_v_done)
                else:
                    aK = cos_chain(psK[:], CKb, [(M, m_k)], rk_guard, sin_guard)
                    v_k_done = V.n
                    aV = cos_chain(psV[:], CVb, [(M, m_v)], [(A_, aK)], sin_guard)
                    v_v_done = V.n
                psfree[kK] = (V, v_k_done)
                psfree[kV] = (V, v_v_done)
                psb = PSbs[tt % 2]
                E("vector", lambda e, psb=psb: e.tensor_tensor(psb[:], CKb[:], CVb[:], ALU.mult),
                  waits=[(A_, aK), (A_, aV)] +
                        ([(M, psb_free[tt % 2])] if (tt % 2) in psb_free else []),
                  incs=[(V, 1)])
                v_P = V.n
                if pend_A is not None:
                    ptt, pv, pfirst, ppsb = pend_A
                    wA = [(V, pv)]
                    if pfirst:
                        wA += [(SCS, consts_s)] + ([(V, v_Acp)] if v_Acp else [])
                    E("tensor",
                      lambda e, ppsb=ppsb, pfirst=pfirst: e.matmul(
                          psA, onc[:], ppsb[:], start=pfirst, stop=False),
                      waits=wA, incs=[(M, 1)])
                    psb_free[ptt % 2] = M.n
                pend_A = (tt, v_P, (tt == 0), psb)
                a_sin_prev = aV
                v_P_prev = v_P
            ptt, pv, pfirst, ppsb = pend_A
            E("tensor",
              lambda e, ppsb=ppsb, pfirst=pfirst: e.matmul(
                  psA, onc[:], ppsb[:], start=pfirst, stop=True),
              waits=[(V, pv)] + ([(SCS, consts_s)] if pfirst else []), incs=[(M, 1)])
            psb_free[ptt % 2] = M.n
            pend_A = None
            m_A_prev = M.n
            m_endKV = M.n
            E("vector",
              lambda e, np_=np_: e.tensor_copy(A_sb[0:1, np_ * 512:(np_ + 1) * 512], psA),
              waits=[(M, m_A_prev)], incs=[(V, 1)])
            v_Acp = V.n
            m_wb_free[wsl] = M.n
        psfree[4] = (V, v_Acp)

        E("scalar", lambda e: e.dma_start(ar_src_flat, A_sb[:]),
          waits=[(V, v_Acp)], incs=[(SCQ, 16)])
        E("gpsimd", lambda e: e.collective_compute(
            "AllReduce", ALU.add, replica_groups=[list(range(NCORES))],
            ins=[ar_src_h.ap().opt()], outs=[ar_dst_h.ap().opt()]),
          waits=[(SCQ, SCQ.n)], incs=[(CO, 1)])
        pre_sx = {}
        for tt in range(2):
            E("sync",
              lambda e, tt=tt: e.dma_start(xsl[tt][:], x0_in[tt * 128:(tt + 1) * 128, :]),
              waits=[(V, p2_lnb[NTT - 2 + tt])], incs=[(SX[tt], 16)])
            pre_sx[tt] = SX[tt].n
        E("sync", lambda e: e.dma_start(A_sb[:], ar_dst_flat),
          waits=[(CO, 1)], incs=[(SARL, 16)])
        E("vector", lambda e: e.tensor_scalar(A_fin[:], A_sb[:], 0.5, 0.5, ALU.mult, ALU.add),
          waits=[(SARL, 16)], incs=[(V, 1)])
        v_afin = V.n
        for i in range(4):
            E("tensor",
              lambda e, i=i: e.matmul(P[i][:], onr[:], A_fin[0:1, i * 512:(i + 1) * 512],
                                      start=True, stop=True),
              waits=[(V, v_afin)] + ps_wait(i), incs=[(M, 1)])
            mi = M.n
            E("vector",
              lambda e, i=i: e.tensor_copy(A_bc[:, i * 512:(i + 1) * 512], P[i][:]),
              waits=[(M, mi)], incs=[(V, 1)])
            psfree[i] = (V, V.n)
        v_abc = V.n

        a2 = {}
        x1w = {}
        a2sq = {}
        a2c_lnb = {}
        m_lastT2 = m_lastT
        for tt in range(NTT):
            sl = tt % 2
            if tt < 2:
                lw = (SX[sl], pre_sx[tt])
            else:
                E("sync",
                  lambda e, tt=tt, sl=sl: e.dma_start(xsl[sl][:],
                                                      x0_in[tt * 128:(tt + 1) * 128, :]),
                  waits=[(V, a2[tt - 2])], incs=[(SX[sl], 16)])
                lw = (SX[sl], SX[sl].n)
            lnw = [lw, (V, v_abc), (V, v_m1s), (A_, a_rstd1)]
            if tt > 0:
                lnw += [(A_, a2sq[tt - 1]), (SCQ, x1w[tt - 1])]
            E("vector",
              lambda e, tt=tt, sl=sl: e.tensor_scalar(ln_f32[:], xsl[sl][:], stc(2, tt),
                                                      stc(3, tt), ALU.subtract, ALU.mult),
              waits=lnw, incs=[(V, 1)])
            E("vector", lambda e: e.tensor_tensor(u_buf[:], A_bc[:], ln_f32[:], ALU.mult),
              incs=[(V, 1)])
            E("vector", lambda e, sl=sl: e.tensor_tensor(ln_f32[:], xsl[sl][:], u_buf[:],
                                                         ALU.add), incs=[(V, 1)])
            v_x1 = V.n
            a2[tt] = v_x1
            E("scalar", lambda e, tt=tt: e.dma_start(x1_d[tt], ln_f32[:]),
              waits=[(V, v_x1)], incs=[(SCQ, 16)])
            x1w[tt] = SCQ.n
            E("vector", lambda e, tt=tt: e.reduce_sum(stc(5, tt), ln_f32[:], AX.X),
              incs=[(V, 1)])
            E("scalar", lambda e, tt=tt: e.activation(u_buf[:], ln_f32[:], AF.Square,
                                                      accum_out=stc(6, tt)),
              waits=[(V, v_x1)], incs=[(A_, 1)])
            a_sq = A_.n
            a2sq[tt] = a_sq
            E("vector", lambda e, tt=tt: e.tensor_scalar(stc(7, tt), stc(5, tt), 1.0 / D,
                                                         None, ALU.mult),
              waits=[(A_, a_sq), (V, V.n)], incs=[(V, 1)])
            E("vector", lambda e, tt=tt: e.tensor_scalar(stc(9, tt), stc(6, tt), 1.0 / D,
                                                         None, ALU.mult),
              waits=[(V, V.n)], incs=[(V, 1)])
            E("vector", lambda e, tt=tt: e.tensor_tensor(stc(10, tt), stc(7, tt), stc(7, tt),
                                                         ALU.mult),
              waits=[(V, V.n)], incs=[(V, 1)])
            E("vector", lambda e, tt=tt: e.tensor_tensor(stc(11, tt), stc(9, tt), stc(10, tt),
                                                         ALU.subtract),
              waits=[(V, V.n)], incs=[(V, 1)])
            v_var = V.n
            E("scalar", lambda e, tt=tt: e.activation(stc(9, tt), stc(11, tt), AF.Ln,
                                                      bias=ceps[:]),
              waits=[(V, v_var)] + ([(A_, a2rste[tt - 1])] if tt > 0 else []),
              incs=[(A_, 1)])
            E("vector", lambda e, tt=tt: e.tensor_scalar(stc(10, tt), stc(9, tt), -0.5,
                                                         None, ALU.mult),
              waits=[(A_, A_.n), (V, V.n)], incs=[(V, 1)])
            E("scalar", lambda e, tt=tt: e.activation(stc(8, tt), stc(10, tt), AF.Exp),
              waits=[(V, V.n)], incs=[(A_, 1)])
            a_rstd2 = A_.n
            if tt == 0:
                a2rste = {}
            a2rste[tt] = a_rstd2
            E("vector",
              lambda e, tt=tt: e.tensor_scalar(ln_bf[:], ln_f32[:], stc(7, tt),
                                               stc(8, tt), ALU.subtract, ALU.mult),
              waits=[(A_, a_rstd2), (V, V.n), (M, m_lastT2)], incs=[(V, 1)])
            v_lnb = V.n
            a2c_lnb[tt] = v_lnb
            for dc in range(NDC):
                pslot = dc % 2
                w = [(V, v_lnb)] + ps_wait(("T", pslot))
                E("tensor",
                  lambda e, dc=dc, pslot=pslot: e.transpose(
                      PT[pslot][:, 0:128], ln_bf[:, dc * 128:(dc + 1) * 128], idb[:]),
                  waits=w, incs=[(M, 1)])
                mt = M.n
                wv = [(M, mt)]
                if ST2[pslot].n > 0:
                    wv.append((ST2[pslot], ST2[pslot].n))
                E("vector",
                  lambda e, pslot=pslot: e.tensor_copy(t2t[pslot][:], PT[pslot][:, 0:128]),
                  waits=wv, incs=[(V, 1)])
                psfree[("T", pslot)] = (V, V.n)
                vev = V.n
                E("scalar",
                  lambda e, dc=dc, tt=tt, pslot=pslot: e.dma_start(l2t_d[dc, tt],
                                                                   t2t[pslot][:]),
                  waits=[(V, vev)], incs=[(ST2[pslot], 16)])
                wmm = [(V, vev)]
                if dc == 0:
                    wmm += [(SCG, consts_g)] + ps_wait(5)
                E("tensor",
                  lambda e, dc=dc, pslot=pslot: e.matmul(
                      psS, t2t[pslot][:], gwT[:, dc * 8:(dc + 1) * 8],
                      start=(dc == 0), stop=(dc == NDC - 1)),
                  waits=wmm, incs=([(M, 1)] if dc == NDC - 1 else []))
            m_lastT2 = M.n
            E("vector",
              lambda e, tt=tt: e.tensor_copy(scores_sb[:, tt * 8:(tt + 1) * 8], psS),
              waits=[(M, m_lastT2)], incs=[(V, 1)])
            psfree[5] = (V, V.n)
            if tt == 7:
                scq_l2t_half0 = [ST2[0].n, ST2[1].n]
        scq_x1 = SCQ.n
        scq_l2t = [ST2[0].n, ST2[1].n]
        scq_l2t_h0 = list(scq_l2t_half0)

        def g3(t):
            a = t if isinstance(t, bass.AP) else t.ap()
            return a.rearrange("p (g k) -> p g k", k=8)

        def bc16(t16):
            return t16.unsqueeze(2).broadcast_to([128, 16, 8])

        v_scores = V.n

        def gv(fn, waits=()):
            E("vector", fn, waits=[(V, V.n)] + list(waits), incs=[(V, 1)])
            return V.n

        def ga(fn, wv):
            E("scalar", fn, waits=[(V, wv)], incs=[(A_, 1)])
            return A_.n

        def gw(fn, wa):
            E("vector", fn, waits=[(A_, wa), (V, V.n)], incs=[(V, 1)])
            return V.n

        BIG = 1.0e30
        v_nv1 = gv(lambda e: e.tensor_reduce(gss(0), g3(scores_sb), AX.X, ALU.max, negate=True))
        gv(lambda e: e.tensor_tensor(g3(gT1), g3(scores_sb), bc16(gss(0)), ALU.add),
           waits=[(V, v_nv1)])
        gv(lambda e: e.tensor_scalar(gS2[:], gT1[:], -BIG, None, ALU.mult))
        gv(lambda e: e.tensor_scalar(gCM[:], gS2[:], 1.0, None, ALU.min))
        gv(lambda e: e.tensor_scalar(gE1[:], gCM[:], -1.0, 1.0, ALU.mult, ALU.add))
        ax = ga(lambda e: e.activation(gAB[:], scores_sb[:], AF.Abs), v_scores)
        v_nm1 = gw(lambda e: e.tensor_scalar(gss(1), gss(0), -1.0, None, ALU.mult), ax)
        gv(lambda e: e.tensor_tensor(g3(gU1), g3(gAB), bc16(gss(1)), ALU.max),
           waits=[(V, v_nm1)])
        gv(lambda e: e.tensor_scalar(gCM[:], gU1[:], 0.02, None, ALU.mult))
        gv(lambda e: e.tensor_tensor(gS2[:], gT1[:], gCM[:], ALU.add))
        gv(lambda e: e.tensor_scalar(gCM[:], gS2[:], -BIG, None, ALU.mult))
        gv(lambda e: e.tensor_scalar(gU1[:], gCM[:], 1.0, None, ALU.min))
        gv(lambda e: e.tensor_scalar(gCM[:], gU1[:], 0.0, None, ALU.max))
        gv(lambda e: e.tensor_scalar(gS2[:], gCM[:], NEG, None, ALU.mult))
        v_sm1 = gv(lambda e: e.tensor_tensor(gU1[:], gT1[:], gS2[:], ALU.add))
        a_e1 = ga(lambda e: e.activation(gTO[:], gU1[:], AF.Exp), v_sm1)
        v_d1 = gw(lambda e: e.tensor_reduce(gss(2), g3(gTO), AX.X, ALU.add), a_e1)
        a_l1 = ga(lambda e: e.activation(gss(3), gss(2), AF.Ln), v_d1)
        v_l1 = gw(lambda e: e.tensor_scalar(gss(9), gss(3), -1.0, None, ALU.mult), a_l1)
        a_r1 = ga(lambda e: e.activation(gss(3), gss(9), AF.Exp), v_l1)
        gv(lambda e: e.tensor_tensor(gU1[:], gTO[:], gE1[:], ALU.mult))
        gv(lambda e: e.tensor_reduce(gss(4), g3(gU1), AX.X, ALU.add))
        v_m1v = gw(lambda e: e.tensor_tensor(gss(5), gss(4), gss(3), ALU.mult), a_r1)
        gv(lambda e: e.tensor_scalar(gS2[:], gE1[:], NEG, None, ALU.mult))
        gv(lambda e: e.tensor_tensor(gU1[:], scores_sb[:], gS2[:], ALU.add))
        v_nv2 = gv(lambda e: e.tensor_reduce(gss(6), g3(gU1), AX.X, ALU.max, negate=True))
        gv(lambda e: e.tensor_tensor(g3(gTO), g3(scores_sb), bc16(gss(6)), ALU.add),
           waits=[(V, v_nv2)])
        gv(lambda e: e.tensor_tensor(g3(gS2), g3(gU1), bc16(gss(6)), ALU.add))
        gv(lambda e: e.tensor_scalar(gCM[:], gS2[:], -BIG, None, ALU.mult))
        gv(lambda e: e.tensor_scalar(gU1[:], gCM[:], 1.0, None, ALU.min))
        gv(lambda e: e.tensor_scalar(gE2[:], gU1[:], -1.0, 1.0, ALU.mult, ALU.add))
        v_nm2 = gv(lambda e: e.tensor_scalar(gss(7), gss(6), -1.0, None, ALU.mult))
        gv(lambda e: e.tensor_tensor(g3(gS2), g3(gAB), bc16(gss(7)), ALU.max),
           waits=[(V, v_nm2)])
        gv(lambda e: e.tensor_scalar(gU1[:], gS2[:], 0.02, None, ALU.mult))
        gv(lambda e: e.tensor_tensor(gS2[:], gTO[:], gU1[:], ALU.add))
        gv(lambda e: e.tensor_scalar(gCM[:], gS2[:], -BIG, None, ALU.mult))
        gv(lambda e: e.tensor_scalar(gU1[:], gCM[:], 1.0, None, ALU.min))
        gv(lambda e: e.tensor_scalar(gCM[:], gU1[:], 0.0, None, ALU.max))
        gv(lambda e: e.tensor_scalar(gU1[:], gCM[:], NEG, None, ALU.mult))
        gv(lambda e: e.tensor_scalar(gCM[:], gE1[:], NEG, None, ALU.mult))
        gv(lambda e: e.tensor_tensor(gS2[:], gU1[:], gCM[:], ALU.add))
        v_sm2 = gv(lambda e: e.tensor_tensor(gU1[:], gTO[:], gS2[:], ALU.add))
        a_e2 = ga(lambda e: e.activation(gCM[:], gU1[:], AF.Exp), v_sm2)
        v_d2 = gw(lambda e: e.tensor_reduce(gss(8), g3(gCM), AX.X, ALU.add), a_e2)
        a_l2 = ga(lambda e: e.activation(gss(9), gss(8), AF.Ln), v_d2)
        v_l2 = gw(lambda e: e.tensor_scalar(gss(10), gss(9), -1.0, None, ALU.mult), a_l2)
        a_r2 = ga(lambda e: e.activation(gss(9), gss(10), AF.Exp), v_l2)
        gv(lambda e: e.tensor_tensor(gU1[:], gCM[:], gE2[:], ALU.mult))
        gv(lambda e: e.tensor_reduce(gss(10), g3(gU1), AX.X, ALU.add))
        v_m2v = gw(lambda e: e.tensor_tensor(gss(11), gss(10), gss(9), ALU.mult), a_r2)
        gv(lambda e: e.tensor_tensor(g3(gT1), g3(gE1), bc16(gss(5)), ALU.mult),
           waits=[(V, v_m1v)])
        gv(lambda e: e.tensor_tensor(g3(gS2), g3(gE2), bc16(gss(11)), ALU.mult),
           waits=[(V, v_m2v)])
        v_coef = gv(lambda e: e.tensor_tensor(gU1[:], gT1[:], gS2[:], ALU.add))
        for tt in range(NTT):
            w = [(V, v_coef), (SCS, consts_s)] + ps_wait(5)
            E("tensor",
              lambda e, tt=tt: e.transpose(psC, gU1[:, tt * 8:(tt + 1) * 8], idf[:]),
              waits=w, incs=[(M, 1)])
            mt = M.n
            E("vector",
              lambda e, tt=tt: e.tensor_copy(coefT[0:8, tt * 128:(tt + 1) * 128], psC),
              waits=[(M, mt)], incs=[(V, 1)])
            psfree[5] = (V, V.n)
        v_coefT = V.n
        m_endA = M.n
        v_endA = V.n
        a_endA = A_.n

        SDB = sem("sdb")
        if dbg:
            E("sync", lambda e: e.dma_start(dbgt["afin"], A_fin[:]),
              waits=[(V, v_afin)], incs=[(SDB, 16)])
            E("sync", lambda e: e.dma_start(dbgt["stats"], stats[:]),
              waits=[(A_, a_rstd2)], incs=[(SDB, 16)])
            E("sync", lambda e: e.dma_start(dbgt["scores"], scores_sb[:]),
              waits=[(V, v_scores)], incs=[(SDB, 16)])
            E("sync", lambda e: e.dma_start(dbgt["coef"], gU1[:]),
              waits=[(V, v_coef)], incs=[(SDB, 16)])
            E("sync", lambda e: e.dma_start(dbgt["coefT"], coefT[:]),
              waits=[(V, v_coefT)], incs=[(SDB, 16)])
            E("sync", lambda e: e.dma_start(dbgt["ln1T"], ln1T[:]),
              waits=[(V, v_p2_done)], incs=[(SDB, 16)])

        sA.close()

        sB = ExitStack()
        enB = sB.enter_context
        hP = enB(nc.sbuf_tensor("hP", [128, NFL * 1024], BF16))
        ln2T = enB(nc.sbuf_tensor("ln2T", [128, NDC * 1024], BF16))
        yac = [enB(nc.sbuf_tensor("yac0", [128, NHC * 1024], BF16)),
               enB(nc.sbuf_tensor("yac1", [128, NHC * 1024], BF16))]
        w1b = [enB(nc.sbuf_tensor("w1b0", [128, NDC * 128], BF16)),
               enB(nc.sbuf_tensor("w1b1", [128, NDC * 128], BF16))]
        w3b = [enB(nc.sbuf_tensor("w3b0", [128, NDC * 128], BF16)),
               enB(nc.sbuf_tensor("w3b1", [128, NDC * 128], BF16))]
        w2b = [enB(nc.sbuf_tensor("w2b0", [128, NFL * 128], BF16)),
               enB(nc.sbuf_tensor("w2b1", [128, NFL * 128], BF16))]
        c_b = enB(nc.sbuf_tensor("c_b", [128, 1024], F32))
        s_tmp = [enB(nc.sbuf_tensor("stmp0", [128, 512], BF16)),
                 enB(nc.sbuf_tensor("stmp1", [128, 512], BF16))]
        ye_t = [enB(nc.sbuf_tensor("ye0", [128, 512], BF16)),
                enB(nc.sbuf_tensor("ye1", [128, 512], BF16))]
        out_t = enB(nc.sbuf_tensor("out_t", [128, D], F32))

        m_h_free = {}
        m_w2_free = {}
        a_silu = {}
        v_hprime = {}
        v_yev = {}
        m_h_last = 0
        m_endhalf = 0
        sst_last = 0
        eh_out = {}
        dbg_cbw = None
        dbg_h0w = None

        for half in range(2):
            if half == 0:
                wl = [(M, m_endKV), (ST2[0], scq_l2t_h0[0]), (ST2[1], scq_l2t_h0[1])]
            else:
                wl = [(M, m_h_last), (ST2[0], scq_l2t[0]), (ST2[1], scq_l2t[1])]
            wl += ([(SDB, SDB.n)] if (dbg and half == 0) else []) + \
                  ([dbg_h0w] if (dbg and half == 1 and dbg_h0w) else [])
            for dc in range(NDC):
                E("sync",
                  lambda e, dc=dc, half=half: e.dma_start(
                      ln2T.ap()[:, dc * 1024:(dc + 1) * 1024].rearrange(
                          "p (tl t) -> p tl t", t=128),
                      l2t_d[dc, half * 8:(half + 1) * 8].transpose([1, 0, 2])),
                  waits=(wl if dc == 0 else []), incs=[(SLN, 16)])
            sln_n = SLN.n
            E("vector", lambda e: e.memset(yac[0][:], 0.0),
              waits=(([(V, v_endA)] if half == 0 else [(M, m_endhalf)]) +
                     ([dbg_h0w] if (dbg and half == 1 and dbg_h0w) else [])),
              incs=[(V, 1)])

            for ee in range(NE):
                for i in range(2):
                    w = [(V, v_coefT)] + ps_wait(4 + i)
                    E("tensor",
                      lambda e, ee=ee, half=half, i=i: e.matmul(
                          P[4 + i][:], oh[0:8, ee * 128:(ee + 1) * 128],
                          coefT[0:8, half * 1024 + i * 512: half * 1024 + (i + 1) * 512],
                          start=True, stop=True),
                      waits=w, incs=[(M, 1)])
                    mi = M.n
                    E("vector",
                      lambda e, i=i: e.tensor_copy(c_b[:, i * 512:(i + 1) * 512], P[4 + i][:]),
                      waits=[(M, mi)] + ([dbg_cbw] if dbg_cbw else []), incs=[(V, 1)])
                    dbg_cbw = None
                    psfree[4 + i] = (V, V.n)
                v_cb_e = V.n
                if dbg and half == 0 and ee == 0:
                    E("sync", lambda e: e.dma_start(dbgt["cb0"], c_b[:]),
                      waits=[(V, v_cb_e)], incs=[(SDB, 16)])
                    dbg_cbw = (SDB, SDB.n)

                for fh in range(2):
                    step = ee * 2 + fh
                    ysrc, ydst = yac[step % 2], yac[(step + 1) % 2]
                    for fl in range(NFL):
                        fc = fh * NFL + fl
                        sl = fl % 2
                        w1w = [(SCG, consts_g)]
                        if half == 0 and ee == 0 and fc == 0:
                            w1w += [(V, v_endA), (A_, a_endA), (SCQ, SCQ.n)]
                        if ("w1", sl) in m_h_free:
                            w1w.append((M, m_h_free[("w1", sl)]))
                        E("gpsimd",
                          lambda e, ee=ee, fc=fc, sl=sl: e.dma_start(w1b[sl][:], w1_in[ee, fc]),
                          waits=w1w, incs=[(SW1[sl], 16)])
                        sw1_n = SW1[sl].n
                        w3w = []
                        if ("w3", sl) in m_h_free:
                            w3w.append((M, m_h_free[("w3", sl)]))
                        E("gpsimd",
                          lambda e, ee=ee, fc=fc, sl=sl: e.dma_start(w3b[sl][:], w3_in[ee, fc]),
                          waits=w3w, incs=[(SW3[sl], 16)])
                        sw3_n = SW3[sl].n
                        for tau in range(2):
                            PA, PB = P[2 * tau], P[2 * tau + 1]
                            wh = [(SW1[sl], sw1_n), (SLN, sln_n)]
                            if tau in a_silu:
                                wh.append((A_, a_silu[tau]))
                            else:
                                wh += ps_wait(2 * tau)
                            for dc in range(NDC):
                                E("tensor",
                                  lambda e, tau=tau, dc=dc, sl=sl, PA=PA: e.matmul(
                                      PA[:], w1b[sl][:, dc * 128:(dc + 1) * 128],
                                      ln2T[:, dc * 1024 + tau * 512:
                                           dc * 1024 + (tau + 1) * 512],
                                      start=(dc == 0), stop=(dc == NDC - 1)),
                                  waits=(wh if dc == 0 else []),
                                  incs=([(M, 1)] if dc == NDC - 1 else []))
                            m_h1 = M.n
                            if tau == 1:
                                m_h_free[("w1", sl)] = m_h1
                            ws = [(M, m_h1)]
                            if tau in v_hprime:
                                ws.append((V, v_hprime[tau]))
                            E("scalar",
                              lambda e, tau=tau, PA=PA: e.activation(s_tmp[tau][:], PA[:],
                                                                     AF.Silu),
                              waits=ws, incs=[(A_, 1)])
                            a_silu[tau] = A_.n
                            wh3 = [(SW3[sl], sw3_n)]
                            if tau in v_hprime:
                                wh3.append((V, v_hprime[tau]))
                            else:
                                wh3 += ps_wait(2 * tau + 1)
                            for dc in range(NDC):
                                E("tensor",
                                  lambda e, tau=tau, dc=dc, sl=sl, PB=PB: e.matmul(
                                      PB[:], w3b[sl][:, dc * 128:(dc + 1) * 128],
                                      ln2T[:, dc * 1024 + tau * 512:
                                           dc * 1024 + (tau + 1) * 512],
                                      start=(dc == 0), stop=(dc == NDC - 1)),
                                  waits=(wh3 if dc == 0 else []),
                                  incs=([(M, 1)] if dc == NDC - 1 else []))
                            m_h3 = M.n
                            if tau == 1:
                                m_h_free[("w3", sl)] = m_h3
                            E("vector",
                              lambda e, fl=fl, tau=tau, PB=PB: e.tensor_tensor(
                                  hP[:, fl * 1024 + tau * 512: fl * 1024 + (tau + 1) * 512],
                                  s_tmp[tau][:], PB[:], ALU.mult),
                              waits=[(A_, a_silu[tau]), (M, m_h3)], incs=[(V, 1)])
                            v_hprime[tau] = V.n
                    m_h_last = M.n
                    v_hp_all = V.n

                    for hc in range(NHC):
                        sl = hc % 2
                        w2w = []
                        if ("w2", sl) in m_w2_free:
                            w2w.append((M, m_w2_free[("w2", sl)]))
                        E("gpsimd",
                          lambda e, ee=ee, fh=fh, hc=hc, sl=sl: e.dma_start(
                              w2b[sl][:], w2_in[ee, fh, hc]),
                          waits=w2w, incs=[(SW2[sl], 16)])
                        sw2_n = SW2[sl].n
                        for tau in range(2):
                            PY = P[4 + tau]
                            wy = [(SW2[sl], sw2_n)]
                            if hc == 0:
                                wy.append((V, v_hp_all))
                            if tau in v_yev:
                                wy.append((V, v_yev[tau]))
                            else:
                                wy += ps_wait(4 + tau)
                            for fl in range(NFL):
                                E("tensor",
                                  lambda e, tau=tau, fl=fl, sl=sl, PY=PY: e.matmul(
                                      PY[:], w2b[sl][:, fl * 128:(fl + 1) * 128],
                                      hP[:, fl * 1024 + tau * 512:
                                         fl * 1024 + (tau + 1) * 512],
                                      start=(fl == 0), stop=(fl == NFL - 1)),
                                  waits=(wy if fl == 0 else []),
                                  incs=([(M, 1)] if fl == NFL - 1 else []))
                            m_y = M.n
                            if tau == 1:
                                m_w2_free[("w2", sl)] = m_y
                            E("vector",
                              lambda e, tau=tau, PY=PY: e.tensor_tensor(
                                  ye_t[tau][:], PY[:], c_b[:, tau * 512:(tau + 1) * 512],
                                  ALU.mult),
                              waits=[(M, m_y)], incs=[(V, 1)])
                            E("vector",
                              lambda e, hc=hc, tau=tau, ysrc=ysrc, ydst=ydst: e.tensor_tensor(
                                  ydst[:, hc * 1024 + tau * 512: hc * 1024 + (tau + 1) * 512],
                                  ysrc[:, hc * 1024 + tau * 512: hc * 1024 + (tau + 1) * 512],
                                  ye_t[tau][:], ALU.add),
                              incs=[(V, 1)])
                            v_yev[tau] = V.n
                    psfree[4] = (V, v_yev[0])
                    psfree[5] = (V, v_yev[1])
            v_yacc_done = V.n
            if dbg and half == 0:
                E("sync", lambda e: e.dma_start(dbgt["yach0"], yac[0][:]),
                  waits=[(V, v_yacc_done)], incs=[(SDB, 16)])
                E("sync", lambda e: e.dma_start(dbgt["hp"], hP[:]),
                  waits=[(V, v_yacc_done)], incs=[(SDB, 16)])
                dbg_h0w = (SDB, SDB.n)

            for tl in range(8):
                tt = half * 8 + tl
                sl = tl % 2
                if tl < 2:
                    if half == 0:
                        wx = [(V, a2c_lnb[NTT - 2 + tl])]
                    else:
                        wx = [(V, eh_out[(0, 6 + tl)])]
                else:
                    wx = [(V, eh_out[(half, tl - 2)])]
                E("sync",
                  lambda e, tt=tt, sl=sl: e.dma_start(xsl[sl][:], x1_d[tt]),
                  waits=wx, incs=[(SX[sl], 16)])
                sx_n = SX[sl].n
                wout = ([(SST, sst_last)] if sst_last else [])
                for hc in range(NHC):
                    pslot = hc % 2
                    w = [(V, v_yacc_done)] + ps_wait(("T", pslot))
                    E("tensor",
                      lambda e, tl=tl, hc=hc, pslot=pslot: e.transpose(
                          PT[pslot][:, 0:128],
                          yac[0][:, hc * 1024 + tl * 128: hc * 1024 + (tl + 1) * 128], idb[:]),
                      waits=w, incs=[(M, 1)])
                    mt = M.n
                    E("vector",
                      lambda e, hc=hc, sl=sl, pslot=pslot: e.tensor_tensor(
                          out_t[:, hc * 128:(hc + 1) * 128],
                          xsl[sl][:, hc * 128:(hc + 1) * 128],
                          PT[pslot][:, 0:128], ALU.add),
                      waits=[(M, mt), (SX[sl], sx_n)] + (wout if hc == 0 else []),
                      incs=[(V, 1)])
                    psfree[("T", pslot)] = (V, V.n)
                v_out = V.n
                eh_out[(half, tl)] = v_out
                E("sync",
                  lambda e, tt=tt: e.dma_start(out_d[tt * 128:(tt + 1) * 128, :], out_t[:]),
                  waits=[(V, v_out)], incs=[(SST, 16)])
                sst_last = SST.n
            m_endhalf = M.n

        if dbg:
            E("sync", lambda e: e.dma_start(dbgt["x1"], x1_d), waits=[(SCQ, scq_x1)],
              incs=[(SDB, 16)])
            E("sync", lambda e: e.dma_start(dbgt["l2t"], l2t_d),
              waits=[(ST2[0], scq_l2t[0]), (ST2[1], scq_l2t[1])], incs=[(SDB, 16)])
            E("sync", lambda e: e.dma_start(dbgt["yac"], yac[0][:]),
              waits=[(V, v_yacc_done)], incs=[(SDB, 16)])
        E("sync", lambda e: None, waits=[(SST, SST.n), (SDB, SDB.n)])
        sB.close()

        block = stk.enter_context(nc.Block())
        sch.build(block)

    if os.environ.get("KDBG"):
        import sys
        print("engine op counts:", sch.counts(), file=sys.stderr)
    return nc


_CACHE = {}


def _prep(W, b, gate_w, w1, w2, w3):
    W = np.asarray(W, dtype=np.float32)
    b = np.asarray(b, dtype=np.float32)
    gate_w = np.asarray(gate_w, dtype=np.float32)
    w1 = np.asarray(w1, dtype=np.float32)
    w2 = np.asarray(w2, dtype=np.float32)
    w3 = np.asarray(w3, dtype=np.float32)
    wp = np.stack([np.concatenate([W[:, i * 512:(i + 1) * 512],
                                   W[:, 2048 + i * 512: 2048 + (i + 1) * 512]], axis=1)
                   for i in range(4)])
    wp = np.ascontiguousarray(
        wp.reshape(4, 16, 128, 1024).transpose(0, 2, 1, 3).reshape(4, 128, 16 * 1024))
    bp = np.concatenate([np.concatenate([b[i * 512:(i + 1) * 512],
                                         b[2048 + i * 512: 2048 + (i + 1) * 512]])
                         for i in range(4)]).reshape(1, 4096)
    gwT = np.ascontiguousarray(
        gate_w.T.reshape(16, 128, 8).transpose(1, 0, 2).reshape(128, 128))

    def prep_w13(w):
        t = w.transpose(0, 2, 1)
        t = t.reshape(NE, 16, 128, 32, 128).transpose(0, 3, 2, 1, 4)
        return np.ascontiguousarray(t.reshape(NE, 32, 128, 2048))
    w1p = prep_w13(w1)
    w3p = prep_w13(w3)
    t = w2.transpose(0, 2, 1)
    t = t.reshape(NE, 2, 16, 128, 16, 128)
    t = t.transpose(0, 1, 4, 3, 2, 5)
    w2p = np.ascontiguousarray(t.reshape(NE, 2, NHC, 128, 2048))
    pos = np.arange(16384, dtype=np.float32)
    inv_freq = (1.0 / (10000.0 ** (np.arange(0, 1024, 2, dtype=np.float32) / 1024.0))
                ).astype(np.float32)
    freqs = (pos[:, None] * inv_freq[None, :]).astype(np.float32)
    cosf = np.cos(freqs).astype(np.float32)
    sinf = np.sin(freqs).astype(np.float32)
    return wp, bp, gwT, w1p, w2p, w3p, cosf, sinf


def kernel(L0_attn_X, L0_attn_W, L0_attn_b, L0_ffn_gate_w, L0_ffn_w1, L0_ffn_w2, L0_ffn_w3):
    X = np.ascontiguousarray(np.asarray(L0_attn_X, dtype=np.float32).reshape(16384, 2048))
    wp, bp, gwT, w1p, w2p, w3p, cosf, sinf = _prep(
        L0_attn_W, L0_attn_b, L0_ffn_gate_w, L0_ffn_w1, L0_ffn_w2, L0_ffn_w3)

    idb = np.eye(128).astype(ml_dtypes.bfloat16)
    idf = np.eye(128, dtype=np.float32)
    onc = np.ones((128, 1), dtype=np.float32)
    onr = np.ones((1, 128), dtype=np.float32)
    onb = np.ones((1, 128), dtype=ml_dtypes.bfloat16)
    oh = np.zeros((8, 8 * 128), dtype=np.float32)
    for e in range(8):
        oh[e, e * 128:(e + 1) * 128] = 1.0
    oh = oh.astype(ml_dtypes.bfloat16)

    if "nc" not in _CACHE:
        _CACHE["nc"] = build_nc()
    nc = _CACHE["nc"]

    in_maps = []
    for c in range(NCORES):
        rows = slice(c * T, (c + 1) * T)
        cs_c = np.ascontiguousarray(
            cosf[rows].reshape(NTT, 128, 2, 256).transpose(2, 0, 1, 3))
        sn_c = np.ascontiguousarray(
            sinf[rows].reshape(NTT, 128, 2, 256).transpose(2, 0, 1, 3))
        in_maps.append({
            "x0_in": np.ascontiguousarray(X[rows]), "wp_in": wp, "b_in": bp, "gw_in": gwT,
            "w1_in": w1p, "w3_in": w3p, "w2_in": w2p,
            "cs_in": cs_c, "sn_in": sn_c,
            "idb_in": idb, "idf_in": idf, "onc_in": onc, "onr_in": onr,
            "onb_in": onb, "oh_in": oh,
        })
    res = run_bass_kernel_spmd(nc, in_maps, core_ids=list(range(NCORES)))
    _CACHE["res"] = res
    out = np.concatenate([np.asarray(res.results[c]["out"]) for c in range(NCORES)], axis=0)
    return out.reshape(1, 16384, 2048).astype(np.float32)


# version 10
# speedup vs baseline: 1.0049x; 1.0022x over previous
import os
from contextlib import ExitStack

import numpy as np
import ml_dtypes
import concourse.bass as bass
import concourse.mybir as mybir
from concourse.bass_utils import run_bass_kernel_spmd

F32 = mybir.dt.float32
BF16 = mybir.dt.bfloat16
AF = mybir.ActivationFunctionType
ALU = mybir.AluOpType
AX = mybir.AxisListType

NCORES = 8
T = 2048
D = 2048
NE = 8
NTT = 16
NDC = 16
NHC = 16
NFL = 16
EPS = 1e-5

MAGIC = float(np.float32(1.5 * 2 ** 23))
INV2PI = float(np.float32(1.0 / (2.0 * np.pi)))
TWOPI = float(np.float32(2.0 * np.pi))
HALFPI = float(np.float32(np.pi / 2.0))
NEG = -1.0e30

ENGINES = ("sync", "scalar", "vector", "tensor", "gpsimd")


class Sem:
    __slots__ = ("h", "n")

    def __init__(self, h):
        self.h = h
        self.n = 0


class Sched:
    def __init__(self):
        self.ops = {e: [] for e in ENGINES}

    def emit(self, eng, fn, waits=(), incs=()):
        for s, v in waits:
            assert v <= s.n, f"deadlock: wait {v} > emitted {s.n} on {s.h.name}"
        lowered = (fn, [(s.h, v) for s, v in waits if v > 0], [])
        for s, k in incs:
            s.n += k
            lowered[2].append((s.h, k))
        self.ops[eng].append(lowered)

    def build(self, block):
        def make(name):
            ops = self.ops[name]

            def body(eng):
                for fn, waits, incs in ops:
                    for h, v in waits:
                        eng.wait_ge(h, v)
                    instr = fn(eng)
                    for h, k in incs:
                        if instr is None:
                            instr = eng.nop()
                        instr = instr.then_inc(h, k)
            return body

        for name in ENGINES:
            if self.ops[name]:
                getattr(block, name)(make(name))

    def counts(self):
        return {e: len(self.ops[e]) for e in ENGINES}


def build_nc():
    nc = bass.Bass("TRN2", target_bir_lowering=False)

    x0_in = nc.dram_tensor("x0_in", [T, D], F32, kind="ExternalInput").ap()
    wp_in = nc.dram_tensor("wp_in", [4, 128, 16 * 1024], F32, kind="ExternalInput").ap()
    b_in = nc.dram_tensor("b_in", [1, 4096], F32, kind="ExternalInput").ap()
    gw_in = nc.dram_tensor("gw_in", [128, 128], F32, kind="ExternalInput").ap()
    w1_in = nc.dram_tensor("w1_in", [NE, 32, 128, 2048], F32, kind="ExternalInput").ap()
    w3_in = nc.dram_tensor("w3_in", [NE, 32, 128, 2048], F32, kind="ExternalInput").ap()
    w2_in = nc.dram_tensor("w2_in", [NE, 2, NHC, 128, 2048], F32, kind="ExternalInput").ap()
    cs_in = nc.dram_tensor("cs_in", [2, NTT, 128, 256], F32, kind="ExternalInput").ap()
    sn_in = nc.dram_tensor("sn_in", [2, NTT, 128, 256], F32, kind="ExternalInput").ap()
    idb_in = nc.dram_tensor("idb_in", [128, 128], BF16, kind="ExternalInput").ap()
    idf_in = nc.dram_tensor("idf_in", [128, 128], F32, kind="ExternalInput").ap()
    onc_in = nc.dram_tensor("onc_in", [128, 1], F32, kind="ExternalInput").ap()
    onr_in = nc.dram_tensor("onr_in", [1, 128], F32, kind="ExternalInput").ap()
    onb_in = nc.dram_tensor("onb_in", [1, 128], BF16, kind="ExternalInput").ap()
    oh_in = nc.dram_tensor("oh_in", [8, 8 * 128], BF16, kind="ExternalInput").ap()
    out_d = nc.dram_tensor("out", [T, D], F32, kind="ExternalOutput").ap()
    x1_d = nc.dram_tensor("x1_d", [NTT, 128, D], F32, kind="Internal").ap()
    l2t_d = nc.dram_tensor("l2t_d", [NDC, NTT, 128, 128], BF16, kind="Internal").ap()
    ar_src_h = [nc.dram_tensor(f"ar_src{i}", [128, 4], F32, kind="Internal") for i in range(4)]
    ar_dst_h = [nc.dram_tensor(f"ar_dst{i}", [128, 4], F32, kind="Internal") for i in range(4)]
    ar_src_flat = [h.ap().rearrange("p g -> (p g)").unsqueeze(0) for h in ar_src_h]
    ar_dst_flat = [h.ap().rearrange("p g -> (p g)").unsqueeze(0) for h in ar_dst_h]

    dbg = bool(os.environ.get("KDBG"))
    dbgt = {}
    if dbg:
        dbgt["afin"] = nc.dram_tensor("dbg_afin", [1, 2048], F32, kind="ExternalOutput").ap()
        dbgt["stats"] = nc.dram_tensor("dbg_stats", [128, 192], F32, kind="ExternalOutput").ap()
        dbgt["scores"] = nc.dram_tensor("dbg_scores", [128, 128], F32, kind="ExternalOutput").ap()
        dbgt["coef"] = nc.dram_tensor("dbg_coef", [128, 128], F32, kind="ExternalOutput").ap()
        dbgt["coefT"] = nc.dram_tensor("dbg_coefT", [8, 2048], BF16, kind="ExternalOutput").ap()
        dbgt["ln1T"] = nc.dram_tensor("dbg_ln1T", [128, NDC * T], BF16, kind="ExternalOutput").ap()
        dbgt["x1"] = nc.dram_tensor("dbg_x1", [NTT, 128, D], F32, kind="ExternalOutput").ap()
        dbgt["l2t"] = nc.dram_tensor("dbg_l2t", [NDC, NTT, 128, 128], BF16,
                                     kind="ExternalOutput").ap()
        dbgt["yac"] = nc.dram_tensor("dbg_yac", [128, NHC * 1024], BF16,
                                     kind="ExternalOutput").ap()
        dbgt["yach0"] = nc.dram_tensor("dbg_yach0", [128, NHC * 1024], BF16,
                                       kind="ExternalOutput").ap()
        dbgt["cb0"] = nc.dram_tensor("dbg_cb0", [128, 1024], F32, kind="ExternalOutput").ap()
        dbgt["hp"] = nc.dram_tensor("dbg_hp", [128, NFL * 1024], BF16,
                                    kind="ExternalOutput").ap()

    sch = Sched()
    E = sch.emit

    with ExitStack() as stk:
        en = stk.enter_context

        def sem(name):
            return Sem(en(nc.semaphore(name)))

        SX = [sem("sx0"), sem("sx1")]
        STB = [sem("stb0"), sem("stb1")]
        SWB = [sem("swb0"), sem("swb1")]
        SW1 = [sem("sw1a"), sem("sw1b")]
        SW3 = [sem("sw3a"), sem("sw3b")]
        SW2 = [sem("sw2a"), sem("sw2b")]
        ST2 = [sem("st2a"), sem("st2b")]
        SLN = sem("sln")
        SCQ = sem("scq")
        SST = sem("sst")
        SARL = sem("sarl")
        SCS = sem("scs")
        SCG = sem("scg")
        CO = sem("co")
        M = sem("m")
        V = sem("v")
        A_ = sem("a")

        P = [en(nc.psum_tensor(f"ps{i}", [128, 512], F32)) for i in range(6)]
        PT = [en(nc.psum_tensor("pt0", [128, 1024], BF16)),
              en(nc.psum_tensor("pt1", [128, 1024], BF16))]
        psA = P[4][0:1, 0:512]
        psS = P[5][:, 0:8]
        psC = P[5][0:8, 128:256]
        psfree = {}

        def ps_wait(key):
            w = psfree.get(key)
            return [w] if w else []

        stats = en(nc.sbuf_tensor("stats", [128, 16 * 12], F32))

        def stc(q, tt=None):
            if tt is None:
                return stats[:, q * 16:(q + 1) * 16]
            return stats[:, q * 16 + tt:q * 16 + tt + 1]
        scores_sb = en(nc.sbuf_tensor("scores", [128, 128], F32))
        coefT = en(nc.sbuf_tensor("coefT", [8, 2048], BF16))
        xsl = [en(nc.sbuf_tensor("xs0", [128, D], F32)),
               en(nc.sbuf_tensor("xs1", [128, D], F32))]
        idb = en(nc.sbuf_tensor("idb", [128, 128], BF16))
        oh = en(nc.sbuf_tensor("oh", [8, 8 * 128], BF16))
        cpi2 = en(nc.sbuf_tensor("cpi2", [128, 1], F32))
        ceps = en(nc.sbuf_tensor("ceps", [128, 1], F32))

        sA = ExitStack()
        enA = sA.enter_context
        ln1T = enA(nc.sbuf_tensor("ln1T", [128, NDC * T], BF16))
        WbT = enA(nc.sbuf_tensor("Wb", [128, 2 * NDC * 1024], BF16))
        Wbs = [WbT[:, 0:NDC * 1024], WbT[:, NDC * 1024:2 * NDC * 1024]]
        ATMP = enA(nc.sbuf_tensor("ATMP", [128, 4096], F32))
        KH = ATMP[:, 0:512]
        G1 = ATMP[:, 512:1024]
        G2 = ATMP[:, 1024:1536]
        RK = ATMP[:, 1536:2048]
        RB = ATMP[:, 2048:2560]
        CKb = ATMP[:, 2560:3072]
        CVb = ATMP[:, 3072:3584]
        PSb = ATMP[:, 3584:4096]
        PSbs = [PSb, RB]
        psb_free = {}
        A_sb = enA(nc.sbuf_tensor("A_sb", [1, 2048], F32))
        A_fin = ATMP[0:1, 0:2048]
        A_bc = enA(nc.sbuf_tensor("A_bc", [128, 2048], F32))
        tabs = enA(nc.sbuf_tensor("tabs", [128, 2 * 256], F32))

        def tab(kind, slot):
            return tabs[:, kind * 256:(kind + 1) * 256]
        ln_f32 = enA(nc.sbuf_tensor("ln_f32", [128, D], F32))
        u_buf = ATMP[:, 0:2048]
        ln_bf = enA(nc.sbuf_tensor("ln_bf", [128, D], BF16))
        b_sb = enA(nc.sbuf_tensor("b_sb", [1, 4096], BF16))
        gwT = enA(nc.sbuf_tensor("gwT", [128, 128], BF16))
        onc = enA(nc.sbuf_tensor("onc", [128, 1], F32))
        onr = enA(nc.sbuf_tensor("onr", [1, 128], F32))
        onb = enA(nc.sbuf_tensor("onb", [1, 128], BF16))
        idf = enA(nc.sbuf_tensor("idf", [128, 128], F32))
        t2t = [enA(nc.sbuf_tensor("t2t0", [128, 128], BF16)),
               enA(nc.sbuf_tensor("t2t1", [128, 128], BF16))]
        gT1 = ATMP[:, 0:128]
        gE1 = ATMP[:, 128:256]
        gAB = ATMP[:, 256:384]
        gU1 = ATMP[:, 384:512]
        gS2 = ATMP[:, 512:640]
        gTO = ATMP[:, 640:768]
        gE2 = ATMP[:, 768:896]
        gCM = ATMP[:, 896:1024]
        gst = ATMP[:, 1024:1216]

        def gss(q):
            return gst[:, q * 16:(q + 1) * 16]

        E("sync", lambda e: e.dma_start(idb[:], idb_in), incs=[(SCS, 16)])
        E("sync", lambda e: e.dma_start(idf[:], idf_in), incs=[(SCS, 16)])
        E("sync", lambda e: e.dma_start(onc[:], onc_in), incs=[(SCS, 16)])
        E("sync", lambda e: e.dma_start(onr[:], onr_in), incs=[(SCS, 16)])
        E("sync", lambda e: e.dma_start(onb[:], onb_in), incs=[(SCS, 16)])
        E("sync", lambda e: e.dma_start(oh[:], oh_in), incs=[(SCS, 16)])
        consts_s = SCS.n
        E("gpsimd", lambda e: e.dma_start(gwT[:], gw_in), incs=[(SCG, 16)])
        E("gpsimd", lambda e: e.dma_start(b_sb[:], b_in), incs=[(SCG, 16)])
        consts_g = SCG.n
        E("vector", lambda e: e.memset(cpi2[:], HALFPI), incs=[(V, 1)])
        E("vector", lambda e: e.memset(ceps[:], EPS), incs=[(V, 1)])

        p2_lnb = {}
        p_rste = {}
        m_lastT = 0
        psq = {}
        for tt in range(NTT):
            sl = tt % 2
            w = []
            if tt >= 2:
                w = [(V, p2_lnb[tt - 2])]
            E("sync",
              lambda e, tt=tt, sl=sl: e.dma_start(xsl[sl][:], x0_in[tt * 128:(tt + 1) * 128, :]),
              waits=w, incs=[(SX[sl], 16)])
            lw = [(SX[sl], SX[sl].n)]
            E("vector",
              lambda e, tt=tt, sl=sl: e.reduce_sum(stc(0, tt), xsl[sl][:], AX.X),
              waits=lw, incs=[(V, 1)])
            E("scalar",
              lambda e, tt=tt, sl=sl: e.activation(ln_f32[:], xsl[sl][:], AF.Square,
                                                   accum_out=stc(1, tt)),
              waits=lw + ([(A_, psq[tt - 1])] if tt > 0 else []), incs=[(A_, 1)])
            a_sq = A_.n
            psq[tt] = a_sq
            E("vector", lambda e, tt=tt: e.tensor_scalar(stc(2, tt), stc(0, tt), 1.0 / D,
                                                         None, ALU.mult),
              waits=[(A_, a_sq), (V, V.n)], incs=[(V, 1)])
            E("vector", lambda e, tt=tt: e.tensor_scalar(stc(9, tt), stc(1, tt), 1.0 / D,
                                                         None, ALU.mult),
              waits=[(V, V.n)], incs=[(V, 1)])
            E("vector", lambda e, tt=tt: e.tensor_tensor(stc(10, tt), stc(2, tt), stc(2, tt),
                                                         ALU.mult),
              waits=[(V, V.n)], incs=[(V, 1)])
            E("vector", lambda e, tt=tt: e.tensor_tensor(stc(11, tt), stc(9, tt), stc(10, tt),
                                                         ALU.subtract),
              waits=[(V, V.n)], incs=[(V, 1)])
            v_var = V.n
            E("scalar", lambda e, tt=tt: e.activation(stc(9, tt), stc(11, tt), AF.Ln,
                                                      bias=ceps[:]),
              waits=[(V, v_var)], incs=[(A_, 1)])
            E("vector", lambda e, tt=tt: e.tensor_scalar(stc(10, tt), stc(9, tt), -0.5,
                                                         None, ALU.mult),
              waits=[(A_, A_.n), (V, V.n)], incs=[(V, 1)])
            E("scalar", lambda e, tt=tt: e.activation(stc(3, tt), stc(10, tt), AF.Exp),
              waits=[(V, V.n)], incs=[(A_, 1)])
            a_rstd1 = A_.n
            p_rste[tt] = a_rstd1
            E("vector",
              lambda e, tt=tt, sl=sl: e.tensor_scalar(ln_bf[:], xsl[sl][:], stc(2, tt),
                                                      stc(3, tt), ALU.subtract, ALU.mult),
              waits=[(A_, a_rstd1), (V, V.n), (M, m_lastT)], incs=[(V, 1)])
            v_lnb = V.n
            p2_lnb[tt] = v_lnb
            for dcg in range(NDC // 4):
                pslot = dcg % 2
                for q in range(4):
                    dc = dcg * 4 + q
                    w = [(V, v_lnb), (SCS, consts_s)] + (ps_wait(("T", pslot)) if q == 0 else [])
                    E("tensor",
                      lambda e, dc=dc, q=q, pslot=pslot: e.transpose(
                          PT[pslot][:, q * 128:(q + 1) * 128],
                          ln_bf[:, dc * 128:(dc + 1) * 128], idb[:]),
                      waits=w, incs=([(M, 1)] if q == 3 else []))
                mt = M.n
                E("vector",
                  lambda e, tt=tt, dcg=dcg, pslot=pslot: e.tensor_copy(
                      ln1T.ap().rearrange("p (d t) -> p d t", t=T)[
                          :, dcg * 4:(dcg + 1) * 4, tt * 128:(tt + 1) * 128],
                      PT[pslot][:, 0:512].rearrange("p (g c) -> p g c", c=128)),
                  waits=[(M, mt)], incs=[(V, 1)])
                psfree[("T", pslot)] = (V, V.n)
            m_lastT = M.n
        v_p2_done = V.n
        v_m1s = V.n
        a_rstd1 = A_.n

        m_wb_free = {}
        v_Acp = 0
        v_tabfree = {}
        a_sin_prev = 0
        m_A_prev = 0
        v_P_prev = 0
        pend_A = None
        arw = {}
        for np_ in range(4):
            rot = np_ < 2
            wsl = np_ % 2
            Wb = Wbs[wsl]
            E("gpsimd",
              lambda e, np_=np_, wsl=wsl: e.dma_start(Wbs[wsl][:], wp_in[np_]),
              waits=([(M, m_wb_free[wsl])] if wsl in m_wb_free else []) + [(SCG, consts_g)],
              incs=[(SWB[wsl], 16)])
            swb_n = SWB[wsl].n
            for tt in range(NTT):
                pr = tt % 2
                psK, psV = P[2 * pr], P[2 * pr + 1]
                kK, kV = 2 * pr, 2 * pr + 1
                if rot:
                    tsl = 0
                    wtb = ([v_tabfree["last"]] if "last" in v_tabfree else [])
                    E("sync",
                      lambda e, np_=np_, tt=tt: e.dma_start(tab(0, 0), cs_in[np_, tt]),
                      waits=wtb, incs=[(STB[0], 16)])
                    E("sync",
                      lambda e, np_=np_, tt=tt: e.dma_start(tab(1, 0), sn_in[np_, tt]),
                      incs=[(STB[0], 16)])
                    stb_n = STB[0].n
                wk = [(SWB[wsl], swb_n)] + ps_wait(kK)
                if np_ == 0 and tt == 0:
                    wk.append((V, v_p2_done))
                for dc in range(NDC):
                    E("tensor",
                      lambda e, tt=tt, dc=dc, psK=psK, Wb=Wb: e.matmul(
                          psK[:], ln1T[:, dc * T + tt * 128: dc * T + (tt + 1) * 128],
                          Wb[:, dc * 1024: dc * 1024 + 512], start=(dc == 0), stop=False),
                      waits=(wk if dc == 0 else []))
                E("tensor",
                  lambda e, np_=np_, psK=psK: e.matmul(
                      psK[:], onb[:], b_sb[0:1, np_ * 1024: np_ * 1024 + 512],
                      start=False, stop=True),
                  incs=[(M, 1)])
                m_k = M.n
                wv = ps_wait(kV)
                for dc in range(NDC):
                    E("tensor",
                      lambda e, tt=tt, dc=dc, psV=psV, Wb=Wb: e.matmul(
                          psV[:], ln1T[:, dc * T + tt * 128: dc * T + (tt + 1) * 128],
                          Wb[:, dc * 1024 + 512: dc * 1024 + 1024], start=(dc == 0), stop=False),
                      waits=(wv if dc == 0 else []))
                E("tensor",
                  lambda e, np_=np_, psV=psV: e.matmul(
                      psV[:], onb[:], b_sb[0:1, np_ * 1024 + 512: np_ * 1024 + 1024],
                      start=False, stop=True),
                  incs=[(M, 1)])
                m_v = M.n

                def rotary(ps, m_ps, stb_val, tslot):
                    ev, od = ps[:, 0:512:2], ps[:, 1:512:2]
                    ct, st = tab(0, tslot), tab(1, tslot)
                    E("vector", lambda e, ev=ev, ct=ct: e.tensor_tensor(
                        G1[:, 0:256], ev, ct, ALU.mult),
                      waits=[(M, m_ps), (STB[tslot], stb_val), (V, V.n)], incs=[(V, 1)])
                    E("vector", lambda e, od=od, st=st: e.tensor_tensor(
                        G2[:, 0:256], od, st, ALU.mult), waits=[(V, V.n)], incs=[(V, 1)])
                    E("vector", lambda e: e.tensor_tensor(
                        KH[:, 0:512:2], G1[:, 0:256], G2[:, 0:256], ALU.subtract),
                      waits=[(V, V.n)], incs=[(V, 1)])
                    E("vector", lambda e, od=od, ct=ct: e.tensor_tensor(
                        G1[:, 0:256], od, ct, ALU.mult), waits=[(V, V.n)], incs=[(V, 1)])
                    E("vector", lambda e, ev=ev, st=st: e.tensor_tensor(
                        G2[:, 0:256], ev, st, ALU.mult), waits=[(V, V.n)], incs=[(V, 1)])
                    E("vector", lambda e: e.tensor_tensor(
                        KH[:, 1:512:2], G1[:, 0:256], G2[:, 0:256], ALU.add),
                      waits=[(V, V.n)], incs=[(V, 1)])
                    return V.n

                def cos_chain(src_ap, out_sb, first_w, rk_w, sin_w):
                    E("vector", lambda e, src_ap=src_ap: e.tensor_scalar(
                        G1[:], src_ap, INV2PI, 0.25, ALU.mult, ALU.add),
                      waits=list(first_w) + [(V, V.n)], incs=[(V, 1)])
                    E("vector", lambda e: e.tensor_scalar(
                        G2[:], G1[:], MAGIC, MAGIC, ALU.add, ALU.subtract), incs=[(V, 1)])
                    E("vector", lambda e: e.tensor_scalar(
                        G1[:], G2[:], TWOPI, None, ALU.mult), incs=[(V, 1)])
                    E("vector", lambda e, src_ap=src_ap: e.tensor_tensor(
                        RK[:], src_ap, G1[:], ALU.subtract), waits=rk_w, incs=[(V, 1)])
                    vr = V.n
                    E("scalar", lambda e, out_sb=out_sb: e.activation(
                        out_sb[:], RK[:], AF.Sin, bias=cpi2[:]),
                      waits=[(V, vr)] + sin_w, incs=[(A_, 1)])
                    return A_.n

                rk_guard = ([(A_, a_sin_prev)] if a_sin_prev else []) + \
                           ([(M, m_A_prev)] if m_A_prev else [])
                sin_guard = [(V, v_P_prev)] if v_P_prev else []
                if rot:
                    rotary(psK, m_k, stb_n, tsl)
                    v_k_done = V.n
                    aK = cos_chain(KH[:], CKb, [], rk_guard, sin_guard)
                    rotary(psV, m_v, stb_n, tsl)
                    v_v_done = V.n
                    aV = cos_chain(KH[:], CVb, [], [(A_, aK)], sin_guard)
                    v_tabfree["last"] = (V, v_v_done)
                else:
                    aK = cos_chain(psK[:], CKb, [(M, m_k)], rk_guard, sin_guard)
                    v_k_done = V.n
                    aV = cos_chain(psV[:], CVb, [(M, m_v)], [(A_, aK)], sin_guard)
                    v_v_done = V.n
                psfree[kK] = (V, v_k_done)
                psfree[kV] = (V, v_v_done)
                psb = PSbs[tt % 2]
                E("vector", lambda e, psb=psb: e.tensor_tensor(psb[:], CKb[:], CVb[:], ALU.mult),
                  waits=[(A_, aK), (A_, aV)] +
                        ([(M, psb_free[tt % 2])] if (tt % 2) in psb_free else []),
                  incs=[(V, 1)])
                v_P = V.n
                if pend_A is not None:
                    ptt, pv, pfirst, ppsb = pend_A
                    wA = [(V, pv)]
                    if pfirst:
                        wA += [(SCS, consts_s)] + ([(V, v_Acp)] if v_Acp else [])
                    E("tensor",
                      lambda e, ppsb=ppsb, pfirst=pfirst: e.matmul(
                          psA, onc[:], ppsb[:], start=pfirst, stop=False),
                      waits=wA, incs=[(M, 1)])
                    psb_free[ptt % 2] = M.n
                pend_A = (tt, v_P, (tt == 0), psb)
                a_sin_prev = aV
                v_P_prev = v_P
            ptt, pv, pfirst, ppsb = pend_A
            E("tensor",
              lambda e, ppsb=ppsb, pfirst=pfirst: e.matmul(
                  psA, onc[:], ppsb[:], start=pfirst, stop=True),
              waits=[(V, pv)] + ([(SCS, consts_s)] if pfirst else []), incs=[(M, 1)])
            psb_free[ptt % 2] = M.n
            pend_A = None
            m_A_prev = M.n
            m_endKV = M.n
            E("vector",
              lambda e, np_=np_: e.tensor_copy(A_sb[0:1, np_ * 512:(np_ + 1) * 512], psA),
              waits=[(M, m_A_prev)], incs=[(V, 1)])
            v_Acp = V.n
            m_wb_free[wsl] = M.n
            E("scalar",
              lambda e, np_=np_: e.dma_start(ar_src_flat[np_],
                                             A_sb[0:1, np_ * 512:(np_ + 1) * 512]),
              waits=[(V, v_Acp)], incs=[(SCQ, 16)])
            arw[np_] = SCQ.n
        psfree[4] = (V, v_Acp)
        for np_ in range(4):
            E("gpsimd", lambda e, np_=np_: e.collective_compute(
                "AllReduce", ALU.add, replica_groups=[list(range(NCORES))],
                ins=[ar_src_h[np_].ap().opt()], outs=[ar_dst_h[np_].ap().opt()]),
              waits=[(SCQ, arw[np_])], incs=[(CO, 1)])

        pre_sx = {}
        for tt in range(2):
            E("sync",
              lambda e, tt=tt: e.dma_start(xsl[tt][:], x0_in[tt * 128:(tt + 1) * 128, :]),
              waits=[(V, p2_lnb[NTT - 2 + tt])], incs=[(SX[tt], 16)])
            pre_sx[tt] = SX[tt].n
        for np_ in range(4):
            E("sync",
              lambda e, np_=np_: e.dma_start(A_sb[0:1, np_ * 512:(np_ + 1) * 512],
                                             ar_dst_flat[np_]),
              waits=[(CO, 4)], incs=[(SARL, 16)])
        E("vector", lambda e: e.tensor_scalar(A_fin[:], A_sb[:], 0.5, 0.5, ALU.mult, ALU.add),
          waits=[(SARL, SARL.n)], incs=[(V, 1)])
        v_afin = V.n
        for i in range(4):
            E("tensor",
              lambda e, i=i: e.matmul(P[i][:], onr[:], A_fin[0:1, i * 512:(i + 1) * 512],
                                      start=True, stop=True),
              waits=[(V, v_afin)] + ps_wait(i), incs=[(M, 1)])
            mi = M.n
            E("vector",
              lambda e, i=i: e.tensor_copy(A_bc[:, i * 512:(i + 1) * 512], P[i][:]),
              waits=[(M, mi)], incs=[(V, 1)])
            psfree[i] = (V, V.n)
        v_abc = V.n

        a2 = {}
        x1w = {}
        a2sq = {}
        a2c_lnb = {}
        m_lastT2 = m_lastT
        for tt in range(NTT):
            sl = tt % 2
            if tt < 2:
                lw = (SX[sl], pre_sx[tt])
            else:
                E("sync",
                  lambda e, tt=tt, sl=sl: e.dma_start(xsl[sl][:],
                                                      x0_in[tt * 128:(tt + 1) * 128, :]),
                  waits=[(V, a2[tt - 2])], incs=[(SX[sl], 16)])
                lw = (SX[sl], SX[sl].n)
            lnw = [lw, (V, v_abc), (V, v_m1s), (A_, a_rstd1)]
            if tt > 0:
                lnw += [(A_, a2sq[tt - 1]), (SCQ, x1w[tt - 1])]
            E("vector",
              lambda e, tt=tt, sl=sl: e.tensor_scalar(ln_f32[:], xsl[sl][:], stc(2, tt),
                                                      stc(3, tt), ALU.subtract, ALU.mult),
              waits=lnw, incs=[(V, 1)])
            E("vector", lambda e: e.tensor_tensor(u_buf[:], A_bc[:], ln_f32[:], ALU.mult),
              incs=[(V, 1)])
            E("vector", lambda e, sl=sl: e.tensor_tensor(ln_f32[:], xsl[sl][:], u_buf[:],
                                                         ALU.add), incs=[(V, 1)])
            v_x1 = V.n
            a2[tt] = v_x1
            E("scalar", lambda e, tt=tt: e.dma_start(x1_d[tt], ln_f32[:]),
              waits=[(V, v_x1)], incs=[(SCQ, 16)])
            x1w[tt] = SCQ.n
            E("vector", lambda e, tt=tt: e.reduce_sum(stc(5, tt), ln_f32[:], AX.X),
              incs=[(V, 1)])
            E("scalar", lambda e, tt=tt: e.activation(u_buf[:], ln_f32[:], AF.Square,
                                                      accum_out=stc(6, tt)),
              waits=[(V, v_x1)], incs=[(A_, 1)])
            a_sq = A_.n
            a2sq[tt] = a_sq
            E("vector", lambda e, tt=tt: e.tensor_scalar(stc(7, tt), stc(5, tt), 1.0 / D,
                                                         None, ALU.mult),
              waits=[(A_, a_sq), (V, V.n)], incs=[(V, 1)])
            E("vector", lambda e, tt=tt: e.tensor_scalar(stc(9, tt), stc(6, tt), 1.0 / D,
                                                         None, ALU.mult),
              waits=[(V, V.n)], incs=[(V, 1)])
            E("vector", lambda e, tt=tt: e.tensor_tensor(stc(10, tt), stc(7, tt), stc(7, tt),
                                                         ALU.mult),
              waits=[(V, V.n)], incs=[(V, 1)])
            E("vector", lambda e, tt=tt: e.tensor_tensor(stc(11, tt), stc(9, tt), stc(10, tt),
                                                         ALU.subtract),
              waits=[(V, V.n)], incs=[(V, 1)])
            v_var = V.n
            E("scalar", lambda e, tt=tt: e.activation(stc(9, tt), stc(11, tt), AF.Ln,
                                                      bias=ceps[:]),
              waits=[(V, v_var)] + ([(A_, a2rste[tt - 1])] if tt > 0 else []),
              incs=[(A_, 1)])
            E("vector", lambda e, tt=tt: e.tensor_scalar(stc(10, tt), stc(9, tt), -0.5,
                                                         None, ALU.mult),
              waits=[(A_, A_.n), (V, V.n)], incs=[(V, 1)])
            E("scalar", lambda e, tt=tt: e.activation(stc(8, tt), stc(10, tt), AF.Exp),
              waits=[(V, V.n)], incs=[(A_, 1)])
            a_rstd2 = A_.n
            if tt == 0:
                a2rste = {}
            a2rste[tt] = a_rstd2
            E("vector",
              lambda e, tt=tt: e.tensor_scalar(ln_bf[:], ln_f32[:], stc(7, tt),
                                               stc(8, tt), ALU.subtract, ALU.mult),
              waits=[(A_, a_rstd2), (V, V.n), (M, m_lastT2)], incs=[(V, 1)])
            v_lnb = V.n
            a2c_lnb[tt] = v_lnb
            for dc in range(NDC):
                pslot = dc % 2
                w = [(V, v_lnb)] + ps_wait(("T", pslot))
                E("tensor",
                  lambda e, dc=dc, pslot=pslot: e.transpose(
                      PT[pslot][:, 0:128], ln_bf[:, dc * 128:(dc + 1) * 128], idb[:]),
                  waits=w, incs=[(M, 1)])
                mt = M.n
                wv = [(M, mt)]
                if ST2[pslot].n > 0:
                    wv.append((ST2[pslot], ST2[pslot].n))
                E("vector",
                  lambda e, pslot=pslot: e.tensor_copy(t2t[pslot][:], PT[pslot][:, 0:128]),
                  waits=wv, incs=[(V, 1)])
                psfree[("T", pslot)] = (V, V.n)
                vev = V.n
                E("scalar",
                  lambda e, dc=dc, tt=tt, pslot=pslot: e.dma_start(l2t_d[dc, tt],
                                                                   t2t[pslot][:]),
                  waits=[(V, vev)], incs=[(ST2[pslot], 16)])
                wmm = [(V, vev)]
                if dc == 0:
                    wmm += [(SCG, consts_g)] + ps_wait(5)
                E("tensor",
                  lambda e, dc=dc, pslot=pslot: e.matmul(
                      psS, t2t[pslot][:], gwT[:, dc * 8:(dc + 1) * 8],
                      start=(dc == 0), stop=(dc == NDC - 1)),
                  waits=wmm, incs=([(M, 1)] if dc == NDC - 1 else []))
            m_lastT2 = M.n
            E("vector",
              lambda e, tt=tt: e.tensor_copy(scores_sb[:, tt * 8:(tt + 1) * 8], psS),
              waits=[(M, m_lastT2)], incs=[(V, 1)])
            psfree[5] = (V, V.n)
            if tt == 7:
                scq_l2t_half0 = [ST2[0].n, ST2[1].n]
        scq_x1 = SCQ.n
        scq_l2t = [ST2[0].n, ST2[1].n]
        scq_l2t_h0 = list(scq_l2t_half0)

        def g3(t):
            a = t if isinstance(t, bass.AP) else t.ap()
            return a.rearrange("p (g k) -> p g k", k=8)

        def bc16(t16):
            return t16.unsqueeze(2).broadcast_to([128, 16, 8])

        v_scores = V.n

        def gv(fn, waits=()):
            E("vector", fn, waits=[(V, V.n)] + list(waits), incs=[(V, 1)])
            return V.n

        def ga(fn, wv):
            E("scalar", fn, waits=[(V, wv)], incs=[(A_, 1)])
            return A_.n

        def gw(fn, wa):
            E("vector", fn, waits=[(A_, wa), (V, V.n)], incs=[(V, 1)])
            return V.n

        BIG = 1.0e30
        v_nv1 = gv(lambda e: e.tensor_reduce(gss(0), g3(scores_sb), AX.X, ALU.max, negate=True))
        gv(lambda e: e.tensor_tensor(g3(gT1), g3(scores_sb), bc16(gss(0)), ALU.add),
           waits=[(V, v_nv1)])
        gv(lambda e: e.tensor_scalar(gS2[:], gT1[:], -BIG, None, ALU.mult))
        gv(lambda e: e.tensor_scalar(gCM[:], gS2[:], 1.0, None, ALU.min))
        gv(lambda e: e.tensor_scalar(gE1[:], gCM[:], -1.0, 1.0, ALU.mult, ALU.add))
        ax = ga(lambda e: e.activation(gAB[:], scores_sb[:], AF.Abs), v_scores)
        v_nm1 = gw(lambda e: e.tensor_scalar(gss(1), gss(0), -1.0, None, ALU.mult), ax)
        gv(lambda e: e.tensor_tensor(g3(gU1), g3(gAB), bc16(gss(1)), ALU.max),
           waits=[(V, v_nm1)])
        gv(lambda e: e.tensor_scalar(gCM[:], gU1[:], 0.02, None, ALU.mult))
        gv(lambda e: e.tensor_tensor(gS2[:], gT1[:], gCM[:], ALU.add))
        gv(lambda e: e.tensor_scalar(gCM[:], gS2[:], -BIG, None, ALU.mult))
        gv(lambda e: e.tensor_scalar(gU1[:], gCM[:], 1.0, None, ALU.min))
        gv(lambda e: e.tensor_scalar(gCM[:], gU1[:], 0.0, None, ALU.max))
        gv(lambda e: e.tensor_scalar(gS2[:], gCM[:], NEG, None, ALU.mult))
        v_sm1 = gv(lambda e: e.tensor_tensor(gU1[:], gT1[:], gS2[:], ALU.add))
        a_e1 = ga(lambda e: e.activation(gTO[:], gU1[:], AF.Exp), v_sm1)
        v_d1 = gw(lambda e: e.tensor_reduce(gss(2), g3(gTO), AX.X, ALU.add), a_e1)
        a_l1 = ga(lambda e: e.activation(gss(3), gss(2), AF.Ln), v_d1)
        v_l1 = gw(lambda e: e.tensor_scalar(gss(9), gss(3), -1.0, None, ALU.mult), a_l1)
        a_r1 = ga(lambda e: e.activation(gss(3), gss(9), AF.Exp), v_l1)
        gv(lambda e: e.tensor_tensor(gU1[:], gTO[:], gE1[:], ALU.mult))
        gv(lambda e: e.tensor_reduce(gss(4), g3(gU1), AX.X, ALU.add))
        v_m1v = gw(lambda e: e.tensor_tensor(gss(5), gss(4), gss(3), ALU.mult), a_r1)
        gv(lambda e: e.tensor_scalar(gS2[:], gE1[:], NEG, None, ALU.mult))
        gv(lambda e: e.tensor_tensor(gU1[:], scores_sb[:], gS2[:], ALU.add))
        v_nv2 = gv(lambda e: e.tensor_reduce(gss(6), g3(gU1), AX.X, ALU.max, negate=True))
        gv(lambda e: e.tensor_tensor(g3(gTO), g3(scores_sb), bc16(gss(6)), ALU.add),
           waits=[(V, v_nv2)])
        gv(lambda e: e.tensor_tensor(g3(gS2), g3(gU1), bc16(gss(6)), ALU.add))
        gv(lambda e: e.tensor_scalar(gCM[:], gS2[:], -BIG, None, ALU.mult))
        gv(lambda e: e.tensor_scalar(gU1[:], gCM[:], 1.0, None, ALU.min))
        gv(lambda e: e.tensor_scalar(gE2[:], gU1[:], -1.0, 1.0, ALU.mult, ALU.add))
        v_nm2 = gv(lambda e: e.tensor_scalar(gss(7), gss(6), -1.0, None, ALU.mult))
        gv(lambda e: e.tensor_tensor(g3(gS2), g3(gAB), bc16(gss(7)), ALU.max),
           waits=[(V, v_nm2)])
        gv(lambda e: e.tensor_scalar(gU1[:], gS2[:], 0.02, None, ALU.mult))
        gv(lambda e: e.tensor_tensor(gS2[:], gTO[:], gU1[:], ALU.add))
        gv(lambda e: e.tensor_scalar(gCM[:], gS2[:], -BIG, None, ALU.mult))
        gv(lambda e: e.tensor_scalar(gU1[:], gCM[:], 1.0, None, ALU.min))
        gv(lambda e: e.tensor_scalar(gCM[:], gU1[:], 0.0, None, ALU.max))
        gv(lambda e: e.tensor_scalar(gU1[:], gCM[:], NEG, None, ALU.mult))
        gv(lambda e: e.tensor_scalar(gCM[:], gE1[:], NEG, None, ALU.mult))
        gv(lambda e: e.tensor_tensor(gS2[:], gU1[:], gCM[:], ALU.add))
        v_sm2 = gv(lambda e: e.tensor_tensor(gU1[:], gTO[:], gS2[:], ALU.add))
        a_e2 = ga(lambda e: e.activation(gCM[:], gU1[:], AF.Exp), v_sm2)
        v_d2 = gw(lambda e: e.tensor_reduce(gss(8), g3(gCM), AX.X, ALU.add), a_e2)
        a_l2 = ga(lambda e: e.activation(gss(9), gss(8), AF.Ln), v_d2)
        v_l2 = gw(lambda e: e.tensor_scalar(gss(10), gss(9), -1.0, None, ALU.mult), a_l2)
        a_r2 = ga(lambda e: e.activation(gss(9), gss(10), AF.Exp), v_l2)
        gv(lambda e: e.tensor_tensor(gU1[:], gCM[:], gE2[:], ALU.mult))
        gv(lambda e: e.tensor_reduce(gss(10), g3(gU1), AX.X, ALU.add))
        v_m2v = gw(lambda e: e.tensor_tensor(gss(11), gss(10), gss(9), ALU.mult), a_r2)
        gv(lambda e: e.tensor_tensor(g3(gT1), g3(gE1), bc16(gss(5)), ALU.mult),
           waits=[(V, v_m1v)])
        gv(lambda e: e.tensor_tensor(g3(gS2), g3(gE2), bc16(gss(11)), ALU.mult),
           waits=[(V, v_m2v)])
        v_coef = gv(lambda e: e.tensor_tensor(gU1[:], gT1[:], gS2[:], ALU.add))
        for tt in range(NTT):
            w = [(V, v_coef), (SCS, consts_s)] + ps_wait(5)
            E("tensor",
              lambda e, tt=tt: e.transpose(psC, gU1[:, tt * 8:(tt + 1) * 8], idf[:]),
              waits=w, incs=[(M, 1)])
            mt = M.n
            E("vector",
              lambda e, tt=tt: e.tensor_copy(coefT[0:8, tt * 128:(tt + 1) * 128], psC),
              waits=[(M, mt)], incs=[(V, 1)])
            psfree[5] = (V, V.n)
        v_coefT = V.n
        m_endA = M.n
        v_endA = V.n
        a_endA = A_.n

        SDB = sem("sdb")
        if dbg:
            E("sync", lambda e: e.dma_start(dbgt["afin"], A_fin[:]),
              waits=[(V, v_afin)], incs=[(SDB, 16)])
            E("sync", lambda e: e.dma_start(dbgt["stats"], stats[:]),
              waits=[(A_, a_rstd2)], incs=[(SDB, 16)])
            E("sync", lambda e: e.dma_start(dbgt["scores"], scores_sb[:]),
              waits=[(V, v_scores)], incs=[(SDB, 16)])
            E("sync", lambda e: e.dma_start(dbgt["coef"], gU1[:]),
              waits=[(V, v_coef)], incs=[(SDB, 16)])
            E("sync", lambda e: e.dma_start(dbgt["coefT"], coefT[:]),
              waits=[(V, v_coefT)], incs=[(SDB, 16)])
            E("sync", lambda e: e.dma_start(dbgt["ln1T"], ln1T[:]),
              waits=[(V, v_p2_done)], incs=[(SDB, 16)])

        sA.close()

        sB = ExitStack()
        enB = sB.enter_context
        hP = enB(nc.sbuf_tensor("hP", [128, NFL * 1024], BF16))
        ln2T = enB(nc.sbuf_tensor("ln2T", [128, NDC * 1024], BF16))
        yac = [enB(nc.sbuf_tensor("yac0", [128, NHC * 1024], BF16)),
               enB(nc.sbuf_tensor("yac1", [128, NHC * 1024], BF16))]
        w1b = [enB(nc.sbuf_tensor("w1b0", [128, NDC * 128], BF16)),
               enB(nc.sbuf_tensor("w1b1", [128, NDC * 128], BF16))]
        w3b = [enB(nc.sbuf_tensor("w3b0", [128, NDC * 128], BF16)),
               enB(nc.sbuf_tensor("w3b1", [128, NDC * 128], BF16))]
        w2b = [enB(nc.sbuf_tensor("w2b0", [128, NFL * 128], BF16)),
               enB(nc.sbuf_tensor("w2b1", [128, NFL * 128], BF16))]
        c_b = enB(nc.sbuf_tensor("c_b", [128, 1024], F32))
        s_tmp = [enB(nc.sbuf_tensor("stmp0", [128, 512], BF16)),
                 enB(nc.sbuf_tensor("stmp1", [128, 512], BF16))]
        ye_t = [enB(nc.sbuf_tensor("ye0", [128, 512], BF16)),
                enB(nc.sbuf_tensor("ye1", [128, 512], BF16))]
        out_t = enB(nc.sbuf_tensor("out_t", [128, D], F32))

        m_h_free = {}
        m_w2_free = {}
        a_silu = {}
        v_hprime = {}
        v_yev = {}
        m_h_last = 0
        m_endhalf = 0
        sst_last = 0
        eh_out = {}
        dbg_cbw = None
        dbg_h0w = None

        for half in range(2):
            if half == 0:
                wl = [(M, m_endKV), (ST2[0], scq_l2t_h0[0]), (ST2[1], scq_l2t_h0[1])]
            else:
                wl = [(M, m_h_last), (ST2[0], scq_l2t[0]), (ST2[1], scq_l2t[1])]
            wl += ([(SDB, SDB.n)] if (dbg and half == 0) else []) + \
                  ([dbg_h0w] if (dbg and half == 1 and dbg_h0w) else [])
            for dc in range(NDC):
                E("sync",
                  lambda e, dc=dc, half=half: e.dma_start(
                      ln2T.ap()[:, dc * 1024:(dc + 1) * 1024].rearrange(
                          "p (tl t) -> p tl t", t=128),
                      l2t_d[dc, half * 8:(half + 1) * 8].transpose([1, 0, 2])),
                  waits=(wl if dc == 0 else []), incs=[(SLN, 16)])
            sln_n = SLN.n
            E("vector", lambda e: e.memset(yac[0][:], 0.0),
              waits=(([(V, v_endA)] if half == 0 else [(M, m_endhalf)]) +
                     ([dbg_h0w] if (dbg and half == 1 and dbg_h0w) else [])),
              incs=[(V, 1)])

            for ee in range(NE):
                for i in range(2):
                    w = [(V, v_coefT)] + ps_wait(4 + i)
                    E("tensor",
                      lambda e, ee=ee, half=half, i=i: e.matmul(
                          P[4 + i][:], oh[0:8, ee * 128:(ee + 1) * 128],
                          coefT[0:8, half * 1024 + i * 512: half * 1024 + (i + 1) * 512],
                          start=True, stop=True),
                      waits=w, incs=[(M, 1)])
                    mi = M.n
                    E("vector",
                      lambda e, i=i: e.tensor_copy(c_b[:, i * 512:(i + 1) * 512], P[4 + i][:]),
                      waits=[(M, mi)] + ([dbg_cbw] if dbg_cbw else []), incs=[(V, 1)])
                    dbg_cbw = None
                    psfree[4 + i] = (V, V.n)
                v_cb_e = V.n
                if dbg and half == 0 and ee == 0:
                    E("sync", lambda e: e.dma_start(dbgt["cb0"], c_b[:]),
                      waits=[(V, v_cb_e)], incs=[(SDB, 16)])
                    dbg_cbw = (SDB, SDB.n)

                for fh in range(2):
                    step = ee * 2 + fh
                    ysrc, ydst = yac[step % 2], yac[(step + 1) % 2]
                    for fl in range(NFL):
                        fc = fh * NFL + fl
                        sl = fl % 2
                        w1w = [(SCG, consts_g)]
                        if half == 0 and ee == 0 and fc == 0:
                            w1w += [(V, v_endA), (A_, a_endA), (SCQ, SCQ.n)]
                        if ("w1", sl) in m_h_free:
                            w1w.append((M, m_h_free[("w1", sl)]))
                        E("gpsimd",
                          lambda e, ee=ee, fc=fc, sl=sl: e.dma_start(w1b[sl][:], w1_in[ee, fc]),
                          waits=w1w, incs=[(SW1[sl], 16)])
                        sw1_n = SW1[sl].n
                        w3w = []
                        if ("w3", sl) in m_h_free:
                            w3w.append((M, m_h_free[("w3", sl)]))
                        E("gpsimd",
                          lambda e, ee=ee, fc=fc, sl=sl: e.dma_start(w3b[sl][:], w3_in[ee, fc]),
                          waits=w3w, incs=[(SW3[sl], 16)])
                        sw3_n = SW3[sl].n
                        for tau in range(2):
                            PA, PB = P[2 * tau], P[2 * tau + 1]
                            wh = [(SW1[sl], sw1_n), (SLN, sln_n)]
                            if tau in a_silu:
                                wh.append((A_, a_silu[tau]))
                            else:
                                wh += ps_wait(2 * tau)
                            for dc in range(NDC):
                                E("tensor",
                                  lambda e, tau=tau, dc=dc, sl=sl, PA=PA: e.matmul(
                                      PA[:], w1b[sl][:, dc * 128:(dc + 1) * 128],
                                      ln2T[:, dc * 1024 + tau * 512:
                                           dc * 1024 + (tau + 1) * 512],
                                      start=(dc == 0), stop=(dc == NDC - 1)),
                                  waits=(wh if dc == 0 else []),
                                  incs=([(M, 1)] if dc == NDC - 1 else []))
                            m_h1 = M.n
                            if tau == 1:
                                m_h_free[("w1", sl)] = m_h1
                            ws = [(M, m_h1)]
                            if tau in v_hprime:
                                ws.append((V, v_hprime[tau]))
                            E("scalar",
                              lambda e, tau=tau, PA=PA: e.activation(s_tmp[tau][:], PA[:],
                                                                     AF.Silu),
                              waits=ws, incs=[(A_, 1)])
                            a_silu[tau] = A_.n
                            wh3 = [(SW3[sl], sw3_n)]
                            if tau in v_hprime:
                                wh3.append((V, v_hprime[tau]))
                            else:
                                wh3 += ps_wait(2 * tau + 1)
                            for dc in range(NDC):
                                E("tensor",
                                  lambda e, tau=tau, dc=dc, sl=sl, PB=PB: e.matmul(
                                      PB[:], w3b[sl][:, dc * 128:(dc + 1) * 128],
                                      ln2T[:, dc * 1024 + tau * 512:
                                           dc * 1024 + (tau + 1) * 512],
                                      start=(dc == 0), stop=(dc == NDC - 1)),
                                  waits=(wh3 if dc == 0 else []),
                                  incs=([(M, 1)] if dc == NDC - 1 else []))
                            m_h3 = M.n
                            if tau == 1:
                                m_h_free[("w3", sl)] = m_h3
                            E("vector",
                              lambda e, fl=fl, tau=tau, PB=PB: e.tensor_tensor(
                                  hP[:, fl * 1024 + tau * 512: fl * 1024 + (tau + 1) * 512],
                                  s_tmp[tau][:], PB[:], ALU.mult),
                              waits=[(A_, a_silu[tau]), (M, m_h3)], incs=[(V, 1)])
                            v_hprime[tau] = V.n
                    m_h_last = M.n
                    v_hp_all = V.n

                    for hc in range(NHC):
                        sl = hc % 2
                        w2w = []
                        if ("w2", sl) in m_w2_free:
                            w2w.append((M, m_w2_free[("w2", sl)]))
                        E("gpsimd",
                          lambda e, ee=ee, fh=fh, hc=hc, sl=sl: e.dma_start(
                              w2b[sl][:], w2_in[ee, fh, hc]),
                          waits=w2w, incs=[(SW2[sl], 16)])
                        sw2_n = SW2[sl].n
                        for tau in range(2):
                            PY = P[4 + tau]
                            wy = [(SW2[sl], sw2_n)]
                            if hc == 0:
                                wy.append((V, v_hp_all))
                            if tau in v_yev:
                                wy.append((V, v_yev[tau]))
                            else:
                                wy += ps_wait(4 + tau)
                            for fl in range(NFL):
                                E("tensor",
                                  lambda e, tau=tau, fl=fl, sl=sl, PY=PY: e.matmul(
                                      PY[:], w2b[sl][:, fl * 128:(fl + 1) * 128],
                                      hP[:, fl * 1024 + tau * 512:
                                         fl * 1024 + (tau + 1) * 512],
                                      start=(fl == 0), stop=(fl == NFL - 1)),
                                  waits=(wy if fl == 0 else []),
                                  incs=([(M, 1)] if fl == NFL - 1 else []))
                            m_y = M.n
                            if tau == 1:
                                m_w2_free[("w2", sl)] = m_y
                            E("vector",
                              lambda e, tau=tau, PY=PY: e.tensor_tensor(
                                  ye_t[tau][:], PY[:], c_b[:, tau * 512:(tau + 1) * 512],
                                  ALU.mult),
                              waits=[(M, m_y)], incs=[(V, 1)])
                            E("vector",
                              lambda e, hc=hc, tau=tau, ysrc=ysrc, ydst=ydst: e.tensor_tensor(
                                  ydst[:, hc * 1024 + tau * 512: hc * 1024 + (tau + 1) * 512],
                                  ysrc[:, hc * 1024 + tau * 512: hc * 1024 + (tau + 1) * 512],
                                  ye_t[tau][:], ALU.add),
                              incs=[(V, 1)])
                            v_yev[tau] = V.n
                    psfree[4] = (V, v_yev[0])
                    psfree[5] = (V, v_yev[1])
            v_yacc_done = V.n
            if dbg and half == 0:
                E("sync", lambda e: e.dma_start(dbgt["yach0"], yac[0][:]),
                  waits=[(V, v_yacc_done)], incs=[(SDB, 16)])
                E("sync", lambda e: e.dma_start(dbgt["hp"], hP[:]),
                  waits=[(V, v_yacc_done)], incs=[(SDB, 16)])
                dbg_h0w = (SDB, SDB.n)

            for tl in range(8):
                tt = half * 8 + tl
                sl = tl % 2
                if tl < 2:
                    if half == 0:
                        wx = [(V, a2c_lnb[NTT - 2 + tl])]
                    else:
                        wx = [(V, eh_out[(0, 6 + tl)])]
                else:
                    wx = [(V, eh_out[(half, tl - 2)])]
                E("sync",
                  lambda e, tt=tt, sl=sl: e.dma_start(xsl[sl][:], x1_d[tt]),
                  waits=wx, incs=[(SX[sl], 16)])
                sx_n = SX[sl].n
                wout = ([(SST, sst_last)] if sst_last else [])
                for hg in range(NHC // 4):
                    pslot = hg % 2
                    for q in range(4):
                        hc = hg * 4 + q
                        w = [(V, v_yacc_done)] + (ps_wait(("T", pslot)) if q == 0 else [])
                        E("tensor",
                          lambda e, tl=tl, hc=hc, q=q, pslot=pslot: e.transpose(
                              PT[pslot][:, q * 128:(q + 1) * 128],
                              yac[0][:, hc * 1024 + tl * 128: hc * 1024 + (tl + 1) * 128],
                              idb[:]),
                          waits=w, incs=([(M, 1)] if q == 3 else []))
                    mt = M.n
                    E("vector",
                      lambda e, hg=hg, sl=sl, pslot=pslot: e.tensor_tensor(
                          out_t[:, hg * 512:(hg + 1) * 512],
                          xsl[sl][:, hg * 512:(hg + 1) * 512],
                          PT[pslot][:, 0:512], ALU.add),
                      waits=[(M, mt), (SX[sl], sx_n)] + (wout if hg == 0 else []),
                      incs=[(V, 1)])
                    psfree[("T", pslot)] = (V, V.n)
                v_out = V.n
                eh_out[(half, tl)] = v_out
                E("sync",
                  lambda e, tt=tt: e.dma_start(out_d[tt * 128:(tt + 1) * 128, :], out_t[:]),
                  waits=[(V, v_out)], incs=[(SST, 16)])
                sst_last = SST.n
            m_endhalf = M.n

        if dbg:
            E("sync", lambda e: e.dma_start(dbgt["x1"], x1_d), waits=[(SCQ, scq_x1)],
              incs=[(SDB, 16)])
            E("sync", lambda e: e.dma_start(dbgt["l2t"], l2t_d),
              waits=[(ST2[0], scq_l2t[0]), (ST2[1], scq_l2t[1])], incs=[(SDB, 16)])
            E("sync", lambda e: e.dma_start(dbgt["yac"], yac[0][:]),
              waits=[(V, v_yacc_done)], incs=[(SDB, 16)])
        E("sync", lambda e: None, waits=[(SST, SST.n), (SDB, SDB.n)])
        sB.close()

        block = stk.enter_context(nc.Block())
        sch.build(block)

    if os.environ.get("KDBG"):
        import sys
        print("engine op counts:", sch.counts(), file=sys.stderr)
    return nc


_CACHE = {}


def _prep(W, b, gate_w, w1, w2, w3):
    W = np.asarray(W, dtype=np.float32)
    b = np.asarray(b, dtype=np.float32)
    gate_w = np.asarray(gate_w, dtype=np.float32)
    w1 = np.asarray(w1, dtype=np.float32)
    w2 = np.asarray(w2, dtype=np.float32)
    w3 = np.asarray(w3, dtype=np.float32)
    wp = np.stack([np.concatenate([W[:, i * 512:(i + 1) * 512],
                                   W[:, 2048 + i * 512: 2048 + (i + 1) * 512]], axis=1)
                   for i in range(4)])
    wp = np.ascontiguousarray(
        wp.reshape(4, 16, 128, 1024).transpose(0, 2, 1, 3).reshape(4, 128, 16 * 1024))
    bp = np.concatenate([np.concatenate([b[i * 512:(i + 1) * 512],
                                         b[2048 + i * 512: 2048 + (i + 1) * 512]])
                         for i in range(4)]).reshape(1, 4096)
    gwT = np.ascontiguousarray(
        gate_w.T.reshape(16, 128, 8).transpose(1, 0, 2).reshape(128, 128))

    def prep_w13(w):
        t = w.transpose(0, 2, 1)
        t = t.reshape(NE, 16, 128, 32, 128).transpose(0, 3, 2, 1, 4)
        return np.ascontiguousarray(t.reshape(NE, 32, 128, 2048))
    w1p = prep_w13(w1)
    w3p = prep_w13(w3)
    t = w2.transpose(0, 2, 1)
    t = t.reshape(NE, 2, 16, 128, 16, 128)
    t = t.transpose(0, 1, 4, 3, 2, 5)
    w2p = np.ascontiguousarray(t.reshape(NE, 2, NHC, 128, 2048))
    pos = np.arange(16384, dtype=np.float32)
    inv_freq = (1.0 / (10000.0 ** (np.arange(0, 1024, 2, dtype=np.float32) / 1024.0))
                ).astype(np.float32)
    freqs = (pos[:, None] * inv_freq[None, :]).astype(np.float32)
    cosf = np.cos(freqs).astype(np.float32)
    sinf = np.sin(freqs).astype(np.float32)
    return wp, bp, gwT, w1p, w2p, w3p, cosf, sinf


def kernel(L0_attn_X, L0_attn_W, L0_attn_b, L0_ffn_gate_w, L0_ffn_w1, L0_ffn_w2, L0_ffn_w3):
    X = np.ascontiguousarray(np.asarray(L0_attn_X, dtype=np.float32).reshape(16384, 2048))
    wp, bp, gwT, w1p, w2p, w3p, cosf, sinf = _prep(
        L0_attn_W, L0_attn_b, L0_ffn_gate_w, L0_ffn_w1, L0_ffn_w2, L0_ffn_w3)

    idb = np.eye(128).astype(ml_dtypes.bfloat16)
    idf = np.eye(128, dtype=np.float32)
    onc = np.ones((128, 1), dtype=np.float32)
    onr = np.ones((1, 128), dtype=np.float32)
    onb = np.ones((1, 128), dtype=ml_dtypes.bfloat16)
    oh = np.zeros((8, 8 * 128), dtype=np.float32)
    for e in range(8):
        oh[e, e * 128:(e + 1) * 128] = 1.0
    oh = oh.astype(ml_dtypes.bfloat16)

    if "nc" not in _CACHE:
        _CACHE["nc"] = build_nc()
    nc = _CACHE["nc"]

    in_maps = []
    for c in range(NCORES):
        rows = slice(c * T, (c + 1) * T)
        cs_c = np.ascontiguousarray(
            cosf[rows].reshape(NTT, 128, 2, 256).transpose(2, 0, 1, 3))
        sn_c = np.ascontiguousarray(
            sinf[rows].reshape(NTT, 128, 2, 256).transpose(2, 0, 1, 3))
        in_maps.append({
            "x0_in": np.ascontiguousarray(X[rows]), "wp_in": wp, "b_in": bp, "gw_in": gwT,
            "w1_in": w1p, "w3_in": w3p, "w2_in": w2p,
            "cs_in": cs_c, "sn_in": sn_c,
            "idb_in": idb, "idf_in": idf, "onc_in": onc, "onr_in": onr,
            "onb_in": onb, "oh_in": oh,
        })
    res = run_bass_kernel_spmd(nc, in_maps, core_ids=list(range(NCORES)))
    _CACHE["res"] = res
    out = np.concatenate([np.asarray(res.results[c]["out"]) for c in range(NCORES)], axis=0)
    return out.reshape(1, 16384, 2048).astype(np.float32)


# version 12
# speedup vs baseline: 1.0185x; 1.0135x over previous
import os
from contextlib import ExitStack

import numpy as np
import ml_dtypes
import concourse.bass as bass
import concourse.mybir as mybir
from concourse.bass_utils import run_bass_kernel_spmd

F32 = mybir.dt.float32
BF16 = mybir.dt.bfloat16
AF = mybir.ActivationFunctionType
ALU = mybir.AluOpType
AX = mybir.AxisListType

NCORES = 8
T = 2048
D = 2048
NE = 8
NTT = 16
NDC = 16
NHC = 16
NFL = 16
EPS = 1e-5

MAGIC = float(np.float32(1.5 * 2 ** 23))
INV2PI = float(np.float32(1.0 / (2.0 * np.pi)))
TWOPI = float(np.float32(2.0 * np.pi))
HALFPI = float(np.float32(np.pi / 2.0))
NEG = -1.0e30

ENGINES = ("sync", "scalar", "vector", "tensor", "gpsimd")


class Sem:
    __slots__ = ("h", "n")

    def __init__(self, h):
        self.h = h
        self.n = 0


class Sched:
    def __init__(self):
        self.ops = {e: [] for e in ENGINES}

    def emit(self, eng, fn, waits=(), incs=()):
        for s, v in waits:
            assert v <= s.n, f"deadlock: wait {v} > emitted {s.n} on {s.h.name}"
        lowered = (fn, [(s.h, v) for s, v in waits if v > 0], [])
        for s, k in incs:
            s.n += k
            lowered[2].append((s.h, k))
        self.ops[eng].append(lowered)

    def build(self, block):
        def make(name):
            ops = self.ops[name]

            def body(eng):
                for fn, waits, incs in ops:
                    for h, v in waits:
                        eng.wait_ge(h, v)
                    instr = fn(eng)
                    for h, k in incs:
                        if instr is None:
                            instr = eng.nop()
                        instr = instr.then_inc(h, k)
            return body

        for name in ENGINES:
            if self.ops[name]:
                getattr(block, name)(make(name))

    def counts(self):
        return {e: len(self.ops[e]) for e in ENGINES}


def build_nc():
    nc = bass.Bass("TRN2", target_bir_lowering=False)

    x0_in = nc.dram_tensor("x0_in", [T, D], F32, kind="ExternalInput").ap()
    wp_in = nc.dram_tensor("wp_in", [4, 128, 16 * 1024], F32, kind="ExternalInput").ap()
    b_in = nc.dram_tensor("b_in", [1, 4096], F32, kind="ExternalInput").ap()
    gw_in = nc.dram_tensor("gw_in", [128, 128], F32, kind="ExternalInput").ap()
    w1_in = nc.dram_tensor("w1_in", [NE, 32, 128, 2048], F32, kind="ExternalInput").ap()
    w3_in = nc.dram_tensor("w3_in", [NE, 32, 128, 2048], F32, kind="ExternalInput").ap()
    w2_in = nc.dram_tensor("w2_in", [NE, 2, NHC, 128, 2048], F32, kind="ExternalInput").ap()
    cs_in = nc.dram_tensor("cs_in", [2, NTT, 128, 256], F32, kind="ExternalInput").ap()
    sn_in = nc.dram_tensor("sn_in", [2, NTT, 128, 256], F32, kind="ExternalInput").ap()
    idb_in = nc.dram_tensor("idb_in", [128, 128], BF16, kind="ExternalInput").ap()
    idf_in = nc.dram_tensor("idf_in", [128, 128], F32, kind="ExternalInput").ap()
    onc_in = nc.dram_tensor("onc_in", [128, 1], F32, kind="ExternalInput").ap()
    onr_in = nc.dram_tensor("onr_in", [1, 128], F32, kind="ExternalInput").ap()
    onb_in = nc.dram_tensor("onb_in", [1, 128], BF16, kind="ExternalInput").ap()
    oh_in = nc.dram_tensor("oh_in", [8, 8 * 128], BF16, kind="ExternalInput").ap()
    out_d = nc.dram_tensor("out", [T, D], F32, kind="ExternalOutput").ap()
    x1_d = nc.dram_tensor("x1_d", [NTT, 128, D], F32, kind="Internal").ap()
    l2t_d = nc.dram_tensor("l2t_d", [4, NTT, 128, 512], BF16, kind="Internal").ap()
    ar_src_h = [nc.dram_tensor(f"ar_src{i}", [128, 4], F32, kind="Internal") for i in range(4)]
    ar_dst_h = [nc.dram_tensor(f"ar_dst{i}", [128, 4], F32, kind="Internal") for i in range(4)]
    ar_src_flat = [h.ap().rearrange("p g -> (p g)").unsqueeze(0) for h in ar_src_h]
    ar_dst_flat = [h.ap().rearrange("p g -> (p g)").unsqueeze(0) for h in ar_dst_h]

    dbg = bool(os.environ.get("KDBG"))
    dbgt = {}
    if dbg:
        dbgt["afin"] = nc.dram_tensor("dbg_afin", [1, 2048], F32, kind="ExternalOutput").ap()
        dbgt["stats"] = nc.dram_tensor("dbg_stats", [128, 192], F32, kind="ExternalOutput").ap()
        dbgt["scores"] = nc.dram_tensor("dbg_scores", [128, 128], F32, kind="ExternalOutput").ap()
        dbgt["coef"] = nc.dram_tensor("dbg_coef", [128, 128], F32, kind="ExternalOutput").ap()
        dbgt["coefT"] = nc.dram_tensor("dbg_coefT", [8, 2048], BF16, kind="ExternalOutput").ap()
        dbgt["ln1T"] = nc.dram_tensor("dbg_ln1T", [128, NDC * T], BF16, kind="ExternalOutput").ap()
        dbgt["x1"] = nc.dram_tensor("dbg_x1", [NTT, 128, D], F32, kind="ExternalOutput").ap()
        dbgt["l2t"] = nc.dram_tensor("dbg_l2t", [4, NTT, 128, 512], BF16,
                                     kind="ExternalOutput").ap()
        dbgt["yac"] = nc.dram_tensor("dbg_yac", [128, NHC * 1024], BF16,
                                     kind="ExternalOutput").ap()
        dbgt["yach0"] = nc.dram_tensor("dbg_yach0", [128, NHC * 1024], BF16,
                                       kind="ExternalOutput").ap()
        dbgt["cb0"] = nc.dram_tensor("dbg_cb0", [128, 1024], F32, kind="ExternalOutput").ap()
        dbgt["hp"] = nc.dram_tensor("dbg_hp", [128, NFL * 1024], BF16,
                                    kind="ExternalOutput").ap()

    sch = Sched()
    E = sch.emit

    with ExitStack() as stk:
        en = stk.enter_context

        def sem(name):
            return Sem(en(nc.semaphore(name)))

        SX = [sem("sx0"), sem("sx1")]
        STB = [sem("stb0"), sem("stb1")]
        SWB = [sem("swb0"), sem("swb1")]
        SW1 = [sem("sw1a"), sem("sw1b")]
        SW3 = [sem("sw3a"), sem("sw3b")]
        SW2 = [sem("sw2a"), sem("sw2b")]
        ST2 = [sem("st2a"), sem("st2b")]
        SLN = sem("sln")
        SCQ = sem("scq")
        SST = sem("sst")
        SARL = sem("sarl")
        SCS = sem("scs")
        SCG = sem("scg")
        CO = sem("co")
        M = sem("m")
        V = sem("v")
        A_ = sem("a")

        P = [en(nc.psum_tensor(f"ps{i}", [128, 512], F32)) for i in range(6)]
        PT = [en(nc.psum_tensor("pt0", [128, 1024], BF16)),
              en(nc.psum_tensor("pt1", [128, 1024], BF16))]
        psA = P[4][0:1, 0:512]
        psS = P[5][:, 0:8]
        psC = P[5][0:8, 128:256]
        psfree = {}

        def ps_wait(key):
            w = psfree.get(key)
            return [w] if w else []

        stats = en(nc.sbuf_tensor("stats", [128, 16 * 12], F32))

        def stc(q, tt=None):
            if tt is None:
                return stats[:, q * 16:(q + 1) * 16]
            return stats[:, q * 16 + tt:q * 16 + tt + 1]
        scores_sb = en(nc.sbuf_tensor("scores", [128, 128], F32))
        coefT = en(nc.sbuf_tensor("coefT", [8, 2048], BF16))
        xsl = [en(nc.sbuf_tensor("xs0", [128, D], F32)),
               en(nc.sbuf_tensor("xs1", [128, D], F32))]
        idb = en(nc.sbuf_tensor("idb", [128, 128], BF16))
        oh = en(nc.sbuf_tensor("oh", [8, 8 * 128], BF16))
        cpi2 = en(nc.sbuf_tensor("cpi2", [128, 1], F32))
        ceps = en(nc.sbuf_tensor("ceps", [128, 1], F32))

        sA = ExitStack()
        enA = sA.enter_context
        ln1T = enA(nc.sbuf_tensor("ln1T", [128, NDC * T], BF16))
        WbT = enA(nc.sbuf_tensor("Wb", [128, 2 * NDC * 1024], BF16))
        Wbs = [WbT[:, 0:NDC * 1024], WbT[:, NDC * 1024:2 * NDC * 1024]]
        ATMP = enA(nc.sbuf_tensor("ATMP", [128, 4096], F32))
        KH = ATMP[:, 0:512]
        G1 = ATMP[:, 512:1024]
        G2 = ATMP[:, 1024:1536]
        RK = ATMP[:, 1536:2048]
        RB = ATMP[:, 2048:2560]
        CKb = ATMP[:, 2560:3072]
        CVb = ATMP[:, 3072:3584]
        PSb = ATMP[:, 3584:4096]
        PSbs = [PSb, RB]
        psb_free = {}
        A_sb = enA(nc.sbuf_tensor("A_sb", [1, 2048], F32))
        A_fin = ATMP[0:1, 0:2048]
        A_bc = enA(nc.sbuf_tensor("A_bc", [128, 2048], F32))
        tabs = enA(nc.sbuf_tensor("tabs", [128, 2 * 256], F32))

        def tab(kind, slot):
            return tabs[:, kind * 256:(kind + 1) * 256]
        ln_f32 = enA(nc.sbuf_tensor("ln_f32", [128, D], F32))
        u_buf = ATMP[:, 0:2048]
        ln_bf = enA(nc.sbuf_tensor("ln_bf", [128, D], BF16))
        b_sb = enA(nc.sbuf_tensor("b_sb", [1, 4096], BF16))
        gwT = enA(nc.sbuf_tensor("gwT", [128, 128], BF16))
        onc = enA(nc.sbuf_tensor("onc", [128, 1], F32))
        onr = enA(nc.sbuf_tensor("onr", [1, 128], F32))
        onb = enA(nc.sbuf_tensor("onb", [1, 128], BF16))
        t2t = [enA(nc.sbuf_tensor("t2t0", [128, 256], BF16)),
               enA(nc.sbuf_tensor("t2t1", [128, 256], BF16))]
        gT1 = ATMP[:, 0:128]
        gE1 = ATMP[:, 128:256]
        gAB = ATMP[:, 256:384]
        gU1 = ATMP[:, 384:512]
        gS2 = ATMP[:, 512:640]
        gTO = ATMP[:, 640:768]
        gE2 = ATMP[:, 768:896]
        gCM = ATMP[:, 896:1024]
        gst = ATMP[:, 1024:1216]

        def gss(q):
            return gst[:, q * 16:(q + 1) * 16]

        E("sync", lambda e: e.dma_start(idb[:], idb_in), incs=[(SCS, 16)])
        E("sync", lambda e: e.dma_start(onc[:], onc_in), incs=[(SCS, 16)])
        E("sync", lambda e: e.dma_start(onr[:], onr_in), incs=[(SCS, 16)])
        E("sync", lambda e: e.dma_start(onb[:], onb_in), incs=[(SCS, 16)])
        E("sync", lambda e: e.dma_start(oh[:], oh_in), incs=[(SCS, 16)])
        consts_s = SCS.n
        E("gpsimd", lambda e: e.dma_start(gwT[:], gw_in), incs=[(SCG, 16)])
        E("gpsimd", lambda e: e.dma_start(b_sb[:], b_in), incs=[(SCG, 16)])
        consts_g = SCG.n
        E("vector", lambda e: e.memset(cpi2[:], HALFPI), incs=[(V, 1)])
        E("vector", lambda e: e.memset(ceps[:], EPS), incs=[(V, 1)])

        p2_lnb = {}
        p_rste = {}
        m_lastT = 0
        psq = {}
        for tt in range(NTT):
            sl = tt % 2
            w = []
            if tt >= 2:
                w = [(V, p2_lnb[tt - 2])]
            E("sync",
              lambda e, tt=tt, sl=sl: e.dma_start(xsl[sl][:], x0_in[tt * 128:(tt + 1) * 128, :]),
              waits=w, incs=[(SX[sl], 16)])
            lw = [(SX[sl], SX[sl].n)]
            E("scalar",
              lambda e, tt=tt, sl=sl: e.activation(ln_f32[:], xsl[sl][:], AF.Copy,
                                                   accum_out=stc(0, tt)),
              waits=lw + ([(A_, psq[tt - 1])] if tt > 0 else []), incs=[(A_, 1)])
            E("scalar",
              lambda e, tt=tt, sl=sl: e.activation(ln_f32[:], xsl[sl][:], AF.Square,
                                                   accum_out=stc(1, tt)),
              incs=[(A_, 1)])
            a_sq = A_.n
            psq[tt] = a_sq
            E("vector", lambda e, tt=tt: e.tensor_scalar(stc(2, tt), stc(0, tt), 1.0 / D,
                                                         None, ALU.mult),
              waits=[(A_, a_sq), (V, V.n)], incs=[(V, 1)])
            E("vector", lambda e, tt=tt: e.tensor_scalar(stc(9, tt), stc(1, tt), 1.0 / D,
                                                         None, ALU.mult),
              waits=[(V, V.n)], incs=[(V, 1)])
            E("vector", lambda e, tt=tt: e.tensor_tensor(stc(10, tt), stc(2, tt), stc(2, tt),
                                                         ALU.mult),
              waits=[(V, V.n)], incs=[(V, 1)])
            E("vector", lambda e, tt=tt: e.tensor_tensor(stc(11, tt), stc(9, tt), stc(10, tt),
                                                         ALU.subtract),
              waits=[(V, V.n)], incs=[(V, 1)])
            v_var = V.n
            E("scalar", lambda e, tt=tt: e.activation(stc(9, tt), stc(11, tt), AF.Ln,
                                                      bias=ceps[:]),
              waits=[(V, v_var)], incs=[(A_, 1)])
            E("vector", lambda e, tt=tt: e.tensor_scalar(stc(10, tt), stc(9, tt), -0.5,
                                                         None, ALU.mult),
              waits=[(A_, A_.n), (V, V.n)], incs=[(V, 1)])
            E("scalar", lambda e, tt=tt: e.activation(stc(3, tt), stc(10, tt), AF.Exp),
              waits=[(V, V.n)], incs=[(A_, 1)])
            a_rstd1 = A_.n
            p_rste[tt] = a_rstd1
            E("vector",
              lambda e, tt=tt, sl=sl: e.tensor_scalar(ln_bf[:], xsl[sl][:], stc(2, tt),
                                                      stc(3, tt), ALU.subtract, ALU.mult),
              waits=[(A_, a_rstd1), (V, V.n), (M, m_lastT)], incs=[(V, 1)])
            v_lnb = V.n
            p2_lnb[tt] = v_lnb
            for dcg in range(NDC // 4):
                pslot = dcg % 2
                for q in range(4):
                    dc = dcg * 4 + q
                    w = [(V, v_lnb), (SCS, consts_s)] + (ps_wait(("T", pslot)) if q == 0 else [])
                    E("tensor",
                      lambda e, dc=dc, q=q, pslot=pslot: e.transpose(
                          PT[pslot][:, q * 128:(q + 1) * 128],
                          ln_bf[:, dc * 128:(dc + 1) * 128], idb[:]),
                      waits=w, incs=([(M, 1)] if q == 3 else []))
                mt = M.n
                E("vector",
                  lambda e, tt=tt, dcg=dcg, pslot=pslot: e.tensor_copy(
                      ln1T.ap().rearrange("p (d t) -> p d t", t=T)[
                          :, dcg * 4:(dcg + 1) * 4, tt * 128:(tt + 1) * 128],
                      PT[pslot][:, 0:512].rearrange("p (g c) -> p g c", c=128)),
                  waits=[(M, mt)], incs=[(V, 1)])
                psfree[("T", pslot)] = (V, V.n)
            m_lastT = M.n
        v_p2_done = V.n
        v_m1s = V.n
        a_rstd1 = A_.n

        m_wb_free = {}
        v_Acp = 0
        v_tabfree = {}
        a_sin_prev = 0
        m_A_prev = 0
        v_P_prev = 0
        pend_A = None
        arw = {}
        for np_ in range(4):
            rot = np_ < 2
            wsl = np_ % 2
            Wb = Wbs[wsl]
            E("gpsimd",
              lambda e, np_=np_, wsl=wsl: e.dma_start(Wbs[wsl][:], wp_in[np_]),
              waits=([(M, m_wb_free[wsl])] if wsl in m_wb_free else []) + [(SCG, consts_g)],
              incs=[(SWB[wsl], 16)])
            swb_n = SWB[wsl].n
            for tt in range(NTT):
                pr = tt % 2
                psK, psV = P[2 * pr], P[2 * pr + 1]
                kK, kV = 2 * pr, 2 * pr + 1
                if rot:
                    tsl = 0
                    wtb = ([v_tabfree["last"]] if "last" in v_tabfree else [])
                    E("sync",
                      lambda e, np_=np_, tt=tt: e.dma_start(tab(0, 0), cs_in[np_, tt]),
                      waits=wtb, incs=[(STB[0], 16)])
                    E("sync",
                      lambda e, np_=np_, tt=tt: e.dma_start(tab(1, 0), sn_in[np_, tt]),
                      incs=[(STB[0], 16)])
                    stb_n = STB[0].n
                wk = [(SWB[wsl], swb_n)] + ps_wait(kK)
                if np_ == 0 and tt == 0:
                    wk.append((V, v_p2_done))
                for dc in range(NDC):
                    E("tensor",
                      lambda e, tt=tt, dc=dc, psK=psK, Wb=Wb: e.matmul(
                          psK[:], ln1T[:, dc * T + tt * 128: dc * T + (tt + 1) * 128],
                          Wb[:, dc * 1024: dc * 1024 + 512], start=(dc == 0), stop=False),
                      waits=(wk if dc == 0 else []))
                E("tensor",
                  lambda e, np_=np_, psK=psK: e.matmul(
                      psK[:], onb[:], b_sb[0:1, np_ * 1024: np_ * 1024 + 512],
                      start=False, stop=True),
                  incs=[(M, 1)])
                m_k = M.n
                wv = ps_wait(kV)
                for dc in range(NDC):
                    E("tensor",
                      lambda e, tt=tt, dc=dc, psV=psV, Wb=Wb: e.matmul(
                          psV[:], ln1T[:, dc * T + tt * 128: dc * T + (tt + 1) * 128],
                          Wb[:, dc * 1024 + 512: dc * 1024 + 1024], start=(dc == 0), stop=False),
                      waits=(wv if dc == 0 else []))
                E("tensor",
                  lambda e, np_=np_, psV=psV: e.matmul(
                      psV[:], onb[:], b_sb[0:1, np_ * 1024 + 512: np_ * 1024 + 1024],
                      start=False, stop=True),
                  incs=[(M, 1)])
                m_v = M.n

                def rotary(ps, m_ps, stb_val, tslot):
                    ev, od = ps[:, 0:512:2], ps[:, 1:512:2]
                    ct, st = tab(0, tslot), tab(1, tslot)
                    E("vector", lambda e, ev=ev, ct=ct: e.tensor_tensor(
                        G1[:, 0:256], ev, ct, ALU.mult),
                      waits=[(M, m_ps), (STB[tslot], stb_val), (V, V.n)], incs=[(V, 1)])
                    E("vector", lambda e, od=od, st=st: e.tensor_tensor(
                        G2[:, 0:256], od, st, ALU.mult), waits=[(V, V.n)], incs=[(V, 1)])
                    E("vector", lambda e: e.tensor_tensor(
                        KH[:, 0:512:2], G1[:, 0:256], G2[:, 0:256], ALU.subtract),
                      waits=[(V, V.n)], incs=[(V, 1)])
                    E("vector", lambda e, od=od, ct=ct: e.tensor_tensor(
                        G1[:, 0:256], od, ct, ALU.mult), waits=[(V, V.n)], incs=[(V, 1)])
                    E("vector", lambda e, ev=ev, st=st: e.tensor_tensor(
                        G2[:, 0:256], ev, st, ALU.mult), waits=[(V, V.n)], incs=[(V, 1)])
                    E("vector", lambda e: e.tensor_tensor(
                        KH[:, 1:512:2], G1[:, 0:256], G2[:, 0:256], ALU.add),
                      waits=[(V, V.n)], incs=[(V, 1)])
                    return V.n

                def cos_chain(src_ap, out_sb, first_w, rk_w, sin_w):
                    E("vector", lambda e, src_ap=src_ap: e.tensor_scalar(
                        G1[:], src_ap, INV2PI, 0.25, ALU.mult, ALU.add),
                      waits=list(first_w) + [(V, V.n)], incs=[(V, 1)])
                    E("vector", lambda e: e.tensor_scalar(
                        G2[:], G1[:], MAGIC, MAGIC, ALU.add, ALU.subtract), incs=[(V, 1)])
                    E("vector", lambda e: e.tensor_scalar(
                        G1[:], G2[:], TWOPI, None, ALU.mult), incs=[(V, 1)])
                    E("vector", lambda e, src_ap=src_ap: e.tensor_tensor(
                        RK[:], src_ap, G1[:], ALU.subtract), waits=rk_w, incs=[(V, 1)])
                    vr = V.n
                    E("scalar", lambda e, out_sb=out_sb: e.activation(
                        out_sb[:], RK[:], AF.Sin, bias=cpi2[:]),
                      waits=[(V, vr)] + sin_w, incs=[(A_, 1)])
                    return A_.n

                rk_guard = ([(A_, a_sin_prev)] if a_sin_prev else []) + \
                           ([(M, m_A_prev)] if m_A_prev else [])
                sin_guard = [(V, v_P_prev)] if v_P_prev else []
                if rot:
                    rotary(psK, m_k, stb_n, tsl)
                    v_k_done = V.n
                    aK = cos_chain(KH[:], CKb, [], rk_guard, sin_guard)
                    rotary(psV, m_v, stb_n, tsl)
                    v_v_done = V.n
                    aV = cos_chain(KH[:], CVb, [], [(A_, aK)], sin_guard)
                    v_tabfree["last"] = (V, v_v_done)
                else:
                    aK = cos_chain(psK[:], CKb, [(M, m_k)], rk_guard, sin_guard)
                    v_k_done = V.n
                    aV = cos_chain(psV[:], CVb, [(M, m_v)], [(A_, aK)], sin_guard)
                    v_v_done = V.n
                psfree[kK] = (V, v_k_done)
                psfree[kV] = (V, v_v_done)
                psb = PSbs[tt % 2]
                E("vector", lambda e, psb=psb: e.tensor_tensor(psb[:], CKb[:], CVb[:], ALU.mult),
                  waits=[(A_, aK), (A_, aV)] +
                        ([(M, psb_free[tt % 2])] if (tt % 2) in psb_free else []),
                  incs=[(V, 1)])
                v_P = V.n
                if pend_A is not None:
                    ptt, pv, pfirst, ppsb = pend_A
                    wA = [(V, pv)]
                    if pfirst:
                        wA += [(SCS, consts_s)] + ([(V, v_Acp)] if v_Acp else [])
                    E("tensor",
                      lambda e, ppsb=ppsb, pfirst=pfirst: e.matmul(
                          psA, onc[:], ppsb[:], start=pfirst, stop=False),
                      waits=wA, incs=[(M, 1)])
                    psb_free[ptt % 2] = M.n
                pend_A = (tt, v_P, (tt == 0), psb)
                a_sin_prev = aV
                v_P_prev = v_P
            ptt, pv, pfirst, ppsb = pend_A
            E("tensor",
              lambda e, ppsb=ppsb, pfirst=pfirst: e.matmul(
                  psA, onc[:], ppsb[:], start=pfirst, stop=True),
              waits=[(V, pv)] + ([(SCS, consts_s)] if pfirst else []), incs=[(M, 1)])
            psb_free[ptt % 2] = M.n
            pend_A = None
            m_A_prev = M.n
            m_endKV = M.n
            E("vector",
              lambda e, np_=np_: e.tensor_copy(A_sb[0:1, np_ * 512:(np_ + 1) * 512], psA),
              waits=[(M, m_A_prev)], incs=[(V, 1)])
            v_Acp = V.n
            m_wb_free[wsl] = M.n
            E("scalar",
              lambda e, np_=np_: e.dma_start(ar_src_flat[np_],
                                             A_sb[0:1, np_ * 512:(np_ + 1) * 512]),
              waits=[(V, v_Acp)], incs=[(SCQ, 16)])
            arw[np_] = SCQ.n
        psfree[4] = (V, v_Acp)
        for np_ in range(4):
            E("gpsimd", lambda e, np_=np_: e.collective_compute(
                "AllReduce", ALU.add, replica_groups=[list(range(NCORES))],
                ins=[ar_src_h[np_].ap().opt()], outs=[ar_dst_h[np_].ap().opt()]),
              waits=[(SCQ, arw[np_])], incs=[(CO, 1)])

        pre_sx = {}
        for tt in range(2):
            E("sync",
              lambda e, tt=tt: e.dma_start(xsl[tt][:], x0_in[tt * 128:(tt + 1) * 128, :]),
              waits=[(V, p2_lnb[NTT - 2 + tt])], incs=[(SX[tt], 16)])
            pre_sx[tt] = SX[tt].n
        for np_ in range(4):
            E("sync",
              lambda e, np_=np_: e.dma_start(A_sb[0:1, np_ * 512:(np_ + 1) * 512],
                                             ar_dst_flat[np_]),
              waits=[(CO, 4)], incs=[(SARL, 16)])
        E("vector", lambda e: e.tensor_scalar(A_fin[:], A_sb[:], 0.5, 0.5, ALU.mult, ALU.add),
          waits=[(SARL, SARL.n)], incs=[(V, 1)])
        v_afin = V.n
        for i in range(4):
            E("tensor",
              lambda e, i=i: e.matmul(P[i][:], onr[:], A_fin[0:1, i * 512:(i + 1) * 512],
                                      start=True, stop=True),
              waits=[(V, v_afin)] + ps_wait(i), incs=[(M, 1)])
            mi = M.n
            E("vector",
              lambda e, i=i: e.tensor_copy(A_bc[:, i * 512:(i + 1) * 512], P[i][:]),
              waits=[(M, mi)], incs=[(V, 1)])
            psfree[i] = (V, V.n)
        v_abc = V.n

        a2 = {}
        x1w = {}
        a2sq = {}
        a2c_lnb = {}
        m_lastT2 = m_lastT
        for tt in range(NTT):
            sl = tt % 2
            if tt < 2:
                lw = (SX[sl], pre_sx[tt])
            else:
                E("sync",
                  lambda e, tt=tt, sl=sl: e.dma_start(xsl[sl][:],
                                                      x0_in[tt * 128:(tt + 1) * 128, :]),
                  waits=[(V, a2[tt - 2])], incs=[(SX[sl], 16)])
                lw = (SX[sl], SX[sl].n)
            lnw = [lw, (V, v_abc), (V, v_m1s), (A_, a_rstd1)]
            if tt > 0:
                lnw += [(A_, a2sq[tt - 1]), (SCQ, x1w[tt - 1])]
            E("vector",
              lambda e, tt=tt, sl=sl: e.tensor_scalar(ln_f32[:], xsl[sl][:], stc(2, tt),
                                                      stc(3, tt), ALU.subtract, ALU.mult),
              waits=lnw, incs=[(V, 1)])
            E("vector", lambda e: e.tensor_tensor(u_buf[:], A_bc[:], ln_f32[:], ALU.mult),
              incs=[(V, 1)])
            E("vector", lambda e, sl=sl: e.tensor_tensor(ln_f32[:], xsl[sl][:], u_buf[:],
                                                         ALU.add), incs=[(V, 1)])
            v_x1 = V.n
            a2[tt] = v_x1
            E("scalar", lambda e, tt=tt: e.dma_start(x1_d[tt], ln_f32[:]),
              waits=[(V, v_x1)], incs=[(SCQ, 16)])
            x1w[tt] = SCQ.n
            E("scalar", lambda e, tt=tt: e.activation(u_buf[:], ln_f32[:], AF.Copy,
                                                      accum_out=stc(5, tt)),
              waits=[(V, v_x1)], incs=[(A_, 1)])
            E("scalar", lambda e, tt=tt: e.activation(u_buf[:], ln_f32[:], AF.Square,
                                                      accum_out=stc(6, tt)),
              incs=[(A_, 1)])
            a_sq = A_.n
            a2sq[tt] = a_sq
            E("vector", lambda e, tt=tt: e.tensor_scalar(stc(7, tt), stc(5, tt), 1.0 / D,
                                                         None, ALU.mult),
              waits=[(A_, a_sq), (V, V.n)], incs=[(V, 1)])
            E("vector", lambda e, tt=tt: e.tensor_scalar(stc(9, tt), stc(6, tt), 1.0 / D,
                                                         None, ALU.mult),
              waits=[(V, V.n)], incs=[(V, 1)])
            E("vector", lambda e, tt=tt: e.tensor_tensor(stc(10, tt), stc(7, tt), stc(7, tt),
                                                         ALU.mult),
              waits=[(V, V.n)], incs=[(V, 1)])
            E("vector", lambda e, tt=tt: e.tensor_tensor(stc(11, tt), stc(9, tt), stc(10, tt),
                                                         ALU.subtract),
              waits=[(V, V.n)], incs=[(V, 1)])
            v_var = V.n
            E("scalar", lambda e, tt=tt: e.activation(stc(9, tt), stc(11, tt), AF.Ln,
                                                      bias=ceps[:]),
              waits=[(V, v_var)] + ([(A_, a2rste[tt - 1])] if tt > 0 else []),
              incs=[(A_, 1)])
            E("vector", lambda e, tt=tt: e.tensor_scalar(stc(10, tt), stc(9, tt), -0.5,
                                                         None, ALU.mult),
              waits=[(A_, A_.n), (V, V.n)], incs=[(V, 1)])
            E("scalar", lambda e, tt=tt: e.activation(stc(8, tt), stc(10, tt), AF.Exp),
              waits=[(V, V.n)], incs=[(A_, 1)])
            a_rstd2 = A_.n
            if tt == 0:
                a2rste = {}
            a2rste[tt] = a_rstd2
            E("vector",
              lambda e, tt=tt: e.tensor_scalar(ln_bf[:], ln_f32[:], stc(7, tt),
                                               stc(8, tt), ALU.subtract, ALU.mult),
              waits=[(A_, a_rstd2), (V, V.n), (M, m_lastT2)], incs=[(V, 1)])
            v_lnb = V.n
            a2c_lnb[tt] = v_lnb
            for dcg in range(NDC // 2):
                pslot = dcg % 2
                for q in range(2):
                    dc = dcg * 2 + q
                    w = [(V, v_lnb)] + (ps_wait(("T", pslot)) if q == 0 else [])
                    E("tensor",
                      lambda e, dc=dc, q=q, pslot=pslot: e.transpose(
                          PT[pslot][:, q * 128:(q + 1) * 128],
                          ln_bf[:, dc * 128:(dc + 1) * 128], idb[:]),
                      waits=w, incs=([(M, 1)] if q == 1 else []))
                mt = M.n
                wv = [(M, mt)] + ps_wait(("t2", pslot))
                if ST2[pslot].n > 0:
                    wv.append((ST2[pslot], ST2[pslot].n))
                E("vector",
                  lambda e, pslot=pslot: e.tensor_copy(t2t[pslot][:], PT[pslot][:, 0:256]),
                  waits=wv, incs=[(V, 1)])
                psfree[("T", pslot)] = (V, V.n)
                vev = V.n
                E("scalar",
                  lambda e, dcg=dcg, tt=tt, pslot=pslot: e.dma_start(
                      l2t_d[dcg // 2, tt, :, (dcg % 2) * 256:(dcg % 2 + 1) * 256],
                      t2t[pslot][:]),
                  waits=[(V, vev)], incs=[(ST2[pslot], 16)])
                for q in range(2):
                    dc = dcg * 2 + q
                    wmm = [(V, vev)]
                    if dc == 0:
                        wmm += [(SCG, consts_g)] + ps_wait(5)
                    E("tensor",
                      lambda e, dc=dc, q=q, pslot=pslot: e.matmul(
                          psS, t2t[pslot][:, q * 128:(q + 1) * 128],
                          gwT[:, dc * 8:(dc + 1) * 8],
                          start=(dc == 0), stop=(dc == NDC - 1)),
                      waits=(wmm if q == 0 else []),
                      incs=([(M, 1)] if dc == NDC - 1 else []))
                psfree[("t2", pslot)] = (M, M.n)
            m_lastT2 = M.n
            E("vector",
              lambda e, tt=tt: e.tensor_copy(scores_sb[:, tt * 8:(tt + 1) * 8], psS),
              waits=[(M, m_lastT2)], incs=[(V, 1)])
            psfree[5] = (V, V.n)
            if tt == 7:
                scq_l2t_half0 = [ST2[0].n, ST2[1].n]
        scq_x1 = SCQ.n
        scq_l2t = [ST2[0].n, ST2[1].n]
        scq_l2t_h0 = list(scq_l2t_half0)

        def g3(t):
            a = t if isinstance(t, bass.AP) else t.ap()
            return a.rearrange("p (g k) -> p g k", k=8)

        def bc16(t16):
            return t16.unsqueeze(2).broadcast_to([128, 16, 8])

        v_scores = V.n

        def gv(fn, waits=()):
            E("vector", fn, waits=[(V, V.n)] + list(waits), incs=[(V, 1)])
            return V.n

        def ga(fn, wv):
            E("scalar", fn, waits=[(V, wv)], incs=[(A_, 1)])
            return A_.n

        def gw(fn, wa):
            E("vector", fn, waits=[(A_, wa), (V, V.n)], incs=[(V, 1)])
            return V.n

        BIG = 1.0e30
        v_nv1 = gv(lambda e: e.tensor_reduce(gss(0), g3(scores_sb), AX.X, ALU.max, negate=True))
        gv(lambda e: e.tensor_tensor(g3(gT1), g3(scores_sb), bc16(gss(0)), ALU.add),
           waits=[(V, v_nv1)])
        gv(lambda e: e.tensor_scalar(gS2[:], gT1[:], -BIG, None, ALU.mult))
        gv(lambda e: e.tensor_scalar(gCM[:], gS2[:], 1.0, None, ALU.min))
        gv(lambda e: e.tensor_scalar(gE1[:], gCM[:], -1.0, 1.0, ALU.mult, ALU.add))
        ax = ga(lambda e: e.activation(gAB[:], scores_sb[:], AF.Abs), v_scores)
        v_nm1 = gw(lambda e: e.tensor_scalar(gss(1), gss(0), -1.0, None, ALU.mult), ax)
        gv(lambda e: e.tensor_tensor(g3(gU1), g3(gAB), bc16(gss(1)), ALU.max),
           waits=[(V, v_nm1)])
        gv(lambda e: e.tensor_scalar(gCM[:], gU1[:], 0.02, None, ALU.mult))
        gv(lambda e: e.tensor_tensor(gS2[:], gT1[:], gCM[:], ALU.add))
        gv(lambda e: e.tensor_scalar(gCM[:], gS2[:], -BIG, None, ALU.mult))
        gv(lambda e: e.tensor_scalar(gU1[:], gCM[:], 1.0, None, ALU.min))
        gv(lambda e: e.tensor_scalar(gCM[:], gU1[:], 0.0, None, ALU.max))
        gv(lambda e: e.tensor_scalar(gS2[:], gCM[:], NEG, None, ALU.mult))
        v_sm1 = gv(lambda e: e.tensor_tensor(gU1[:], gT1[:], gS2[:], ALU.add))
        a_e1 = ga(lambda e: e.activation(gTO[:], gU1[:], AF.Exp), v_sm1)
        v_d1 = gw(lambda e: e.tensor_reduce(gss(2), g3(gTO), AX.X, ALU.add), a_e1)
        a_l1 = ga(lambda e: e.activation(gss(3), gss(2), AF.Ln), v_d1)
        v_l1 = gw(lambda e: e.tensor_scalar(gss(9), gss(3), -1.0, None, ALU.mult), a_l1)
        a_r1 = ga(lambda e: e.activation(gss(3), gss(9), AF.Exp), v_l1)
        gv(lambda e: e.tensor_tensor(gU1[:], gTO[:], gE1[:], ALU.mult))
        gv(lambda e: e.tensor_reduce(gss(4), g3(gU1), AX.X, ALU.add))
        v_m1v = gw(lambda e: e.tensor_tensor(gss(5), gss(4), gss(3), ALU.mult), a_r1)
        gv(lambda e: e.tensor_scalar(gS2[:], gE1[:], NEG, None, ALU.mult))
        gv(lambda e: e.tensor_tensor(gU1[:], scores_sb[:], gS2[:], ALU.add))
        v_nv2 = gv(lambda e: e.tensor_reduce(gss(6), g3(gU1), AX.X, ALU.max, negate=True))
        gv(lambda e: e.tensor_tensor(g3(gTO), g3(scores_sb), bc16(gss(6)), ALU.add),
           waits=[(V, v_nv2)])
        gv(lambda e: e.tensor_tensor(g3(gS2), g3(gU1), bc16(gss(6)), ALU.add))
        gv(lambda e: e.tensor_scalar(gCM[:], gS2[:], -BIG, None, ALU.mult))
        gv(lambda e: e.tensor_scalar(gU1[:], gCM[:], 1.0, None, ALU.min))
        gv(lambda e: e.tensor_scalar(gE2[:], gU1[:], -1.0, 1.0, ALU.mult, ALU.add))
        v_nm2 = gv(lambda e: e.tensor_scalar(gss(7), gss(6), -1.0, None, ALU.mult))
        gv(lambda e: e.tensor_tensor(g3(gS2), g3(gAB), bc16(gss(7)), ALU.max),
           waits=[(V, v_nm2)])
        gv(lambda e: e.tensor_scalar(gU1[:], gS2[:], 0.02, None, ALU.mult))
        gv(lambda e: e.tensor_tensor(gS2[:], gTO[:], gU1[:], ALU.add))
        gv(lambda e: e.tensor_scalar(gCM[:], gS2[:], -BIG, None, ALU.mult))
        gv(lambda e: e.tensor_scalar(gU1[:], gCM[:], 1.0, None, ALU.min))
        gv(lambda e: e.tensor_scalar(gCM[:], gU1[:], 0.0, None, ALU.max))
        gv(lambda e: e.tensor_scalar(gU1[:], gCM[:], NEG, None, ALU.mult))
        gv(lambda e: e.tensor_scalar(gCM[:], gE1[:], NEG, None, ALU.mult))
        gv(lambda e: e.tensor_tensor(gS2[:], gU1[:], gCM[:], ALU.add))
        v_sm2 = gv(lambda e: e.tensor_tensor(gU1[:], gTO[:], gS2[:], ALU.add))
        a_e2 = ga(lambda e: e.activation(gCM[:], gU1[:], AF.Exp), v_sm2)
        v_d2 = gw(lambda e: e.tensor_reduce(gss(8), g3(gCM), AX.X, ALU.add), a_e2)
        a_l2 = ga(lambda e: e.activation(gss(9), gss(8), AF.Ln), v_d2)
        v_l2 = gw(lambda e: e.tensor_scalar(gss(10), gss(9), -1.0, None, ALU.mult), a_l2)
        a_r2 = ga(lambda e: e.activation(gss(9), gss(10), AF.Exp), v_l2)
        gv(lambda e: e.tensor_tensor(gU1[:], gCM[:], gE2[:], ALU.mult))
        gv(lambda e: e.tensor_reduce(gss(10), g3(gU1), AX.X, ALU.add))
        v_m2v = gw(lambda e: e.tensor_tensor(gss(11), gss(10), gss(9), ALU.mult), a_r2)
        gv(lambda e: e.tensor_tensor(g3(gT1), g3(gE1), bc16(gss(5)), ALU.mult),
           waits=[(V, v_m1v)])
        gv(lambda e: e.tensor_tensor(g3(gS2), g3(gE2), bc16(gss(11)), ALU.mult),
           waits=[(V, v_m2v)])
        v_coef = gv(lambda e: e.tensor_tensor(gU1[:], gT1[:], gS2[:], ALU.add))
        cfb = ln_bf[:, 0:128]
        v_cfb = gv(lambda e: e.tensor_copy(cfb, gU1[:]), waits=[(V, v_coef)])
        psCb = PT[0][0:8, 512:640]
        for tt in range(NTT):
            w = [(V, v_cfb), (SCS, consts_s)] + ps_wait(("cf",))
            E("tensor",
              lambda e, tt=tt: e.transpose(psCb, ln_bf[:, tt * 8: tt * 8 + 8], idb[:]),
              waits=w, incs=[(M, 1)])
            mt = M.n
            E("vector",
              lambda e, tt=tt: e.tensor_copy(coefT[0:8, tt * 128:(tt + 1) * 128], psCb),
              waits=[(M, mt)], incs=[(V, 1)])
            psfree[("cf",)] = (V, V.n)
        v_coefT = V.n
        m_endA = M.n
        v_endA = V.n
        a_endA = A_.n

        SDB = sem("sdb")
        if dbg:
            E("sync", lambda e: e.dma_start(dbgt["afin"], A_fin[:]),
              waits=[(V, v_afin)], incs=[(SDB, 16)])
            E("sync", lambda e: e.dma_start(dbgt["stats"], stats[:]),
              waits=[(A_, a_rstd2)], incs=[(SDB, 16)])
            E("sync", lambda e: e.dma_start(dbgt["scores"], scores_sb[:]),
              waits=[(V, v_scores)], incs=[(SDB, 16)])
            E("sync", lambda e: e.dma_start(dbgt["coef"], gU1[:]),
              waits=[(V, v_coef)], incs=[(SDB, 16)])
            E("sync", lambda e: e.dma_start(dbgt["coefT"], coefT[:]),
              waits=[(V, v_coefT)], incs=[(SDB, 16)])
            E("sync", lambda e: e.dma_start(dbgt["ln1T"], ln1T[:]),
              waits=[(V, v_p2_done)], incs=[(SDB, 16)])

        sA.close()

        sB = ExitStack()
        enB = sB.enter_context
        hP = enB(nc.sbuf_tensor("hP", [128, NFL * 1024], BF16))
        ln2T = enB(nc.sbuf_tensor("ln2T", [128, NDC * 1024], BF16))
        yac = [enB(nc.sbuf_tensor("yac0", [128, NHC * 1024], BF16)),
               enB(nc.sbuf_tensor("yac1", [128, NHC * 1024], BF16))]
        w1b = [enB(nc.sbuf_tensor("w1b0", [128, NDC * 128], BF16)),
               enB(nc.sbuf_tensor("w1b1", [128, NDC * 128], BF16))]
        w3b = [enB(nc.sbuf_tensor("w3b0", [128, NDC * 128], BF16)),
               enB(nc.sbuf_tensor("w3b1", [128, NDC * 128], BF16))]
        w2b = [enB(nc.sbuf_tensor("w2b0", [128, NFL * 128], BF16)),
               enB(nc.sbuf_tensor("w2b1", [128, NFL * 128], BF16))]
        c_b = enB(nc.sbuf_tensor("c_b", [128, 1024], F32))
        s_tmp = [enB(nc.sbuf_tensor("stmp0", [128, 512], BF16)),
                 enB(nc.sbuf_tensor("stmp1", [128, 512], BF16))]
        ye_t = [enB(nc.sbuf_tensor("ye0", [128, 512], BF16)),
                enB(nc.sbuf_tensor("ye1", [128, 512], BF16))]
        out_t = enB(nc.sbuf_tensor("out_t", [128, D], F32))

        m_h_free = {}
        m_w2_free = {}
        a_silu = {}
        v_hprime = {}
        v_yev = {}
        m_h_last = 0
        m_endhalf = 0
        sst_last = 0
        eh_out = {}
        dbg_cbw = None
        dbg_h0w = None

        for half in range(2):
            if half == 0:
                wl = [(M, m_endKV), (ST2[0], scq_l2t_h0[0]), (ST2[1], scq_l2t_h0[1])]
            else:
                wl = [(M, m_h_last), (ST2[0], scq_l2t[0]), (ST2[1], scq_l2t[1])]
            wl += ([(SDB, SDB.n)] if (dbg and half == 0) else []) + \
                  ([dbg_h0w] if (dbg and half == 1 and dbg_h0w) else [])
            for dc in range(NDC):
                E("sync",
                  lambda e, dc=dc, half=half: e.dma_start(
                      ln2T.ap()[:, dc * 1024:(dc + 1) * 1024].rearrange(
                          "p (tl t) -> p tl t", t=128),
                      l2t_d[dc // 4, half * 8:(half + 1) * 8, :,
                            (dc % 4) * 128:(dc % 4 + 1) * 128].transpose([1, 0, 2])),
                  waits=(wl if dc == 0 else []), incs=[(SLN, 16)])
            sln_n = SLN.n
            E("vector", lambda e: e.memset(yac[0][:], 0.0),
              waits=(([(V, v_endA)] if half == 0 else [(M, m_endhalf)]) +
                     ([dbg_h0w] if (dbg and half == 1 and dbg_h0w) else [])),
              incs=[(V, 1)])

            for ee in range(NE):
                for i in range(2):
                    w = [(V, v_coefT)] + ps_wait(4 + i)
                    E("tensor",
                      lambda e, ee=ee, half=half, i=i: e.matmul(
                          P[4 + i][:], oh[0:8, ee * 128:(ee + 1) * 128],
                          coefT[0:8, half * 1024 + i * 512: half * 1024 + (i + 1) * 512],
                          start=True, stop=True),
                      waits=w, incs=[(M, 1)])
                    mi = M.n
                    E("vector",
                      lambda e, i=i: e.tensor_copy(c_b[:, i * 512:(i + 1) * 512], P[4 + i][:]),
                      waits=[(M, mi)] + ([dbg_cbw] if dbg_cbw else []), incs=[(V, 1)])
                    dbg_cbw = None
                    psfree[4 + i] = (V, V.n)
                v_cb_e = V.n
                if dbg and half == 0 and ee == 0:
                    E("sync", lambda e: e.dma_start(dbgt["cb0"], c_b[:]),
                      waits=[(V, v_cb_e)], incs=[(SDB, 16)])
                    dbg_cbw = (SDB, SDB.n)

                for fh in range(2):
                    step = ee * 2 + fh
                    ysrc, ydst = yac[step % 2], yac[(step + 1) % 2]
                    for fl in range(NFL):
                        fc = fh * NFL + fl
                        sl = fl % 2
                        w1w = [(SCG, consts_g)]
                        if half == 0 and ee == 0 and fc == 0:
                            w1w += [(V, v_endA), (A_, a_endA), (SCQ, SCQ.n)]
                        if ("w1", sl) in m_h_free:
                            w1w.append((M, m_h_free[("w1", sl)]))
                        E("gpsimd",
                          lambda e, ee=ee, fc=fc, sl=sl: e.dma_start(w1b[sl][:], w1_in[ee, fc]),
                          waits=w1w, incs=[(SW1[sl], 16)])
                        sw1_n = SW1[sl].n
                        w3w = []
                        if ("w3", sl) in m_h_free:
                            w3w.append((M, m_h_free[("w3", sl)]))
                        E("gpsimd",
                          lambda e, ee=ee, fc=fc, sl=sl: e.dma_start(w3b[sl][:], w3_in[ee, fc]),
                          waits=w3w, incs=[(SW3[sl], 16)])
                        sw3_n = SW3[sl].n
                        for tau in range(2):
                            PA, PB = P[2 * tau], P[2 * tau + 1]
                            wh = [(SW1[sl], sw1_n), (SLN, sln_n)]
                            if tau in a_silu:
                                wh.append((A_, a_silu[tau]))
                            else:
                                wh += ps_wait(2 * tau)
                            for dc in range(NDC):
                                E("tensor",
                                  lambda e, tau=tau, dc=dc, sl=sl, PA=PA: e.matmul(
                                      PA[:], w1b[sl][:, dc * 128:(dc + 1) * 128],
                                      ln2T[:, dc * 1024 + tau * 512:
                                           dc * 1024 + (tau + 1) * 512],
                                      start=(dc == 0), stop=(dc == NDC - 1)),
                                  waits=(wh if dc == 0 else []),
                                  incs=([(M, 1)] if dc == NDC - 1 else []))
                            m_h1 = M.n
                            if tau == 1:
                                m_h_free[("w1", sl)] = m_h1
                            ws = [(M, m_h1)]
                            if tau in v_hprime:
                                ws.append((V, v_hprime[tau]))
                            E("scalar",
                              lambda e, tau=tau, PA=PA: e.activation(s_tmp[tau][:], PA[:],
                                                                     AF.Silu),
                              waits=ws, incs=[(A_, 1)])
                            a_silu[tau] = A_.n
                            wh3 = [(SW3[sl], sw3_n)]
                            if tau in v_hprime:
                                wh3.append((V, v_hprime[tau]))
                            else:
                                wh3 += ps_wait(2 * tau + 1)
                            for dc in range(NDC):
                                E("tensor",
                                  lambda e, tau=tau, dc=dc, sl=sl, PB=PB: e.matmul(
                                      PB[:], w3b[sl][:, dc * 128:(dc + 1) * 128],
                                      ln2T[:, dc * 1024 + tau * 512:
                                           dc * 1024 + (tau + 1) * 512],
                                      start=(dc == 0), stop=(dc == NDC - 1)),
                                  waits=(wh3 if dc == 0 else []),
                                  incs=([(M, 1)] if dc == NDC - 1 else []))
                            m_h3 = M.n
                            if tau == 1:
                                m_h_free[("w3", sl)] = m_h3
                            E("vector",
                              lambda e, fl=fl, tau=tau, PB=PB: e.tensor_tensor(
                                  hP[:, fl * 1024 + tau * 512: fl * 1024 + (tau + 1) * 512],
                                  s_tmp[tau][:], PB[:], ALU.mult),
                              waits=[(A_, a_silu[tau]), (M, m_h3)], incs=[(V, 1)])
                            v_hprime[tau] = V.n
                    m_h_last = M.n
                    v_hp_all = V.n

                    for hc in range(NHC):
                        sl = hc % 2
                        w2w = []
                        if ("w2", sl) in m_w2_free:
                            w2w.append((M, m_w2_free[("w2", sl)]))
                        E("gpsimd",
                          lambda e, ee=ee, fh=fh, hc=hc, sl=sl: e.dma_start(
                              w2b[sl][:], w2_in[ee, fh, hc]),
                          waits=w2w, incs=[(SW2[sl], 16)])
                        sw2_n = SW2[sl].n
                        for tau in range(2):
                            PY = P[4 + tau]
                            wy = [(SW2[sl], sw2_n)]
                            if hc == 0:
                                wy.append((V, v_hp_all))
                            if tau in v_yev:
                                wy.append((V, v_yev[tau]))
                            else:
                                wy += ps_wait(4 + tau)
                            for fl in range(NFL):
                                E("tensor",
                                  lambda e, tau=tau, fl=fl, sl=sl, PY=PY: e.matmul(
                                      PY[:], w2b[sl][:, fl * 128:(fl + 1) * 128],
                                      hP[:, fl * 1024 + tau * 512:
                                         fl * 1024 + (tau + 1) * 512],
                                      start=(fl == 0), stop=(fl == NFL - 1)),
                                  waits=(wy if fl == 0 else []),
                                  incs=([(M, 1)] if fl == NFL - 1 else []))
                            m_y = M.n
                            if tau == 1:
                                m_w2_free[("w2", sl)] = m_y
                            E("vector",
                              lambda e, tau=tau, PY=PY: e.tensor_tensor(
                                  ye_t[tau][:], PY[:], c_b[:, tau * 512:(tau + 1) * 512],
                                  ALU.mult),
                              waits=[(M, m_y)], incs=[(V, 1)])
                            E("vector",
                              lambda e, hc=hc, tau=tau, ysrc=ysrc, ydst=ydst: e.tensor_tensor(
                                  ydst[:, hc * 1024 + tau * 512: hc * 1024 + (tau + 1) * 512],
                                  ysrc[:, hc * 1024 + tau * 512: hc * 1024 + (tau + 1) * 512],
                                  ye_t[tau][:], ALU.add),
                              incs=[(V, 1)])
                            v_yev[tau] = V.n
                    psfree[4] = (V, v_yev[0])
                    psfree[5] = (V, v_yev[1])
            v_yacc_done = V.n
            if dbg and half == 0:
                E("sync", lambda e: e.dma_start(dbgt["yach0"], yac[0][:]),
                  waits=[(V, v_yacc_done)], incs=[(SDB, 16)])
                E("sync", lambda e: e.dma_start(dbgt["hp"], hP[:]),
                  waits=[(V, v_yacc_done)], incs=[(SDB, 16)])
                dbg_h0w = (SDB, SDB.n)

            for tl in range(8):
                tt = half * 8 + tl
                sl = tl % 2
                if tl < 2:
                    if half == 0:
                        wx = [(V, a2c_lnb[NTT - 2 + tl])]
                    else:
                        wx = [(V, eh_out[(0, 6 + tl)])]
                else:
                    wx = [(V, eh_out[(half, tl - 2)])]
                E("sync",
                  lambda e, tt=tt, sl=sl: e.dma_start(xsl[sl][:], x1_d[tt]),
                  waits=wx, incs=[(SX[sl], 16)])
                sx_n = SX[sl].n
                wout = ([(SST, sst_last)] if sst_last else [])
                for hg in range(NHC // 4):
                    pslot = hg % 2
                    for q in range(4):
                        hc = hg * 4 + q
                        w = [(V, v_yacc_done)] + (ps_wait(("T", pslot)) if q == 0 else [])
                        E("tensor",
                          lambda e, tl=tl, hc=hc, q=q, pslot=pslot: e.transpose(
                              PT[pslot][:, q * 128:(q + 1) * 128],
                              yac[0][:, hc * 1024 + tl * 128: hc * 1024 + (tl + 1) * 128],
                              idb[:]),
                          waits=w, incs=([(M, 1)] if q == 3 else []))
                    mt = M.n
                    E("vector",
                      lambda e, hg=hg, sl=sl, pslot=pslot: e.tensor_tensor(
                          out_t[:, hg * 512:(hg + 1) * 512],
                          xsl[sl][:, hg * 512:(hg + 1) * 512],
                          PT[pslot][:, 0:512], ALU.add),
                      waits=[(M, mt), (SX[sl], sx_n)] + (wout if hg == 0 else []),
                      incs=[(V, 1)])
                    psfree[("T", pslot)] = (V, V.n)
                v_out = V.n
                eh_out[(half, tl)] = v_out
                E("sync",
                  lambda e, tt=tt: e.dma_start(out_d[tt * 128:(tt + 1) * 128, :], out_t[:]),
                  waits=[(V, v_out)], incs=[(SST, 16)])
                sst_last = SST.n
            m_endhalf = M.n

        if dbg:
            E("sync", lambda e: e.dma_start(dbgt["x1"], x1_d), waits=[(SCQ, scq_x1)],
              incs=[(SDB, 16)])
            E("sync", lambda e: e.dma_start(dbgt["l2t"], l2t_d),
              waits=[(ST2[0], scq_l2t[0]), (ST2[1], scq_l2t[1])], incs=[(SDB, 16)])
            E("sync", lambda e: e.dma_start(dbgt["yac"], yac[0][:]),
              waits=[(V, v_yacc_done)], incs=[(SDB, 16)])
        E("sync", lambda e: None, waits=[(SST, SST.n), (SDB, SDB.n)])
        sB.close()

        block = stk.enter_context(nc.Block())
        sch.build(block)

    if os.environ.get("KDBG"):
        import sys
        print("engine op counts:", sch.counts(), file=sys.stderr)
    return nc


_CACHE = {}


def _prep(W, b, gate_w, w1, w2, w3):
    W = np.asarray(W, dtype=np.float32)
    b = np.asarray(b, dtype=np.float32)
    gate_w = np.asarray(gate_w, dtype=np.float32)
    w1 = np.asarray(w1, dtype=np.float32)
    w2 = np.asarray(w2, dtype=np.float32)
    w3 = np.asarray(w3, dtype=np.float32)
    wp = np.stack([np.concatenate([W[:, i * 512:(i + 1) * 512],
                                   W[:, 2048 + i * 512: 2048 + (i + 1) * 512]], axis=1)
                   for i in range(4)])
    wp = np.ascontiguousarray(
        wp.reshape(4, 16, 128, 1024).transpose(0, 2, 1, 3).reshape(4, 128, 16 * 1024))
    bp = np.concatenate([np.concatenate([b[i * 512:(i + 1) * 512],
                                         b[2048 + i * 512: 2048 + (i + 1) * 512]])
                         for i in range(4)]).reshape(1, 4096)
    gwT = np.ascontiguousarray(
        gate_w.T.reshape(16, 128, 8).transpose(1, 0, 2).reshape(128, 128))

    def prep_w13(w):
        t = w.transpose(0, 2, 1)
        t = t.reshape(NE, 16, 128, 32, 128).transpose(0, 3, 2, 1, 4)
        return np.ascontiguousarray(t.reshape(NE, 32, 128, 2048))
    w1p = prep_w13(w1)
    w3p = prep_w13(w3)
    t = w2.transpose(0, 2, 1)
    t = t.reshape(NE, 2, 16, 128, 16, 128)
    t = t.transpose(0, 1, 4, 3, 2, 5)
    w2p = np.ascontiguousarray(t.reshape(NE, 2, NHC, 128, 2048))
    pos = np.arange(16384, dtype=np.float32)
    inv_freq = (1.0 / (10000.0 ** (np.arange(0, 1024, 2, dtype=np.float32) / 1024.0))
                ).astype(np.float32)
    freqs = (pos[:, None] * inv_freq[None, :]).astype(np.float32)
    cosf = np.cos(freqs).astype(np.float32)
    sinf = np.sin(freqs).astype(np.float32)
    return wp, bp, gwT, w1p, w2p, w3p, cosf, sinf


def kernel(L0_attn_X, L0_attn_W, L0_attn_b, L0_ffn_gate_w, L0_ffn_w1, L0_ffn_w2, L0_ffn_w3):
    X = np.ascontiguousarray(np.asarray(L0_attn_X, dtype=np.float32).reshape(16384, 2048))
    wp, bp, gwT, w1p, w2p, w3p, cosf, sinf = _prep(
        L0_attn_W, L0_attn_b, L0_ffn_gate_w, L0_ffn_w1, L0_ffn_w2, L0_ffn_w3)

    idb = np.eye(128).astype(ml_dtypes.bfloat16)
    idf = np.eye(128, dtype=np.float32)
    onc = np.ones((128, 1), dtype=np.float32)
    onr = np.ones((1, 128), dtype=np.float32)
    onb = np.ones((1, 128), dtype=ml_dtypes.bfloat16)
    oh = np.zeros((8, 8 * 128), dtype=np.float32)
    for e in range(8):
        oh[e, e * 128:(e + 1) * 128] = 1.0
    oh = oh.astype(ml_dtypes.bfloat16)

    if "nc" not in _CACHE:
        _CACHE["nc"] = build_nc()
    nc = _CACHE["nc"]

    in_maps = []
    for c in range(NCORES):
        rows = slice(c * T, (c + 1) * T)
        cs_c = np.ascontiguousarray(
            cosf[rows].reshape(NTT, 128, 2, 256).transpose(2, 0, 1, 3))
        sn_c = np.ascontiguousarray(
            sinf[rows].reshape(NTT, 128, 2, 256).transpose(2, 0, 1, 3))
        in_maps.append({
            "x0_in": np.ascontiguousarray(X[rows]), "wp_in": wp, "b_in": bp, "gw_in": gwT,
            "w1_in": w1p, "w3_in": w3p, "w2_in": w2p,
            "cs_in": cs_c, "sn_in": sn_c,
            "idb_in": idb, "idf_in": idf, "onc_in": onc, "onr_in": onr,
            "onb_in": onb, "oh_in": oh,
        })
    res = run_bass_kernel_spmd(nc, in_maps, core_ids=list(range(NCORES)))
    _CACHE["res"] = res
    out = np.concatenate([np.asarray(res.results[c]["out"]) for c in range(NCORES)], axis=0)
    return out.reshape(1, 16384, 2048).astype(np.float32)


# version 13
# speedup vs baseline: 1.0201x; 1.0015x over previous
import os
from contextlib import ExitStack

import numpy as np
import ml_dtypes
import concourse.bass as bass
import concourse.mybir as mybir
from concourse.bass_utils import run_bass_kernel_spmd

F32 = mybir.dt.float32
BF16 = mybir.dt.bfloat16
AF = mybir.ActivationFunctionType
ALU = mybir.AluOpType
AX = mybir.AxisListType

NCORES = 8
T = 2048
D = 2048
NE = 8
NTT = 16
NDC = 16
NHC = 16
NFL = 16
EPS = 1e-5

MAGIC = float(np.float32(1.5 * 2 ** 23))
INV2PI = float(np.float32(1.0 / (2.0 * np.pi)))
TWOPI = float(np.float32(2.0 * np.pi))
HALFPI = float(np.float32(np.pi / 2.0))
NEG = -1.0e30

ENGINES = ("sync", "scalar", "vector", "tensor", "gpsimd")


class Sem:
    __slots__ = ("h", "n")

    def __init__(self, h):
        self.h = h
        self.n = 0


class Sched:
    def __init__(self):
        self.ops = {e: [] for e in ENGINES}

    def emit(self, eng, fn, waits=(), incs=()):
        for s, v in waits:
            assert v <= s.n, f"deadlock: wait {v} > emitted {s.n} on {s.h.name}"
        lowered = (fn, [(s.h, v) for s, v in waits if v > 0], [])
        for s, k in incs:
            s.n += k
            lowered[2].append((s.h, k))
        self.ops[eng].append(lowered)

    def build(self, block):
        def make(name):
            ops = self.ops[name]

            def body(eng):
                for fn, waits, incs in ops:
                    for h, v in waits:
                        eng.wait_ge(h, v)
                    instr = fn(eng)
                    for h, k in incs:
                        if instr is None:
                            instr = eng.nop()
                        instr = instr.then_inc(h, k)
            return body

        for name in ENGINES:
            if self.ops[name]:
                getattr(block, name)(make(name))

    def counts(self):
        return {e: len(self.ops[e]) for e in ENGINES}


def build_nc():
    nc = bass.Bass("TRN2", target_bir_lowering=False)

    x0_in = nc.dram_tensor("x0_in", [T, D], F32, kind="ExternalInput").ap()
    wp_in = nc.dram_tensor("wp_in", [4, 128, 16 * 1024], F32, kind="ExternalInput").ap()
    b_in = nc.dram_tensor("b_in", [1, 4096], F32, kind="ExternalInput").ap()
    gw_in = nc.dram_tensor("gw_in", [128, 128], F32, kind="ExternalInput").ap()
    w1_in = nc.dram_tensor("w1_in", [NE, 32, 128, 2048], F32, kind="ExternalInput").ap()
    w3_in = nc.dram_tensor("w3_in", [NE, 32, 128, 2048], F32, kind="ExternalInput").ap()
    w2_in = nc.dram_tensor("w2_in", [NE, 2, NHC, 128, 2048], F32, kind="ExternalInput").ap()
    cs_in = nc.dram_tensor("cs_in", [2, NTT, 128, 256], F32, kind="ExternalInput").ap()
    sn_in = nc.dram_tensor("sn_in", [2, NTT, 128, 256], F32, kind="ExternalInput").ap()
    idb_in = nc.dram_tensor("idb_in", [128, 128], BF16, kind="ExternalInput").ap()
    idf_in = nc.dram_tensor("idf_in", [128, 128], F32, kind="ExternalInput").ap()
    onc_in = nc.dram_tensor("onc_in", [128, 1], F32, kind="ExternalInput").ap()
    onr_in = nc.dram_tensor("onr_in", [1, 128], F32, kind="ExternalInput").ap()
    onb_in = nc.dram_tensor("onb_in", [1, 128], BF16, kind="ExternalInput").ap()
    oh_in = nc.dram_tensor("oh_in", [8, 8 * 128], BF16, kind="ExternalInput").ap()
    out_d = nc.dram_tensor("out", [T, D], F32, kind="ExternalOutput").ap()
    x1_d = nc.dram_tensor("x1_d", [NTT, 128, D], F32, kind="Internal").ap()
    l2t_d = nc.dram_tensor("l2t_d", [4, NTT, 128, 512], BF16, kind="Internal").ap()
    ar_src_h = [nc.dram_tensor(f"ar_src{i}", [128, 4], F32, kind="Internal") for i in range(4)]
    ar_dst_h = [nc.dram_tensor(f"ar_dst{i}", [128, 4], F32, kind="Internal") for i in range(4)]
    ar_src_flat = [h.ap().rearrange("p g -> (p g)").unsqueeze(0) for h in ar_src_h]
    ar_dst_flat = [h.ap().rearrange("p g -> (p g)").unsqueeze(0) for h in ar_dst_h]

    dbg = bool(os.environ.get("KDBG"))
    dbgt = {}
    if dbg:
        dbgt["afin"] = nc.dram_tensor("dbg_afin", [1, 2048], F32, kind="ExternalOutput").ap()
        dbgt["stats"] = nc.dram_tensor("dbg_stats", [128, 192], F32, kind="ExternalOutput").ap()
        dbgt["scores"] = nc.dram_tensor("dbg_scores", [128, 128], F32, kind="ExternalOutput").ap()
        dbgt["coef"] = nc.dram_tensor("dbg_coef", [128, 128], F32, kind="ExternalOutput").ap()
        dbgt["coefT"] = nc.dram_tensor("dbg_coefT", [8, 2048], BF16, kind="ExternalOutput").ap()
        dbgt["ln1T"] = nc.dram_tensor("dbg_ln1T", [128, NDC * T], BF16, kind="ExternalOutput").ap()
        dbgt["x1"] = nc.dram_tensor("dbg_x1", [NTT, 128, D], F32, kind="ExternalOutput").ap()
        dbgt["l2t"] = nc.dram_tensor("dbg_l2t", [4, NTT, 128, 512], BF16,
                                     kind="ExternalOutput").ap()
        dbgt["yac"] = nc.dram_tensor("dbg_yac", [128, NHC * 1024], BF16,
                                     kind="ExternalOutput").ap()
        dbgt["yach0"] = nc.dram_tensor("dbg_yach0", [128, NHC * 1024], BF16,
                                       kind="ExternalOutput").ap()
        dbgt["cb0"] = nc.dram_tensor("dbg_cb0", [128, 1024], F32, kind="ExternalOutput").ap()
        dbgt["hp"] = nc.dram_tensor("dbg_hp", [128, NFL * 1024], BF16,
                                    kind="ExternalOutput").ap()

    sch = Sched()
    E = sch.emit

    with ExitStack() as stk:
        en = stk.enter_context

        def sem(name):
            return Sem(en(nc.semaphore(name)))

        SX = [sem("sx0"), sem("sx1")]
        STB = [sem("stb0"), sem("stb1")]
        SWB = [sem("swb0"), sem("swb1")]
        SW1 = [sem("sw1a"), sem("sw1b")]
        SW3 = [sem("sw3a"), sem("sw3b")]
        SW2 = [sem("sw2a"), sem("sw2b")]
        ST2 = [sem("st2a"), sem("st2b")]
        SLN = sem("sln")
        SCQ = sem("scq")
        SST = sem("sst")
        SARL = sem("sarl")
        SCS = sem("scs")
        SCG = sem("scg")
        CO = sem("co")
        M = sem("m")
        V = sem("v")
        A_ = sem("a")

        P = [en(nc.psum_tensor(f"ps{i}", [128, 512], F32)) for i in range(6)]
        PT = [en(nc.psum_tensor("pt0", [128, 1024], BF16)),
              en(nc.psum_tensor("pt1", [128, 1024], BF16))]
        psA = P[4][0:1, 0:512]
        psS = P[5][:, 0:8]
        psC = P[5][0:8, 128:256]
        psfree = {}

        def ps_wait(key):
            w = psfree.get(key)
            return [w] if w else []

        stats = en(nc.sbuf_tensor("stats", [128, 16 * 12], F32))

        def stc(q, tt=None):
            if tt is None:
                return stats[:, q * 16:(q + 1) * 16]
            return stats[:, q * 16 + tt:q * 16 + tt + 1]
        scores_sb = en(nc.sbuf_tensor("scores", [128, 128], F32))
        coefT = en(nc.sbuf_tensor("coefT", [8, 2048], BF16))
        xsl = [en(nc.sbuf_tensor("xs0", [128, D], F32)),
               en(nc.sbuf_tensor("xs1", [128, D], F32))]
        idb = en(nc.sbuf_tensor("idb", [128, 128], BF16))
        oh = en(nc.sbuf_tensor("oh", [8, 8 * 128], BF16))
        cpi2 = en(nc.sbuf_tensor("cpi2", [128, 1], F32))
        ceps = en(nc.sbuf_tensor("ceps", [128, 1], F32))
        cm05 = en(nc.sbuf_tensor("cm05", [128, 1], F32))
        cneg1 = en(nc.sbuf_tensor("cneg1", [128, 1], F32))

        sA = ExitStack()
        enA = sA.enter_context
        ln1T = enA(nc.sbuf_tensor("ln1T", [128, NDC * T], BF16))
        WbT = enA(nc.sbuf_tensor("Wb", [128, 2 * NDC * 1024], BF16))
        Wbs = [WbT[:, 0:NDC * 1024], WbT[:, NDC * 1024:2 * NDC * 1024]]
        ATMP = enA(nc.sbuf_tensor("ATMP", [128, 4096], F32))
        KH = ATMP[:, 0:512]
        G1 = ATMP[:, 512:1024]
        G2 = ATMP[:, 1024:1536]
        RK = ATMP[:, 1536:2048]
        RB = ATMP[:, 2048:2560]
        CKb = ATMP[:, 2560:3072]
        CVb = ATMP[:, 3072:3584]
        PSb = ATMP[:, 3584:4096]
        PSbs = [PSb, RB]
        psb_free = {}
        A_sb = enA(nc.sbuf_tensor("A_sb", [1, 2048], F32))
        A_fin = ATMP[0:1, 0:2048]
        A_bc = enA(nc.sbuf_tensor("A_bc", [128, 2048], F32))
        tabs = enA(nc.sbuf_tensor("tabs", [128, 2 * 256], F32))

        def tab(kind, slot):
            return tabs[:, kind * 256:(kind + 1) * 256]
        ln_f32 = enA(nc.sbuf_tensor("ln_f32", [128, D], F32))
        u_buf = ATMP[:, 0:2048]
        ln_bf = enA(nc.sbuf_tensor("ln_bf", [128, D], BF16))
        b_sb = enA(nc.sbuf_tensor("b_sb", [1, 4096], BF16))
        gwT = enA(nc.sbuf_tensor("gwT", [128, 128], BF16))
        onc = enA(nc.sbuf_tensor("onc", [128, 1], F32))
        onr = enA(nc.sbuf_tensor("onr", [1, 128], F32))
        onb = enA(nc.sbuf_tensor("onb", [1, 128], BF16))
        t2t = [enA(nc.sbuf_tensor("t2t0", [128, 256], BF16)),
               enA(nc.sbuf_tensor("t2t1", [128, 256], BF16))]
        gT1 = ATMP[:, 0:128]
        gE1 = ATMP[:, 128:256]
        gAB = ATMP[:, 256:384]
        gU1 = ATMP[:, 384:512]
        gS2 = ATMP[:, 512:640]
        gTO = ATMP[:, 640:768]
        gE2 = ATMP[:, 768:896]
        gCM = ATMP[:, 896:1024]
        gst = ATMP[:, 1024:1216]

        def gss(q):
            return gst[:, q * 16:(q + 1) * 16]

        E("sync", lambda e: e.dma_start(idb[:], idb_in), incs=[(SCS, 16)])
        E("sync", lambda e: e.dma_start(onc[:], onc_in), incs=[(SCS, 16)])
        E("sync", lambda e: e.dma_start(onr[:], onr_in), incs=[(SCS, 16)])
        E("sync", lambda e: e.dma_start(onb[:], onb_in), incs=[(SCS, 16)])
        E("sync", lambda e: e.dma_start(oh[:], oh_in), incs=[(SCS, 16)])
        consts_s = SCS.n
        E("gpsimd", lambda e: e.dma_start(gwT[:], gw_in), incs=[(SCG, 16)])
        E("gpsimd", lambda e: e.dma_start(b_sb[:], b_in), incs=[(SCG, 16)])
        consts_g = SCG.n
        E("vector", lambda e: e.memset(cpi2[:], HALFPI), incs=[(V, 1)])
        E("vector", lambda e: e.memset(ceps[:], EPS), incs=[(V, 1)])
        E("vector", lambda e: e.memset(cm05[:], -0.5), incs=[(V, 1)])
        E("vector", lambda e: e.memset(cneg1[:], -1.0), incs=[(V, 1)])

        p2_lnb = {}
        p_rste = {}
        m_lastT = 0
        psq = {}
        for tt in range(NTT):
            sl = tt % 2
            w = []
            if tt >= 2:
                w = [(V, p2_lnb[tt - 2])]
            E("sync",
              lambda e, tt=tt, sl=sl: e.dma_start(xsl[sl][:], x0_in[tt * 128:(tt + 1) * 128, :]),
              waits=w, incs=[(SX[sl], 16)])
            lw = [(SX[sl], SX[sl].n)]
            E("scalar",
              lambda e, tt=tt, sl=sl: e.activation(ln_f32[:], xsl[sl][:], AF.Copy,
                                                   accum_out=stc(0, tt)),
              waits=lw + ([(A_, psq[tt - 1])] if tt > 0 else []), incs=[(A_, 1)])
            E("scalar",
              lambda e, tt=tt, sl=sl: e.activation(ln_f32[:], xsl[sl][:], AF.Square,
                                                   accum_out=stc(1, tt)),
              incs=[(A_, 1)])
            a_sq = A_.n
            psq[tt] = a_sq
            E("vector", lambda e, tt=tt: e.tensor_scalar(stc(2, tt), stc(0, tt), 1.0 / D,
                                                         None, ALU.mult),
              waits=[(A_, a_sq), (V, V.n)], incs=[(V, 1)])
            E("vector", lambda e, tt=tt: e.tensor_scalar(stc(9, tt), stc(1, tt), 1.0 / D,
                                                         None, ALU.mult),
              waits=[(V, V.n)], incs=[(V, 1)])
            E("vector", lambda e, tt=tt: e.tensor_tensor(stc(10, tt), stc(2, tt), stc(2, tt),
                                                         ALU.mult),
              waits=[(V, V.n)], incs=[(V, 1)])
            E("vector", lambda e, tt=tt: e.tensor_tensor(stc(11, tt), stc(9, tt), stc(10, tt),
                                                         ALU.subtract),
              waits=[(V, V.n)], incs=[(V, 1)])
            v_var = V.n
            E("scalar", lambda e, tt=tt: e.activation(stc(9, tt), stc(11, tt), AF.Ln,
                                                      bias=ceps[:]),
              waits=[(V, v_var)], incs=[(A_, 1)])
            E("scalar", lambda e, tt=tt: e.activation(stc(3, tt), stc(9, tt), AF.Exp,
                                                      scale=cm05[:]),
              incs=[(A_, 1)])
            a_rstd1 = A_.n
            p_rste[tt] = a_rstd1
            E("vector",
              lambda e, tt=tt, sl=sl: e.tensor_scalar(ln_bf[:], xsl[sl][:], stc(2, tt),
                                                      stc(3, tt), ALU.subtract, ALU.mult),
              waits=[(A_, a_rstd1), (V, V.n), (M, m_lastT)], incs=[(V, 1)])
            v_lnb = V.n
            p2_lnb[tt] = v_lnb
            for dcg in range(NDC // 4):
                pslot = dcg % 2
                for q in range(4):
                    dc = dcg * 4 + q
                    w = [(V, v_lnb), (SCS, consts_s)] + (ps_wait(("T", pslot)) if q == 0 else [])
                    E("tensor",
                      lambda e, dc=dc, q=q, pslot=pslot: e.transpose(
                          PT[pslot][:, q * 128:(q + 1) * 128],
                          ln_bf[:, dc * 128:(dc + 1) * 128], idb[:]),
                      waits=w, incs=([(M, 1)] if q == 3 else []))
                mt = M.n
                E("vector",
                  lambda e, tt=tt, dcg=dcg, pslot=pslot: e.tensor_copy(
                      ln1T.ap().rearrange("p (d t) -> p d t", t=T)[
                          :, dcg * 4:(dcg + 1) * 4, tt * 128:(tt + 1) * 128],
                      PT[pslot][:, 0:512].rearrange("p (g c) -> p g c", c=128)),
                  waits=[(M, mt)], incs=[(V, 1)])
                psfree[("T", pslot)] = (V, V.n)
            m_lastT = M.n
        v_p2_done = V.n
        v_m1s = V.n
        a_rstd1 = A_.n

        m_wb_free = {}
        v_Acp = 0
        v_tabfree = {}
        a_sin_prev = 0
        m_A_prev = 0
        v_P_prev = 0
        pend_A = None
        arw = {}
        for np_ in range(4):
            rot = np_ < 2
            wsl = np_ % 2
            Wb = Wbs[wsl]
            E("gpsimd",
              lambda e, np_=np_, wsl=wsl: e.dma_start(Wbs[wsl][:], wp_in[np_]),
              waits=([(M, m_wb_free[wsl])] if wsl in m_wb_free else []) + [(SCG, consts_g)],
              incs=[(SWB[wsl], 16)])
            swb_n = SWB[wsl].n
            for tt in range(NTT):
                pr = tt % 2
                psK, psV = P[2 * pr], P[2 * pr + 1]
                kK, kV = 2 * pr, 2 * pr + 1
                if rot:
                    tsl = 0
                    wtb = ([v_tabfree["last"]] if "last" in v_tabfree else [])
                    E("sync",
                      lambda e, np_=np_, tt=tt: e.dma_start(tab(0, 0), cs_in[np_, tt]),
                      waits=wtb, incs=[(STB[0], 16)])
                    E("sync",
                      lambda e, np_=np_, tt=tt: e.dma_start(tab(1, 0), sn_in[np_, tt]),
                      incs=[(STB[0], 16)])
                    stb_n = STB[0].n
                wk = [(SWB[wsl], swb_n)] + ps_wait(kK)
                if np_ == 0 and tt == 0:
                    wk.append((V, v_p2_done))
                for dc in range(NDC):
                    E("tensor",
                      lambda e, tt=tt, dc=dc, psK=psK, Wb=Wb: e.matmul(
                          psK[:], ln1T[:, dc * T + tt * 128: dc * T + (tt + 1) * 128],
                          Wb[:, dc * 1024: dc * 1024 + 512], start=(dc == 0), stop=False),
                      waits=(wk if dc == 0 else []))
                E("tensor",
                  lambda e, np_=np_, psK=psK: e.matmul(
                      psK[:], onb[:], b_sb[0:1, np_ * 1024: np_ * 1024 + 512],
                      start=False, stop=True),
                  incs=[(M, 1)])
                m_k = M.n
                wv = ps_wait(kV)
                for dc in range(NDC):
                    E("tensor",
                      lambda e, tt=tt, dc=dc, psV=psV, Wb=Wb: e.matmul(
                          psV[:], ln1T[:, dc * T + tt * 128: dc * T + (tt + 1) * 128],
                          Wb[:, dc * 1024 + 512: dc * 1024 + 1024], start=(dc == 0), stop=False),
                      waits=(wv if dc == 0 else []))
                E("tensor",
                  lambda e, np_=np_, psV=psV: e.matmul(
                      psV[:], onb[:], b_sb[0:1, np_ * 1024 + 512: np_ * 1024 + 1024],
                      start=False, stop=True),
                  incs=[(M, 1)])
                m_v = M.n

                def rotary(ps, m_ps, stb_val, tslot):
                    ev, od = ps[:, 0:512:2], ps[:, 1:512:2]
                    ct, st = tab(0, tslot), tab(1, tslot)
                    E("vector", lambda e, ev=ev, ct=ct: e.tensor_tensor(
                        G1[:, 0:256], ev, ct, ALU.mult),
                      waits=[(M, m_ps), (STB[tslot], stb_val), (V, V.n)], incs=[(V, 1)])
                    E("vector", lambda e, od=od, st=st: e.tensor_tensor(
                        G2[:, 0:256], od, st, ALU.mult), waits=[(V, V.n)], incs=[(V, 1)])
                    E("vector", lambda e: e.tensor_tensor(
                        KH[:, 0:512:2], G1[:, 0:256], G2[:, 0:256], ALU.subtract),
                      waits=[(V, V.n)], incs=[(V, 1)])
                    E("vector", lambda e, od=od, ct=ct: e.tensor_tensor(
                        G1[:, 0:256], od, ct, ALU.mult), waits=[(V, V.n)], incs=[(V, 1)])
                    E("vector", lambda e, ev=ev, st=st: e.tensor_tensor(
                        G2[:, 0:256], ev, st, ALU.mult), waits=[(V, V.n)], incs=[(V, 1)])
                    E("vector", lambda e: e.tensor_tensor(
                        KH[:, 1:512:2], G1[:, 0:256], G2[:, 0:256], ALU.add),
                      waits=[(V, V.n)], incs=[(V, 1)])
                    return V.n

                def cos_chain(src_ap, out_sb, first_w, rk_w, sin_w):
                    E("vector", lambda e, src_ap=src_ap: e.tensor_scalar(
                        G1[:], src_ap, INV2PI, 0.25, ALU.mult, ALU.add),
                      waits=list(first_w) + [(V, V.n)], incs=[(V, 1)])
                    E("vector", lambda e: e.tensor_scalar(
                        G2[:], G1[:], MAGIC, MAGIC, ALU.add, ALU.subtract), incs=[(V, 1)])
                    E("vector", lambda e: e.tensor_scalar(
                        G1[:], G2[:], TWOPI, None, ALU.mult), incs=[(V, 1)])
                    E("vector", lambda e, src_ap=src_ap: e.tensor_tensor(
                        RK[:], src_ap, G1[:], ALU.subtract), waits=rk_w, incs=[(V, 1)])
                    vr = V.n
                    E("scalar", lambda e, out_sb=out_sb: e.activation(
                        out_sb[:], RK[:], AF.Sin, bias=cpi2[:]),
                      waits=[(V, vr)] + sin_w, incs=[(A_, 1)])
                    return A_.n

                rk_guard = ([(A_, a_sin_prev)] if a_sin_prev else []) + \
                           ([(M, m_A_prev)] if m_A_prev else [])
                sin_guard = [(V, v_P_prev)] if v_P_prev else []
                if rot:
                    rotary(psK, m_k, stb_n, tsl)
                    v_k_done = V.n
                    aK = cos_chain(KH[:], CKb, [], rk_guard, sin_guard)
                    rotary(psV, m_v, stb_n, tsl)
                    v_v_done = V.n
                    aV = cos_chain(KH[:], CVb, [], [(A_, aK)], sin_guard)
                    v_tabfree["last"] = (V, v_v_done)
                else:
                    aK = cos_chain(psK[:], CKb, [(M, m_k)], rk_guard, sin_guard)
                    v_k_done = V.n
                    aV = cos_chain(psV[:], CVb, [(M, m_v)], [(A_, aK)], sin_guard)
                    v_v_done = V.n
                psfree[kK] = (V, v_k_done)
                psfree[kV] = (V, v_v_done)
                psb = PSbs[tt % 2]
                E("vector", lambda e, psb=psb: e.tensor_tensor(psb[:], CKb[:], CVb[:], ALU.mult),
                  waits=[(A_, aK), (A_, aV)] +
                        ([(M, psb_free[tt % 2])] if (tt % 2) in psb_free else []),
                  incs=[(V, 1)])
                v_P = V.n
                if pend_A is not None:
                    ptt, pv, pfirst, ppsb = pend_A
                    wA = [(V, pv)]
                    if pfirst:
                        wA += [(SCS, consts_s)] + ([(V, v_Acp)] if v_Acp else [])
                    E("tensor",
                      lambda e, ppsb=ppsb, pfirst=pfirst: e.matmul(
                          psA, onc[:], ppsb[:], start=pfirst, stop=False),
                      waits=wA, incs=[(M, 1)])
                    psb_free[ptt % 2] = M.n
                pend_A = (tt, v_P, (tt == 0), psb)
                a_sin_prev = aV
                v_P_prev = v_P
            ptt, pv, pfirst, ppsb = pend_A
            E("tensor",
              lambda e, ppsb=ppsb, pfirst=pfirst: e.matmul(
                  psA, onc[:], ppsb[:], start=pfirst, stop=True),
              waits=[(V, pv)] + ([(SCS, consts_s)] if pfirst else []), incs=[(M, 1)])
            psb_free[ptt % 2] = M.n
            pend_A = None
            m_A_prev = M.n
            m_endKV = M.n
            E("vector",
              lambda e, np_=np_: e.tensor_copy(A_sb[0:1, np_ * 512:(np_ + 1) * 512], psA),
              waits=[(M, m_A_prev)], incs=[(V, 1)])
            v_Acp = V.n
            m_wb_free[wsl] = M.n
            E("scalar",
              lambda e, np_=np_: e.dma_start(ar_src_flat[np_],
                                             A_sb[0:1, np_ * 512:(np_ + 1) * 512]),
              waits=[(V, v_Acp)], incs=[(SCQ, 16)])
            arw[np_] = SCQ.n
        psfree[4] = (V, v_Acp)
        for np_ in range(4):
            E("gpsimd", lambda e, np_=np_: e.collective_compute(
                "AllReduce", ALU.add, replica_groups=[list(range(NCORES))],
                ins=[ar_src_h[np_].ap().opt()], outs=[ar_dst_h[np_].ap().opt()]),
              waits=[(SCQ, arw[np_])], incs=[(CO, 1)])

        pre_sx = {}
        for tt in range(2):
            E("sync",
              lambda e, tt=tt: e.dma_start(xsl[tt][:], x0_in[tt * 128:(tt + 1) * 128, :]),
              waits=[(V, p2_lnb[NTT - 2 + tt])], incs=[(SX[tt], 16)])
            pre_sx[tt] = SX[tt].n
        for np_ in range(4):
            E("sync",
              lambda e, np_=np_: e.dma_start(A_sb[0:1, np_ * 512:(np_ + 1) * 512],
                                             ar_dst_flat[np_]),
              waits=[(CO, 4)], incs=[(SARL, 16)])
        E("vector", lambda e: e.tensor_scalar(A_fin[:], A_sb[:], 0.5, 0.5, ALU.mult, ALU.add),
          waits=[(SARL, SARL.n)], incs=[(V, 1)])
        v_afin = V.n
        for i in range(4):
            E("tensor",
              lambda e, i=i: e.matmul(P[i][:], onr[:], A_fin[0:1, i * 512:(i + 1) * 512],
                                      start=True, stop=True),
              waits=[(V, v_afin)] + ps_wait(i), incs=[(M, 1)])
            mi = M.n
            E("vector",
              lambda e, i=i: e.tensor_copy(A_bc[:, i * 512:(i + 1) * 512], P[i][:]),
              waits=[(M, mi)], incs=[(V, 1)])
            psfree[i] = (V, V.n)
        v_abc = V.n

        a2 = {}
        x1w = {}
        a2sq = {}
        a2c_lnb = {}
        m_lastT2 = m_lastT
        for tt in range(NTT):
            sl = tt % 2
            if tt < 2:
                lw = (SX[sl], pre_sx[tt])
            else:
                E("sync",
                  lambda e, tt=tt, sl=sl: e.dma_start(xsl[sl][:],
                                                      x0_in[tt * 128:(tt + 1) * 128, :]),
                  waits=[(V, a2[tt - 2])], incs=[(SX[sl], 16)])
                lw = (SX[sl], SX[sl].n)
            lnw = [lw, (V, v_abc), (V, v_m1s), (A_, a_rstd1)]
            if tt > 0:
                lnw += [(A_, a2sq[tt - 1]), (SCQ, x1w[tt - 1])]
            E("vector",
              lambda e, tt=tt, sl=sl: e.tensor_scalar(ln_f32[:], xsl[sl][:], stc(2, tt),
                                                      stc(3, tt), ALU.subtract, ALU.mult),
              waits=lnw, incs=[(V, 1)])
            E("vector", lambda e: e.tensor_tensor(u_buf[:], A_bc[:], ln_f32[:], ALU.mult),
              incs=[(V, 1)])
            E("vector", lambda e, sl=sl: e.tensor_tensor(ln_f32[:], xsl[sl][:], u_buf[:],
                                                         ALU.add), incs=[(V, 1)])
            v_x1 = V.n
            a2[tt] = v_x1
            E("scalar", lambda e, tt=tt: e.dma_start(x1_d[tt], ln_f32[:]),
              waits=[(V, v_x1)], incs=[(SCQ, 16)])
            x1w[tt] = SCQ.n
            E("scalar", lambda e, tt=tt: e.activation(u_buf[:], ln_f32[:], AF.Copy,
                                                      accum_out=stc(5, tt)),
              waits=[(V, v_x1)], incs=[(A_, 1)])
            E("scalar", lambda e, tt=tt: e.activation(u_buf[:], ln_f32[:], AF.Square,
                                                      accum_out=stc(6, tt)),
              incs=[(A_, 1)])
            a_sq = A_.n
            a2sq[tt] = a_sq
            E("vector", lambda e, tt=tt: e.tensor_scalar(stc(7, tt), stc(5, tt), 1.0 / D,
                                                         None, ALU.mult),
              waits=[(A_, a_sq), (V, V.n)], incs=[(V, 1)])
            E("vector", lambda e, tt=tt: e.tensor_scalar(stc(9, tt), stc(6, tt), 1.0 / D,
                                                         None, ALU.mult),
              waits=[(V, V.n)], incs=[(V, 1)])
            E("vector", lambda e, tt=tt: e.tensor_tensor(stc(10, tt), stc(7, tt), stc(7, tt),
                                                         ALU.mult),
              waits=[(V, V.n)], incs=[(V, 1)])
            E("vector", lambda e, tt=tt: e.tensor_tensor(stc(11, tt), stc(9, tt), stc(10, tt),
                                                         ALU.subtract),
              waits=[(V, V.n)], incs=[(V, 1)])
            v_var = V.n
            E("scalar", lambda e, tt=tt: e.activation(stc(9, tt), stc(11, tt), AF.Ln,
                                                      bias=ceps[:]),
              waits=[(V, v_var)], incs=[(A_, 1)])
            E("scalar", lambda e, tt=tt: e.activation(stc(8, tt), stc(9, tt), AF.Exp,
                                                      scale=cm05[:]),
              incs=[(A_, 1)])
            a_rstd2 = A_.n
            E("vector",
              lambda e, tt=tt: e.tensor_scalar(ln_bf[:], ln_f32[:], stc(7, tt),
                                               stc(8, tt), ALU.subtract, ALU.mult),
              waits=[(A_, a_rstd2), (V, V.n), (M, m_lastT2)], incs=[(V, 1)])
            v_lnb = V.n
            a2c_lnb[tt] = v_lnb
            for dcg in range(NDC // 2):
                pslot = dcg % 2
                for q in range(2):
                    dc = dcg * 2 + q
                    w = [(V, v_lnb)] + (ps_wait(("T", pslot)) if q == 0 else [])
                    E("tensor",
                      lambda e, dc=dc, q=q, pslot=pslot: e.transpose(
                          PT[pslot][:, q * 128:(q + 1) * 128],
                          ln_bf[:, dc * 128:(dc + 1) * 128], idb[:]),
                      waits=w, incs=([(M, 1)] if q == 1 else []))
                mt = M.n
                wv = [(M, mt)] + ps_wait(("t2", pslot))
                if ST2[pslot].n > 0:
                    wv.append((ST2[pslot], ST2[pslot].n))
                E("vector",
                  lambda e, pslot=pslot: e.tensor_copy(t2t[pslot][:], PT[pslot][:, 0:256]),
                  waits=wv, incs=[(V, 1)])
                psfree[("T", pslot)] = (V, V.n)
                vev = V.n
                E("scalar",
                  lambda e, dcg=dcg, tt=tt, pslot=pslot: e.dma_start(
                      l2t_d[dcg // 2, tt, :, (dcg % 2) * 256:(dcg % 2 + 1) * 256],
                      t2t[pslot][:]),
                  waits=[(V, vev)], incs=[(ST2[pslot], 16)])
                for q in range(2):
                    dc = dcg * 2 + q
                    wmm = [(V, vev)]
                    if dc == 0:
                        wmm += [(SCG, consts_g)] + ps_wait(5)
                    E("tensor",
                      lambda e, dc=dc, q=q, pslot=pslot: e.matmul(
                          psS, t2t[pslot][:, q * 128:(q + 1) * 128],
                          gwT[:, dc * 8:(dc + 1) * 8],
                          start=(dc == 0), stop=(dc == NDC - 1)),
                      waits=(wmm if q == 0 else []),
                      incs=([(M, 1)] if dc == NDC - 1 else []))
                psfree[("t2", pslot)] = (M, M.n)
            m_lastT2 = M.n
            E("vector",
              lambda e, tt=tt: e.tensor_copy(scores_sb[:, tt * 8:(tt + 1) * 8], psS),
              waits=[(M, m_lastT2)], incs=[(V, 1)])
            psfree[5] = (V, V.n)
            if tt == 7:
                scq_l2t_half0 = [ST2[0].n, ST2[1].n]
        scq_x1 = SCQ.n
        scq_l2t = [ST2[0].n, ST2[1].n]
        scq_l2t_h0 = list(scq_l2t_half0)

        def g3(t):
            a = t if isinstance(t, bass.AP) else t.ap()
            return a.rearrange("p (g k) -> p g k", k=8)

        def bc16(t16):
            return t16.unsqueeze(2).broadcast_to([128, 16, 8])

        v_scores = V.n

        def gv(fn, waits=()):
            E("vector", fn, waits=[(V, V.n)] + list(waits), incs=[(V, 1)])
            return V.n

        def ga(fn, wv):
            E("scalar", fn, waits=[(V, wv)], incs=[(A_, 1)])
            return A_.n

        def gw(fn, wa):
            E("vector", fn, waits=[(A_, wa), (V, V.n)], incs=[(V, 1)])
            return V.n

        BIG = 1.0e30
        v_nv1 = gv(lambda e: e.tensor_reduce(gss(0), g3(scores_sb), AX.X, ALU.max, negate=True))
        gv(lambda e: e.tensor_tensor(g3(gT1), g3(scores_sb), bc16(gss(0)), ALU.add),
           waits=[(V, v_nv1)])
        gv(lambda e: e.tensor_scalar(gS2[:], gT1[:], -BIG, None, ALU.mult))
        gv(lambda e: e.tensor_scalar(gCM[:], gS2[:], 1.0, None, ALU.min))
        gv(lambda e: e.tensor_scalar(gE1[:], gCM[:], -1.0, 1.0, ALU.mult, ALU.add))
        ax = ga(lambda e: e.activation(gAB[:], scores_sb[:], AF.Abs), v_scores)
        v_nm1 = gw(lambda e: e.tensor_scalar(gss(1), gss(0), -1.0, None, ALU.mult), ax)
        gv(lambda e: e.tensor_tensor(g3(gU1), g3(gAB), bc16(gss(1)), ALU.max),
           waits=[(V, v_nm1)])
        gv(lambda e: e.tensor_scalar(gCM[:], gU1[:], 0.02, None, ALU.mult))
        gv(lambda e: e.tensor_tensor(gS2[:], gT1[:], gCM[:], ALU.add))
        gv(lambda e: e.tensor_scalar(gCM[:], gS2[:], -BIG, None, ALU.mult))
        gv(lambda e: e.tensor_scalar(gU1[:], gCM[:], 1.0, None, ALU.min))
        gv(lambda e: e.tensor_scalar(gCM[:], gU1[:], 0.0, None, ALU.max))
        gv(lambda e: e.tensor_scalar(gS2[:], gCM[:], NEG, None, ALU.mult))
        v_sm1 = gv(lambda e: e.tensor_tensor(gU1[:], gT1[:], gS2[:], ALU.add))
        a_e1 = ga(lambda e: e.activation(gTO[:], gU1[:], AF.Exp), v_sm1)
        v_d1 = gw(lambda e: e.tensor_reduce(gss(2), g3(gTO), AX.X, ALU.add), a_e1)
        a_l1 = ga(lambda e: e.activation(gss(9), gss(2), AF.Ln), v_d1)
        E("scalar", lambda e: e.activation(gss(3), gss(9), AF.Exp, scale=cneg1[:]),
          incs=[(A_, 1)])
        a_r1 = A_.n
        gv(lambda e: e.tensor_tensor(gU1[:], gTO[:], gE1[:], ALU.mult))
        gv(lambda e: e.tensor_reduce(gss(4), g3(gU1), AX.X, ALU.add))
        v_m1v = gw(lambda e: e.tensor_tensor(gss(5), gss(4), gss(3), ALU.mult), a_r1)
        gv(lambda e: e.tensor_scalar(gS2[:], gE1[:], NEG, None, ALU.mult))
        gv(lambda e: e.tensor_tensor(gU1[:], scores_sb[:], gS2[:], ALU.add))
        v_nv2 = gv(lambda e: e.tensor_reduce(gss(6), g3(gU1), AX.X, ALU.max, negate=True))
        gv(lambda e: e.tensor_tensor(g3(gTO), g3(scores_sb), bc16(gss(6)), ALU.add),
           waits=[(V, v_nv2)])
        gv(lambda e: e.tensor_tensor(g3(gS2), g3(gU1), bc16(gss(6)), ALU.add))
        gv(lambda e: e.tensor_scalar(gCM[:], gS2[:], -BIG, None, ALU.mult))
        gv(lambda e: e.tensor_scalar(gU1[:], gCM[:], 1.0, None, ALU.min))
        gv(lambda e: e.tensor_scalar(gE2[:], gU1[:], -1.0, 1.0, ALU.mult, ALU.add))
        v_nm2 = gv(lambda e: e.tensor_scalar(gss(7), gss(6), -1.0, None, ALU.mult))
        gv(lambda e: e.tensor_tensor(g3(gS2), g3(gAB), bc16(gss(7)), ALU.max),
           waits=[(V, v_nm2)])
        gv(lambda e: e.tensor_scalar(gU1[:], gS2[:], 0.02, None, ALU.mult))
        gv(lambda e: e.tensor_tensor(gS2[:], gTO[:], gU1[:], ALU.add))
        gv(lambda e: e.tensor_scalar(gCM[:], gS2[:], -BIG, None, ALU.mult))
        gv(lambda e: e.tensor_scalar(gU1[:], gCM[:], 1.0, None, ALU.min))
        gv(lambda e: e.tensor_scalar(gCM[:], gU1[:], 0.0, None, ALU.max))
        gv(lambda e: e.tensor_scalar(gU1[:], gCM[:], NEG, None, ALU.mult))
        gv(lambda e: e.tensor_scalar(gCM[:], gE1[:], NEG, None, ALU.mult))
        gv(lambda e: e.tensor_tensor(gS2[:], gU1[:], gCM[:], ALU.add))
        v_sm2 = gv(lambda e: e.tensor_tensor(gU1[:], gTO[:], gS2[:], ALU.add))
        a_e2 = ga(lambda e: e.activation(gCM[:], gU1[:], AF.Exp), v_sm2)
        v_d2 = gw(lambda e: e.tensor_reduce(gss(8), g3(gCM), AX.X, ALU.add), a_e2)
        a_l2 = ga(lambda e: e.activation(gss(10), gss(8), AF.Ln), v_d2)
        E("scalar", lambda e: e.activation(gss(9), gss(10), AF.Exp, scale=cneg1[:]),
          incs=[(A_, 1)])
        a_r2 = A_.n
        gv(lambda e: e.tensor_tensor(gU1[:], gCM[:], gE2[:], ALU.mult))
        gv(lambda e: e.tensor_reduce(gss(10), g3(gU1), AX.X, ALU.add))
        v_m2v = gw(lambda e: e.tensor_tensor(gss(11), gss(10), gss(9), ALU.mult), a_r2)
        gv(lambda e: e.tensor_tensor(g3(gT1), g3(gE1), bc16(gss(5)), ALU.mult),
           waits=[(V, v_m1v)])
        gv(lambda e: e.tensor_tensor(g3(gS2), g3(gE2), bc16(gss(11)), ALU.mult),
           waits=[(V, v_m2v)])
        v_coef = gv(lambda e: e.tensor_tensor(gU1[:], gT1[:], gS2[:], ALU.add))
        cfb = ln_bf[:, 0:128]
        v_cfb = gv(lambda e: e.tensor_copy(cfb, gU1[:]), waits=[(V, v_coef)])
        psCb = PT[0][0:8, 512:640]
        for tt in range(NTT):
            w = [(V, v_cfb), (SCS, consts_s)] + ps_wait(("cf",))
            E("tensor",
              lambda e, tt=tt: e.transpose(psCb, ln_bf[:, tt * 8: tt * 8 + 8], idb[:]),
              waits=w, incs=[(M, 1)])
            mt = M.n
            E("vector",
              lambda e, tt=tt: e.tensor_copy(coefT[0:8, tt * 128:(tt + 1) * 128], psCb),
              waits=[(M, mt)], incs=[(V, 1)])
            psfree[("cf",)] = (V, V.n)
        v_coefT = V.n
        m_endA = M.n
        v_endA = V.n
        a_endA = A_.n

        SDB = sem("sdb")
        if dbg:
            E("sync", lambda e: e.dma_start(dbgt["afin"], A_fin[:]),
              waits=[(V, v_afin)], incs=[(SDB, 16)])
            E("sync", lambda e: e.dma_start(dbgt["stats"], stats[:]),
              waits=[(A_, a_rstd2)], incs=[(SDB, 16)])
            E("sync", lambda e: e.dma_start(dbgt["scores"], scores_sb[:]),
              waits=[(V, v_scores)], incs=[(SDB, 16)])
            E("sync", lambda e: e.dma_start(dbgt["coef"], gU1[:]),
              waits=[(V, v_coef)], incs=[(SDB, 16)])
            E("sync", lambda e: e.dma_start(dbgt["coefT"], coefT[:]),
              waits=[(V, v_coefT)], incs=[(SDB, 16)])
            E("sync", lambda e: e.dma_start(dbgt["ln1T"], ln1T[:]),
              waits=[(V, v_p2_done)], incs=[(SDB, 16)])

        sA.close()

        sB = ExitStack()
        enB = sB.enter_context
        hP = enB(nc.sbuf_tensor("hP", [128, NFL * 1024], BF16))
        ln2T = enB(nc.sbuf_tensor("ln2T", [128, NDC * 1024], BF16))
        yac = [enB(nc.sbuf_tensor("yac0", [128, NHC * 1024], BF16)),
               enB(nc.sbuf_tensor("yac1", [128, NHC * 1024], BF16))]
        w1b = [enB(nc.sbuf_tensor("w1b0", [128, NDC * 128], BF16)),
               enB(nc.sbuf_tensor("w1b1", [128, NDC * 128], BF16))]
        w3b = [enB(nc.sbuf_tensor("w3b0", [128, NDC * 128], BF16)),
               enB(nc.sbuf_tensor("w3b1", [128, NDC * 128], BF16))]
        w2b = [enB(nc.sbuf_tensor("w2b0", [128, NFL * 128], BF16)),
               enB(nc.sbuf_tensor("w2b1", [128, NFL * 128], BF16))]
        c_b = enB(nc.sbuf_tensor("c_b", [128, 1024], F32))
        s_tmp = [enB(nc.sbuf_tensor("stmp0", [128, 512], BF16)),
                 enB(nc.sbuf_tensor("stmp1", [128, 512], BF16))]
        ye_t = [enB(nc.sbuf_tensor("ye0", [128, 512], BF16)),
                enB(nc.sbuf_tensor("ye1", [128, 512], BF16))]
        out_t = enB(nc.sbuf_tensor("out_t", [128, D], F32))

        m_h_free = {}
        m_w2_free = {}
        a_silu = {}
        v_hprime = {}
        v_yev = {}
        m_h_last = 0
        m_endhalf = 0
        sst_last = 0
        eh_out = {}
        dbg_cbw = None
        dbg_h0w = None

        for half in range(2):
            if half == 0:
                wl = [(M, m_endKV), (ST2[0], scq_l2t_h0[0]), (ST2[1], scq_l2t_h0[1])]
            else:
                wl = [(M, m_h_last), (ST2[0], scq_l2t[0]), (ST2[1], scq_l2t[1])]
            wl += ([(SDB, SDB.n)] if (dbg and half == 0) else []) + \
                  ([dbg_h0w] if (dbg and half == 1 and dbg_h0w) else [])
            for dc in range(NDC):
                E("sync",
                  lambda e, dc=dc, half=half: e.dma_start(
                      ln2T.ap()[:, dc * 1024:(dc + 1) * 1024].rearrange(
                          "p (tl t) -> p tl t", t=128),
                      l2t_d[dc // 4, half * 8:(half + 1) * 8, :,
                            (dc % 4) * 128:(dc % 4 + 1) * 128].transpose([1, 0, 2])),
                  waits=(wl if dc == 0 else []), incs=[(SLN, 16)])
            sln_n = SLN.n
            E("vector", lambda e: e.memset(yac[0][:], 0.0),
              waits=(([(V, v_endA)] if half == 0 else [(M, m_endhalf)]) +
                     ([dbg_h0w] if (dbg and half == 1 and dbg_h0w) else [])),
              incs=[(V, 1)])

            for ee in range(NE):
                for i in range(2):
                    w = [(V, v_coefT)] + ps_wait(4 + i)
                    E("tensor",
                      lambda e, ee=ee, half=half, i=i: e.matmul(
                          P[4 + i][:], oh[0:8, ee * 128:(ee + 1) * 128],
                          coefT[0:8, half * 1024 + i * 512: half * 1024 + (i + 1) * 512],
                          start=True, stop=True),
                      waits=w, incs=[(M, 1)])
                    mi = M.n
                    E("vector",
                      lambda e, i=i: e.tensor_copy(c_b[:, i * 512:(i + 1) * 512], P[4 + i][:]),
                      waits=[(M, mi)] + ([dbg_cbw] if dbg_cbw else []), incs=[(V, 1)])
                    dbg_cbw = None
                    psfree[4 + i] = (V, V.n)
                v_cb_e = V.n
                if dbg and half == 0 and ee == 0:
                    E("sync", lambda e: e.dma_start(dbgt["cb0"], c_b[:]),
                      waits=[(V, v_cb_e)], incs=[(SDB, 16)])
                    dbg_cbw = (SDB, SDB.n)

                for fh in range(2):
                    step = ee * 2 + fh
                    ysrc, ydst = yac[step % 2], yac[(step + 1) % 2]
                    for fl in range(NFL):
                        fc = fh * NFL + fl
                        sl = fl % 2
                        w1w = [(SCG, consts_g)]
                        if half == 0 and ee == 0 and fc == 0:
                            w1w += [(V, v_endA), (A_, a_endA), (SCQ, SCQ.n)]
                        if ("w1", sl) in m_h_free:
                            w1w.append((M, m_h_free[("w1", sl)]))
                        E("gpsimd",
                          lambda e, ee=ee, fc=fc, sl=sl: e.dma_start(w1b[sl][:], w1_in[ee, fc]),
                          waits=w1w, incs=[(SW1[sl], 16)])
                        sw1_n = SW1[sl].n
                        w3w = []
                        if ("w3", sl) in m_h_free:
                            w3w.append((M, m_h_free[("w3", sl)]))
                        E("gpsimd",
                          lambda e, ee=ee, fc=fc, sl=sl: e.dma_start(w3b[sl][:], w3_in[ee, fc]),
                          waits=w3w, incs=[(SW3[sl], 16)])
                        sw3_n = SW3[sl].n
                        for tau in range(2):
                            PA, PB = P[2 * tau], P[2 * tau + 1]
                            wh = [(SW1[sl], sw1_n), (SLN, sln_n)]
                            if tau in a_silu:
                                wh.append((A_, a_silu[tau]))
                            else:
                                wh += ps_wait(2 * tau)
                            for dc in range(NDC):
                                E("tensor",
                                  lambda e, tau=tau, dc=dc, sl=sl, PA=PA: e.matmul(
                                      PA[:], w1b[sl][:, dc * 128:(dc + 1) * 128],
                                      ln2T[:, dc * 1024 + tau * 512:
                                           dc * 1024 + (tau + 1) * 512],
                                      start=(dc == 0), stop=(dc == NDC - 1)),
                                  waits=(wh if dc == 0 else []),
                                  incs=([(M, 1)] if dc == NDC - 1 else []))
                            m_h1 = M.n
                            if tau == 1:
                                m_h_free[("w1", sl)] = m_h1
                            ws = [(M, m_h1)]
                            if tau in v_hprime:
                                ws.append((V, v_hprime[tau]))
                            E("scalar",
                              lambda e, tau=tau, PA=PA: e.activation(s_tmp[tau][:], PA[:],
                                                                     AF.Silu),
                              waits=ws, incs=[(A_, 1)])
                            a_silu[tau] = A_.n
                            wh3 = [(SW3[sl], sw3_n)]
                            if tau in v_hprime:
                                wh3.append((V, v_hprime[tau]))
                            else:
                                wh3 += ps_wait(2 * tau + 1)
                            for dc in range(NDC):
                                E("tensor",
                                  lambda e, tau=tau, dc=dc, sl=sl, PB=PB: e.matmul(
                                      PB[:], w3b[sl][:, dc * 128:(dc + 1) * 128],
                                      ln2T[:, dc * 1024 + tau * 512:
                                           dc * 1024 + (tau + 1) * 512],
                                      start=(dc == 0), stop=(dc == NDC - 1)),
                                  waits=(wh3 if dc == 0 else []),
                                  incs=([(M, 1)] if dc == NDC - 1 else []))
                            m_h3 = M.n
                            if tau == 1:
                                m_h_free[("w3", sl)] = m_h3
                            E("vector",
                              lambda e, fl=fl, tau=tau, PB=PB: e.tensor_tensor(
                                  hP[:, fl * 1024 + tau * 512: fl * 1024 + (tau + 1) * 512],
                                  s_tmp[tau][:], PB[:], ALU.mult),
                              waits=[(A_, a_silu[tau]), (M, m_h3)], incs=[(V, 1)])
                            v_hprime[tau] = V.n
                    m_h_last = M.n
                    v_hp_all = V.n

                    for hc in range(NHC):
                        sl = hc % 2
                        w2w = []
                        if ("w2", sl) in m_w2_free:
                            w2w.append((M, m_w2_free[("w2", sl)]))
                        E("gpsimd",
                          lambda e, ee=ee, fh=fh, hc=hc, sl=sl: e.dma_start(
                              w2b[sl][:], w2_in[ee, fh, hc]),
                          waits=w2w, incs=[(SW2[sl], 16)])
                        sw2_n = SW2[sl].n
                        for tau in range(2):
                            PY = P[4 + tau]
                            wy = [(SW2[sl], sw2_n)]
                            if hc == 0:
                                wy.append((V, v_hp_all))
                            if tau in v_yev:
                                wy.append((V, v_yev[tau]))
                            else:
                                wy += ps_wait(4 + tau)
                            for fl in range(NFL):
                                E("tensor",
                                  lambda e, tau=tau, fl=fl, sl=sl, PY=PY: e.matmul(
                                      PY[:], w2b[sl][:, fl * 128:(fl + 1) * 128],
                                      hP[:, fl * 1024 + tau * 512:
                                         fl * 1024 + (tau + 1) * 512],
                                      start=(fl == 0), stop=(fl == NFL - 1)),
                                  waits=(wy if fl == 0 else []),
                                  incs=([(M, 1)] if fl == NFL - 1 else []))
                            m_y = M.n
                            if tau == 1:
                                m_w2_free[("w2", sl)] = m_y
                            E("vector",
                              lambda e, tau=tau, PY=PY: e.tensor_tensor(
                                  ye_t[tau][:], PY[:], c_b[:, tau * 512:(tau + 1) * 512],
                                  ALU.mult),
                              waits=[(M, m_y)], incs=[(V, 1)])
                            E("vector",
                              lambda e, hc=hc, tau=tau, ysrc=ysrc, ydst=ydst: e.tensor_tensor(
                                  ydst[:, hc * 1024 + tau * 512: hc * 1024 + (tau + 1) * 512],
                                  ysrc[:, hc * 1024 + tau * 512: hc * 1024 + (tau + 1) * 512],
                                  ye_t[tau][:], ALU.add),
                              incs=[(V, 1)])
                            v_yev[tau] = V.n
                    psfree[4] = (V, v_yev[0])
                    psfree[5] = (V, v_yev[1])
            v_yacc_done = V.n
            if dbg and half == 0:
                E("sync", lambda e: e.dma_start(dbgt["yach0"], yac[0][:]),
                  waits=[(V, v_yacc_done)], incs=[(SDB, 16)])
                E("sync", lambda e: e.dma_start(dbgt["hp"], hP[:]),
                  waits=[(V, v_yacc_done)], incs=[(SDB, 16)])
                dbg_h0w = (SDB, SDB.n)

            for tl in range(8):
                tt = half * 8 + tl
                sl = tl % 2
                if tl < 2:
                    if half == 0:
                        wx = [(V, a2c_lnb[NTT - 2 + tl])]
                    else:
                        wx = [(V, eh_out[(0, 6 + tl)])]
                else:
                    wx = [(V, eh_out[(half, tl - 2)])]
                E("sync",
                  lambda e, tt=tt, sl=sl: e.dma_start(xsl[sl][:], x1_d[tt]),
                  waits=wx, incs=[(SX[sl], 16)])
                sx_n = SX[sl].n
                wout = ([(SST, sst_last)] if sst_last else [])
                for hg in range(NHC // 4):
                    pslot = hg % 2
                    for q in range(4):
                        hc = hg * 4 + q
                        w = [(V, v_yacc_done)] + (ps_wait(("T", pslot)) if q == 0 else [])
                        E("tensor",
                          lambda e, tl=tl, hc=hc, q=q, pslot=pslot: e.transpose(
                              PT[pslot][:, q * 128:(q + 1) * 128],
                              yac[0][:, hc * 1024 + tl * 128: hc * 1024 + (tl + 1) * 128],
                              idb[:]),
                          waits=w, incs=([(M, 1)] if q == 3 else []))
                    mt = M.n
                    E("vector",
                      lambda e, hg=hg, sl=sl, pslot=pslot: e.tensor_tensor(
                          out_t[:, hg * 512:(hg + 1) * 512],
                          xsl[sl][:, hg * 512:(hg + 1) * 512],
                          PT[pslot][:, 0:512], ALU.add),
                      waits=[(M, mt), (SX[sl], sx_n)] + (wout if hg == 0 else []),
                      incs=[(V, 1)])
                    psfree[("T", pslot)] = (V, V.n)
                v_out = V.n
                eh_out[(half, tl)] = v_out
                E("sync",
                  lambda e, tt=tt: e.dma_start(out_d[tt * 128:(tt + 1) * 128, :], out_t[:]),
                  waits=[(V, v_out)], incs=[(SST, 16)])
                sst_last = SST.n
            m_endhalf = M.n

        if dbg:
            E("sync", lambda e: e.dma_start(dbgt["x1"], x1_d), waits=[(SCQ, scq_x1)],
              incs=[(SDB, 16)])
            E("sync", lambda e: e.dma_start(dbgt["l2t"], l2t_d),
              waits=[(ST2[0], scq_l2t[0]), (ST2[1], scq_l2t[1])], incs=[(SDB, 16)])
            E("sync", lambda e: e.dma_start(dbgt["yac"], yac[0][:]),
              waits=[(V, v_yacc_done)], incs=[(SDB, 16)])
        E("sync", lambda e: None, waits=[(SST, SST.n), (SDB, SDB.n)])
        sB.close()

        block = stk.enter_context(nc.Block())
        sch.build(block)

    if os.environ.get("KDBG"):
        import sys
        print("engine op counts:", sch.counts(), file=sys.stderr)
    return nc


_CACHE = {}


def _prep(W, b, gate_w, w1, w2, w3):
    W = np.asarray(W, dtype=np.float32)
    b = np.asarray(b, dtype=np.float32)
    gate_w = np.asarray(gate_w, dtype=np.float32)
    w1 = np.asarray(w1, dtype=np.float32)
    w2 = np.asarray(w2, dtype=np.float32)
    w3 = np.asarray(w3, dtype=np.float32)
    wp = np.stack([np.concatenate([W[:, i * 512:(i + 1) * 512],
                                   W[:, 2048 + i * 512: 2048 + (i + 1) * 512]], axis=1)
                   for i in range(4)])
    wp = np.ascontiguousarray(
        wp.reshape(4, 16, 128, 1024).transpose(0, 2, 1, 3).reshape(4, 128, 16 * 1024))
    bp = np.concatenate([np.concatenate([b[i * 512:(i + 1) * 512],
                                         b[2048 + i * 512: 2048 + (i + 1) * 512]])
                         for i in range(4)]).reshape(1, 4096)
    gwT = np.ascontiguousarray(
        gate_w.T.reshape(16, 128, 8).transpose(1, 0, 2).reshape(128, 128))

    def prep_w13(w):
        t = w.transpose(0, 2, 1)
        t = t.reshape(NE, 16, 128, 32, 128).transpose(0, 3, 2, 1, 4)
        return np.ascontiguousarray(t.reshape(NE, 32, 128, 2048))
    w1p = prep_w13(w1)
    w3p = prep_w13(w3)
    t = w2.transpose(0, 2, 1)
    t = t.reshape(NE, 2, 16, 128, 16, 128)
    t = t.transpose(0, 1, 4, 3, 2, 5)
    w2p = np.ascontiguousarray(t.reshape(NE, 2, NHC, 128, 2048))
    pos = np.arange(16384, dtype=np.float32)
    inv_freq = (1.0 / (10000.0 ** (np.arange(0, 1024, 2, dtype=np.float32) / 1024.0))
                ).astype(np.float32)
    freqs = (pos[:, None] * inv_freq[None, :]).astype(np.float32)
    cosf = np.cos(freqs).astype(np.float32)
    sinf = np.sin(freqs).astype(np.float32)
    return wp, bp, gwT, w1p, w2p, w3p, cosf, sinf


def kernel(L0_attn_X, L0_attn_W, L0_attn_b, L0_ffn_gate_w, L0_ffn_w1, L0_ffn_w2, L0_ffn_w3):
    X = np.ascontiguousarray(np.asarray(L0_attn_X, dtype=np.float32).reshape(16384, 2048))
    wp, bp, gwT, w1p, w2p, w3p, cosf, sinf = _prep(
        L0_attn_W, L0_attn_b, L0_ffn_gate_w, L0_ffn_w1, L0_ffn_w2, L0_ffn_w3)

    idb = np.eye(128).astype(ml_dtypes.bfloat16)
    idf = np.eye(128, dtype=np.float32)
    onc = np.ones((128, 1), dtype=np.float32)
    onr = np.ones((1, 128), dtype=np.float32)
    onb = np.ones((1, 128), dtype=ml_dtypes.bfloat16)
    oh = np.zeros((8, 8 * 128), dtype=np.float32)
    for e in range(8):
        oh[e, e * 128:(e + 1) * 128] = 1.0
    oh = oh.astype(ml_dtypes.bfloat16)

    if "nc" not in _CACHE:
        _CACHE["nc"] = build_nc()
    nc = _CACHE["nc"]

    in_maps = []
    for c in range(NCORES):
        rows = slice(c * T, (c + 1) * T)
        cs_c = np.ascontiguousarray(
            cosf[rows].reshape(NTT, 128, 2, 256).transpose(2, 0, 1, 3))
        sn_c = np.ascontiguousarray(
            sinf[rows].reshape(NTT, 128, 2, 256).transpose(2, 0, 1, 3))
        in_maps.append({
            "x0_in": np.ascontiguousarray(X[rows]), "wp_in": wp, "b_in": bp, "gw_in": gwT,
            "w1_in": w1p, "w3_in": w3p, "w2_in": w2p,
            "cs_in": cs_c, "sn_in": sn_c,
            "idb_in": idb, "idf_in": idf, "onc_in": onc, "onr_in": onr,
            "onb_in": onb, "oh_in": oh,
        })
    res = run_bass_kernel_spmd(nc, in_maps, core_ids=list(range(NCORES)))
    _CACHE["res"] = res
    out = np.concatenate([np.asarray(res.results[c]["out"]) for c in range(NCORES)], axis=0)
    return out.reshape(1, 16384, 2048).astype(np.float32)


# version 14 (final)
# speedup vs baseline: 1.0237x; 1.0035x over previous
import os
from contextlib import ExitStack

import numpy as np
import ml_dtypes
import concourse.bass as bass
import concourse.mybir as mybir
from concourse.bass_utils import run_bass_kernel_spmd

F32 = mybir.dt.float32
BF16 = mybir.dt.bfloat16
AF = mybir.ActivationFunctionType
ALU = mybir.AluOpType
AX = mybir.AxisListType

NCORES = 8
T = 2048
D = 2048
NE = 8
NTT = 16
NDC = 16
NHC = 16
NFL = 16
EPS = 1e-5

MAGIC = float(np.float32(1.5 * 2 ** 23))
INV2PI = float(np.float32(1.0 / (2.0 * np.pi)))
TWOPI = float(np.float32(2.0 * np.pi))
HALFPI = float(np.float32(np.pi / 2.0))
NEG = -1.0e30

ENGINES = ("sync", "scalar", "vector", "tensor", "gpsimd")


class Sem:
    __slots__ = ("h", "n")

    def __init__(self, h):
        self.h = h
        self.n = 0


class Sched:
    def __init__(self):
        self.ops = {e: [] for e in ENGINES}

    def emit(self, eng, fn, waits=(), incs=()):
        for s, v in waits:
            assert v <= s.n, f"deadlock: wait {v} > emitted {s.n} on {s.h.name}"
        lowered = (fn, [(s.h, v) for s, v in waits if v > 0], [])
        for s, k in incs:
            s.n += k
            lowered[2].append((s.h, k))
        self.ops[eng].append(lowered)

    def build(self, block):
        def make(name):
            ops = self.ops[name]

            def body(eng):
                for fn, waits, incs in ops:
                    for h, v in waits:
                        eng.wait_ge(h, v)
                    instr = fn(eng)
                    for h, k in incs:
                        if instr is None:
                            instr = eng.nop()
                        instr = instr.then_inc(h, k)
            return body

        for name in ENGINES:
            if self.ops[name]:
                getattr(block, name)(make(name))

    def counts(self):
        return {e: len(self.ops[e]) for e in ENGINES}


def build_nc():
    nc = bass.Bass("TRN2", target_bir_lowering=False)

    x0_in = nc.dram_tensor("x0_in", [T, D], F32, kind="ExternalInput").ap()
    wp_in = nc.dram_tensor("wp_in", [4, 128, 16 * 1024], F32, kind="ExternalInput").ap()
    b_in = nc.dram_tensor("b_in", [1, 4096], F32, kind="ExternalInput").ap()
    gw_in = nc.dram_tensor("gw_in", [128, 128], F32, kind="ExternalInput").ap()
    w1_in = nc.dram_tensor("w1_in", [NE, 32, 128, 2048], F32, kind="ExternalInput").ap()
    w3_in = nc.dram_tensor("w3_in", [NE, 32, 128, 2048], F32, kind="ExternalInput").ap()
    w2_in = nc.dram_tensor("w2_in", [NE, 2, NHC, 128, 2048], F32, kind="ExternalInput").ap()
    cs_in = nc.dram_tensor("cs_in", [2, NTT, 128, 256], F32, kind="ExternalInput").ap()
    sn_in = nc.dram_tensor("sn_in", [2, NTT, 128, 256], F32, kind="ExternalInput").ap()
    idb_in = nc.dram_tensor("idb_in", [128, 128], BF16, kind="ExternalInput").ap()
    idf_in = nc.dram_tensor("idf_in", [128, 128], F32, kind="ExternalInput").ap()
    onc_in = nc.dram_tensor("onc_in", [128, 1], F32, kind="ExternalInput").ap()
    onr_in = nc.dram_tensor("onr_in", [1, 128], F32, kind="ExternalInput").ap()
    onb_in = nc.dram_tensor("onb_in", [1, 128], BF16, kind="ExternalInput").ap()
    oh_in = nc.dram_tensor("oh_in", [8, 8 * 128], BF16, kind="ExternalInput").ap()
    out_d = nc.dram_tensor("out", [T, D], F32, kind="ExternalOutput").ap()
    x1_d = nc.dram_tensor("x1_d", [NTT, 128, D], F32, kind="Internal").ap()
    l2t_d = nc.dram_tensor("l2t_d", [4, NTT, 128, 512], BF16, kind="Internal").ap()
    ar_src_h = [nc.dram_tensor(f"ar_src{i}", [128, 4], F32, kind="Internal") for i in range(4)]
    ar_dst_h = [nc.dram_tensor(f"ar_dst{i}", [128, 4], F32, kind="Internal") for i in range(4)]
    ar_src_flat = [h.ap().rearrange("p g -> (p g)").unsqueeze(0) for h in ar_src_h]
    ar_dst_flat = [h.ap().rearrange("p g -> (p g)").unsqueeze(0) for h in ar_dst_h]

    dbg = bool(os.environ.get("KDBG"))
    dbgt = {}
    if dbg:
        dbgt["afin"] = nc.dram_tensor("dbg_afin", [1, 2048], F32, kind="ExternalOutput").ap()
        dbgt["stats"] = nc.dram_tensor("dbg_stats", [128, 192], F32, kind="ExternalOutput").ap()
        dbgt["scores"] = nc.dram_tensor("dbg_scores", [128, 128], F32, kind="ExternalOutput").ap()
        dbgt["coef"] = nc.dram_tensor("dbg_coef", [128, 128], F32, kind="ExternalOutput").ap()
        dbgt["coefT"] = nc.dram_tensor("dbg_coefT", [8, 2048], BF16, kind="ExternalOutput").ap()
        dbgt["ln1T"] = nc.dram_tensor("dbg_ln1T", [128, NDC * T], BF16, kind="ExternalOutput").ap()
        dbgt["x1"] = nc.dram_tensor("dbg_x1", [NTT, 128, D], F32, kind="ExternalOutput").ap()
        dbgt["l2t"] = nc.dram_tensor("dbg_l2t", [4, NTT, 128, 512], BF16,
                                     kind="ExternalOutput").ap()
        dbgt["yac"] = nc.dram_tensor("dbg_yac", [128, NHC * 1024], BF16,
                                     kind="ExternalOutput").ap()
        dbgt["yach0"] = nc.dram_tensor("dbg_yach0", [128, NHC * 1024], BF16,
                                       kind="ExternalOutput").ap()
        dbgt["cb0"] = nc.dram_tensor("dbg_cb0", [128, 1024], F32, kind="ExternalOutput").ap()
        dbgt["hp"] = nc.dram_tensor("dbg_hp", [128, NFL * 1024], BF16,
                                    kind="ExternalOutput").ap()

    sch = Sched()
    E = sch.emit

    with ExitStack() as stk:
        en = stk.enter_context

        def sem(name):
            return Sem(en(nc.semaphore(name)))

        SX = [sem("sx0"), sem("sx1")]
        STB = [sem("stb0"), sem("stb1")]
        SWB = [sem("swb0"), sem("swb1")]
        SW1 = [sem("sw1a"), sem("sw1b")]
        SW3 = [sem("sw3a"), sem("sw3b")]
        SW2 = [sem("sw2a"), sem("sw2b")]
        ST2 = [sem("st2a"), sem("st2b")]
        SLN = sem("sln")
        SCQ = sem("scq")
        SST = sem("sst")
        SARL = sem("sarl")
        SCS = sem("scs")
        SCG = sem("scg")
        CO = sem("co")
        M = sem("m")
        V = sem("v")
        A_ = sem("a")

        P = [en(nc.psum_tensor(f"ps{i}", [128, 512], F32)) for i in range(6)]
        PT = [en(nc.psum_tensor("pt0", [128, 1024], BF16)),
              en(nc.psum_tensor("pt1", [128, 1024], BF16))]
        psA = P[4][0:1, 0:512]
        psS = P[5][:, 0:8]
        psC = P[5][0:8, 128:256]
        psfree = {}

        def ps_wait(key):
            w = psfree.get(key)
            return [w] if w else []

        stats = en(nc.sbuf_tensor("stats", [128, 16 * 12], F32))

        def stc(q, tt=None):
            if tt is None:
                return stats[:, q * 16:(q + 1) * 16]
            return stats[:, q * 16 + tt:q * 16 + tt + 1]
        scores_sb = en(nc.sbuf_tensor("scores", [128, 128], F32))
        coefT = en(nc.sbuf_tensor("coefT", [8, 2048], BF16))
        xsl = [en(nc.sbuf_tensor("xs0", [128, D], F32)),
               en(nc.sbuf_tensor("xs1", [128, D], F32))]
        idb = en(nc.sbuf_tensor("idb", [128, 128], BF16))
        oh = en(nc.sbuf_tensor("oh", [8, 8 * 128], BF16))
        cpi2 = en(nc.sbuf_tensor("cpi2", [128, 1], F32))
        ceps = en(nc.sbuf_tensor("ceps", [128, 1], F32))
        cm05 = en(nc.sbuf_tensor("cm05", [128, 1], F32))
        cneg1 = en(nc.sbuf_tensor("cneg1", [128, 1], F32))

        sA = ExitStack()
        enA = sA.enter_context
        ln1T = enA(nc.sbuf_tensor("ln1T", [128, NDC * T], BF16))
        WbT = enA(nc.sbuf_tensor("Wb", [128, 2 * NDC * 1024], BF16))
        Wbs = [WbT[:, 0:NDC * 1024], WbT[:, NDC * 1024:2 * NDC * 1024]]
        ATMP = enA(nc.sbuf_tensor("ATMP", [128, 4096], F32))
        KH = ATMP[:, 0:512]
        G1 = ATMP[:, 512:1024]
        G2 = ATMP[:, 1024:1536]
        RK = ATMP[:, 1536:2048]
        RB = ATMP[:, 2048:2560]
        CKb = ATMP[:, 2560:3072]
        CVb = ATMP[:, 3072:3584]
        PSb = ATMP[:, 3584:4096]
        PSbs = [PSb, RB]
        psb_free = {}
        A_sb = enA(nc.sbuf_tensor("A_sb", [1, 2048], F32))
        A_fin = ATMP[0:1, 0:2048]
        A_bc = enA(nc.sbuf_tensor("A_bc", [128, 2048], F32))
        tabs = enA(nc.sbuf_tensor("tabs", [128, 2 * 256], F32))

        def tab(kind, slot):
            return tabs[:, kind * 256:(kind + 1) * 256]
        ln_f32 = enA(nc.sbuf_tensor("ln_f32", [128, D], F32))
        u_buf = ATMP[:, 0:2048]
        ln_bf = enA(nc.sbuf_tensor("ln_bf", [128, D], BF16))
        b_sb = enA(nc.sbuf_tensor("b_sb", [1, 4096], BF16))
        gwT = enA(nc.sbuf_tensor("gwT", [128, 128], BF16))
        onc = enA(nc.sbuf_tensor("onc", [128, 1], F32))
        onr = enA(nc.sbuf_tensor("onr", [1, 128], F32))
        onb = enA(nc.sbuf_tensor("onb", [1, 128], BF16))
        t2t = [enA(nc.sbuf_tensor("t2t0", [128, 256], BF16)),
               enA(nc.sbuf_tensor("t2t1", [128, 256], BF16))]
        gT1 = ATMP[:, 0:128]
        gE1 = ATMP[:, 128:256]
        gAB = ATMP[:, 256:384]
        gU1 = ATMP[:, 384:512]
        gS2 = ATMP[:, 512:640]
        gTO = ATMP[:, 640:768]
        gE2 = ATMP[:, 768:896]
        gCM = ATMP[:, 896:1024]
        gst = ATMP[:, 1024:1216]

        def gss(q):
            return gst[:, q * 16:(q + 1) * 16]

        E("sync", lambda e: e.dma_start(idb[:], idb_in), incs=[(SCS, 16)])
        E("sync", lambda e: e.dma_start(onc[:], onc_in), incs=[(SCS, 16)])
        E("sync", lambda e: e.dma_start(onr[:], onr_in), incs=[(SCS, 16)])
        E("sync", lambda e: e.dma_start(onb[:], onb_in), incs=[(SCS, 16)])
        E("sync", lambda e: e.dma_start(oh[:], oh_in), incs=[(SCS, 16)])
        consts_s = SCS.n
        E("gpsimd", lambda e: e.dma_start(gwT[:], gw_in), incs=[(SCG, 16)])
        E("gpsimd", lambda e: e.dma_start(b_sb[:], b_in), incs=[(SCG, 16)])
        consts_g = SCG.n
        E("vector", lambda e: e.memset(cpi2[:], HALFPI), incs=[(V, 1)])
        E("vector", lambda e: e.memset(ceps[:], EPS), incs=[(V, 1)])
        E("vector", lambda e: e.memset(cm05[:], -0.5), incs=[(V, 1)])
        E("vector", lambda e: e.memset(cneg1[:], -1.0), incs=[(V, 1)])

        p2_lnb = {}
        p_rste = {}
        m_lastT = 0
        psq = {}
        for tt in range(NTT):
            sl = tt % 2
            w = []
            if tt >= 2:
                w = [(V, p2_lnb[tt - 2])]
            E("sync",
              lambda e, tt=tt, sl=sl: e.dma_start(xsl[sl][:], x0_in[tt * 128:(tt + 1) * 128, :]),
              waits=w, incs=[(SX[sl], 16)])
            lw = [(SX[sl], SX[sl].n)]
            E("scalar",
              lambda e, tt=tt, sl=sl: e.activation(ln_f32[:], xsl[sl][:], AF.Copy,
                                                   accum_out=stc(0, tt)),
              waits=lw + ([(A_, psq[tt - 1])] if tt > 0 else []), incs=[(A_, 1)])
            E("scalar",
              lambda e, tt=tt, sl=sl: e.activation(ln_f32[:], xsl[sl][:], AF.Square,
                                                   accum_out=stc(1, tt)),
              incs=[(A_, 1)])
            a_sq = A_.n
            psq[tt] = a_sq
            E("vector", lambda e, tt=tt: e.tensor_scalar(stc(2, tt), stc(0, tt), 1.0 / D,
                                                         None, ALU.mult),
              waits=[(A_, a_sq), (V, V.n)], incs=[(V, 1)])
            E("vector", lambda e, tt=tt: e.tensor_scalar(stc(9, tt), stc(1, tt), 1.0 / D,
                                                         None, ALU.mult),
              waits=[(V, V.n)], incs=[(V, 1)])
            E("vector", lambda e, tt=tt: e.tensor_tensor(stc(10, tt), stc(2, tt), stc(2, tt),
                                                         ALU.mult),
              waits=[(V, V.n)], incs=[(V, 1)])
            E("vector", lambda e, tt=tt: e.tensor_tensor(stc(11, tt), stc(9, tt), stc(10, tt),
                                                         ALU.subtract),
              waits=[(V, V.n)], incs=[(V, 1)])
            v_var = V.n
            E("scalar", lambda e, tt=tt: e.activation(stc(9, tt), stc(11, tt), AF.Ln,
                                                      bias=ceps[:]),
              waits=[(V, v_var)], incs=[(A_, 1)])
            E("scalar", lambda e, tt=tt: e.activation(stc(3, tt), stc(9, tt), AF.Exp,
                                                      scale=cm05[:]),
              incs=[(A_, 1)])
            a_rstd1 = A_.n
            p_rste[tt] = a_rstd1
            E("vector",
              lambda e, tt=tt, sl=sl: e.tensor_scalar(ln_bf[:], xsl[sl][:], stc(2, tt),
                                                      stc(3, tt), ALU.subtract, ALU.mult),
              waits=[(A_, a_rstd1), (V, V.n), (M, m_lastT)], incs=[(V, 1)])
            v_lnb = V.n
            p2_lnb[tt] = v_lnb
            for dcg in range(NDC // 4):
                pslot = dcg % 2
                for q in range(4):
                    dc = dcg * 4 + q
                    w = [(V, v_lnb), (SCS, consts_s)] + (ps_wait(("T", pslot)) if q == 0 else [])
                    E("tensor",
                      lambda e, dc=dc, q=q, pslot=pslot: e.transpose(
                          PT[pslot][:, q * 128:(q + 1) * 128],
                          ln_bf[:, dc * 128:(dc + 1) * 128], idb[:]),
                      waits=w, incs=([(M, 1)] if q == 3 else []))
                mt = M.n
                E("vector",
                  lambda e, tt=tt, dcg=dcg, pslot=pslot: e.tensor_copy(
                      ln1T.ap().rearrange("p (d t) -> p d t", t=T)[
                          :, dcg * 4:(dcg + 1) * 4, tt * 128:(tt + 1) * 128],
                      PT[pslot][:, 0:512].rearrange("p (g c) -> p g c", c=128)),
                  waits=[(M, mt)], incs=[(V, 1)])
                psfree[("T", pslot)] = (V, V.n)
            m_lastT = M.n
        v_p2_done = V.n
        v_m1s = V.n
        a_rstd1 = A_.n

        m_wb_free = {}
        v_Acp = 0
        v_tabfree = {}
        a_sin_prev = 0
        m_A_prev = 0
        v_P_prev = 0
        pend_A = None
        arw = {}
        for np_ in range(4):
            rot = np_ < 2
            wsl = np_ % 2
            Wb = Wbs[wsl]
            E("gpsimd",
              lambda e, np_=np_, wsl=wsl: e.dma_start(Wbs[wsl][:], wp_in[np_]),
              waits=([(M, m_wb_free[wsl])] if wsl in m_wb_free else []) + [(SCG, consts_g)],
              incs=[(SWB[wsl], 16)])
            swb_n = SWB[wsl].n
            for tt in range(NTT):
                pr = tt % 2
                psK, psV = P[2 * pr], P[2 * pr + 1]
                kK, kV = 2 * pr, 2 * pr + 1
                if rot:
                    tsl = 0
                    wtb = ([v_tabfree["last"]] if "last" in v_tabfree else [])
                    E("sync",
                      lambda e, np_=np_, tt=tt: e.dma_start(tab(0, 0), cs_in[np_, tt]),
                      waits=wtb, incs=[(STB[0], 16)])
                    E("sync",
                      lambda e, np_=np_, tt=tt: e.dma_start(tab(1, 0), sn_in[np_, tt]),
                      incs=[(STB[0], 16)])
                    stb_n = STB[0].n
                wk = [(SWB[wsl], swb_n)] + ps_wait(kK)
                if np_ == 0 and tt == 0:
                    wk.append((V, v_p2_done))
                for dc in range(NDC):
                    E("tensor",
                      lambda e, tt=tt, dc=dc, psK=psK, Wb=Wb: e.matmul(
                          psK[:], ln1T[:, dc * T + tt * 128: dc * T + (tt + 1) * 128],
                          Wb[:, dc * 1024: dc * 1024 + 512], start=(dc == 0), stop=False),
                      waits=(wk if dc == 0 else []))
                E("tensor",
                  lambda e, np_=np_, psK=psK: e.matmul(
                      psK[:], onb[:], b_sb[0:1, np_ * 1024: np_ * 1024 + 512],
                      start=False, stop=True),
                  incs=[(M, 1)])
                m_k = M.n
                wv = ps_wait(kV)
                for dc in range(NDC):
                    E("tensor",
                      lambda e, tt=tt, dc=dc, psV=psV, Wb=Wb: e.matmul(
                          psV[:], ln1T[:, dc * T + tt * 128: dc * T + (tt + 1) * 128],
                          Wb[:, dc * 1024 + 512: dc * 1024 + 1024], start=(dc == 0), stop=False),
                      waits=(wv if dc == 0 else []))
                E("tensor",
                  lambda e, np_=np_, psV=psV: e.matmul(
                      psV[:], onb[:], b_sb[0:1, np_ * 1024 + 512: np_ * 1024 + 1024],
                      start=False, stop=True),
                  incs=[(M, 1)])
                m_v = M.n

                def rotary(ps, m_ps, stb_val, tslot):
                    ev, od = ps[:, 0:512:2], ps[:, 1:512:2]
                    ct, st = tab(0, tslot), tab(1, tslot)
                    E("vector", lambda e, ev=ev, ct=ct: e.tensor_tensor(
                        G1[:, 0:256], ev, ct, ALU.mult),
                      waits=[(M, m_ps), (STB[tslot], stb_val), (V, V.n)], incs=[(V, 1)])
                    E("vector", lambda e, od=od, st=st: e.tensor_tensor(
                        G2[:, 0:256], od, st, ALU.mult), waits=[(V, V.n)], incs=[(V, 1)])
                    E("vector", lambda e: e.tensor_tensor(
                        KH[:, 0:512:2], G1[:, 0:256], G2[:, 0:256], ALU.subtract),
                      waits=[(V, V.n)], incs=[(V, 1)])
                    E("vector", lambda e, od=od, ct=ct: e.tensor_tensor(
                        G1[:, 0:256], od, ct, ALU.mult), waits=[(V, V.n)], incs=[(V, 1)])
                    E("vector", lambda e, ev=ev, st=st: e.tensor_tensor(
                        G2[:, 0:256], ev, st, ALU.mult), waits=[(V, V.n)], incs=[(V, 1)])
                    E("vector", lambda e: e.tensor_tensor(
                        KH[:, 1:512:2], G1[:, 0:256], G2[:, 0:256], ALU.add),
                      waits=[(V, V.n)], incs=[(V, 1)])
                    return V.n

                def cos_chain(src_ap, out_sb, first_w, rk_w, sin_w):
                    E("vector", lambda e, src_ap=src_ap: e.tensor_scalar(
                        G1[:], src_ap, INV2PI, 0.25, ALU.mult, ALU.add),
                      waits=list(first_w) + [(V, V.n)], incs=[(V, 1)])
                    E("vector", lambda e: e.tensor_scalar(
                        G2[:], G1[:], MAGIC, MAGIC, ALU.add, ALU.subtract), incs=[(V, 1)])
                    E("vector", lambda e: e.tensor_scalar(
                        G1[:], G2[:], TWOPI, None, ALU.mult), incs=[(V, 1)])
                    E("vector", lambda e, src_ap=src_ap: e.tensor_tensor(
                        RK[:], src_ap, G1[:], ALU.subtract), waits=rk_w, incs=[(V, 1)])
                    vr = V.n
                    E("scalar", lambda e, out_sb=out_sb: e.activation(
                        out_sb[:], RK[:], AF.Sin, bias=cpi2[:]),
                      waits=[(V, vr)] + sin_w, incs=[(A_, 1)])
                    return A_.n

                rk_guard = ([(A_, a_sin_prev)] if a_sin_prev else []) + \
                           ([(M, m_A_prev)] if m_A_prev else [])
                sin_guard = [(V, v_P_prev)] if v_P_prev else []
                if rot:
                    rotary(psK, m_k, stb_n, tsl)
                    v_k_done = V.n
                    aK = cos_chain(KH[:], CKb, [], rk_guard, sin_guard)
                    rotary(psV, m_v, stb_n, tsl)
                    v_v_done = V.n
                    aV = cos_chain(KH[:], CVb, [], [(A_, aK)], sin_guard)
                    v_tabfree["last"] = (V, v_v_done)
                else:
                    aK = cos_chain(psK[:], CKb, [(M, m_k)], rk_guard, sin_guard)
                    v_k_done = V.n
                    aV = cos_chain(psV[:], CVb, [(M, m_v)], [(A_, aK)], sin_guard)
                    v_v_done = V.n
                psfree[kK] = (V, v_k_done)
                psfree[kV] = (V, v_v_done)
                psb = PSbs[tt % 2]
                E("vector", lambda e, psb=psb: e.tensor_tensor(psb[:], CKb[:], CVb[:], ALU.mult),
                  waits=[(A_, aK), (A_, aV)] +
                        ([(M, psb_free[tt % 2])] if (tt % 2) in psb_free else []),
                  incs=[(V, 1)])
                v_P = V.n
                if pend_A is not None:
                    ptt, pv, pfirst, ppsb = pend_A
                    wA = [(V, pv)]
                    if pfirst:
                        wA += [(SCS, consts_s)] + ([(V, v_Acp)] if v_Acp else [])
                    E("tensor",
                      lambda e, ppsb=ppsb, pfirst=pfirst: e.matmul(
                          psA, onc[:], ppsb[:], start=pfirst, stop=False),
                      waits=wA, incs=[(M, 1)])
                    psb_free[ptt % 2] = M.n
                pend_A = (tt, v_P, (tt == 0), psb)
                a_sin_prev = aV
                v_P_prev = v_P
            ptt, pv, pfirst, ppsb = pend_A
            E("tensor",
              lambda e, ppsb=ppsb, pfirst=pfirst: e.matmul(
                  psA, onc[:], ppsb[:], start=pfirst, stop=True),
              waits=[(V, pv)] + ([(SCS, consts_s)] if pfirst else []), incs=[(M, 1)])
            psb_free[ptt % 2] = M.n
            pend_A = None
            m_A_prev = M.n
            m_endKV = M.n
            E("vector",
              lambda e, np_=np_: e.tensor_copy(A_sb[0:1, np_ * 512:(np_ + 1) * 512], psA),
              waits=[(M, m_A_prev)], incs=[(V, 1)])
            v_Acp = V.n
            m_wb_free[wsl] = M.n
            E("scalar",
              lambda e, np_=np_: e.dma_start(ar_src_flat[np_],
                                             A_sb[0:1, np_ * 512:(np_ + 1) * 512]),
              waits=[(V, v_Acp)], incs=[(SCQ, 16)])
            arw[np_] = SCQ.n
        psfree[4] = (V, v_Acp)
        for np_ in range(4):
            E("gpsimd", lambda e, np_=np_: e.collective_compute(
                "AllReduce", ALU.add, replica_groups=[list(range(NCORES))],
                ins=[ar_src_h[np_].ap().opt()], outs=[ar_dst_h[np_].ap().opt()]),
              waits=[(SCQ, arw[np_])], incs=[(CO, 1)])

        pre_sx = {}
        for tt in range(2):
            E("sync",
              lambda e, tt=tt: e.dma_start(xsl[tt][:], x0_in[tt * 128:(tt + 1) * 128, :]),
              waits=[(V, p2_lnb[NTT - 2 + tt])], incs=[(SX[tt], 16)])
            pre_sx[tt] = SX[tt].n
        for np_ in range(4):
            E("sync",
              lambda e, np_=np_: e.dma_start(A_sb[0:1, np_ * 512:(np_ + 1) * 512],
                                             ar_dst_flat[np_]),
              waits=[(CO, 4)], incs=[(SARL, 16)])
        E("vector", lambda e: e.tensor_scalar(A_fin[:], A_sb[:], 0.5, 0.5, ALU.mult, ALU.add),
          waits=[(SARL, SARL.n)], incs=[(V, 1)])
        v_afin = V.n
        for i in range(4):
            E("tensor",
              lambda e, i=i: e.matmul(P[i][:], onr[:], A_fin[0:1, i * 512:(i + 1) * 512],
                                      start=True, stop=True),
              waits=[(V, v_afin)] + ps_wait(i), incs=[(M, 1)])
            mi = M.n
            E("vector",
              lambda e, i=i: e.tensor_copy(A_bc[:, i * 512:(i + 1) * 512], P[i][:]),
              waits=[(M, mi)], incs=[(V, 1)])
            psfree[i] = (V, V.n)
        v_abc = V.n

        a2 = {}
        x1w = {}
        a2sq = {}
        a2c_lnb = {}
        m_lastT2 = m_lastT
        for tt in range(NTT):
            sl = tt % 2
            if tt < 2:
                lw = (SX[sl], pre_sx[tt])
            else:
                E("sync",
                  lambda e, tt=tt, sl=sl: e.dma_start(xsl[sl][:],
                                                      x0_in[tt * 128:(tt + 1) * 128, :]),
                  waits=[(V, a2[tt - 2])], incs=[(SX[sl], 16)])
                lw = (SX[sl], SX[sl].n)
            lnw = [lw, (V, v_abc), (V, v_m1s), (A_, a_rstd1)]
            if tt > 0:
                lnw += [(A_, a2sq[tt - 1]), (SCQ, x1w[tt - 1])]
            E("vector",
              lambda e, tt=tt, sl=sl: e.tensor_scalar(ln_f32[:], xsl[sl][:], stc(2, tt),
                                                      stc(3, tt), ALU.subtract, ALU.mult),
              waits=lnw, incs=[(V, 1)])
            E("vector", lambda e: e.tensor_tensor(u_buf[:], A_bc[:], ln_f32[:], ALU.mult),
              incs=[(V, 1)])
            E("vector", lambda e, sl=sl: e.tensor_tensor(ln_f32[:], xsl[sl][:], u_buf[:],
                                                         ALU.add), incs=[(V, 1)])
            v_x1 = V.n
            a2[tt] = v_x1
            E("scalar", lambda e, tt=tt: e.dma_start(x1_d[tt], ln_f32[:]),
              waits=[(V, v_x1)], incs=[(SCQ, 16)])
            x1w[tt] = SCQ.n
            E("scalar", lambda e, tt=tt: e.activation(u_buf[:], ln_f32[:], AF.Copy,
                                                      accum_out=stc(5, tt)),
              waits=[(V, v_x1)], incs=[(A_, 1)])
            E("scalar", lambda e, tt=tt: e.activation(u_buf[:], ln_f32[:], AF.Square,
                                                      accum_out=stc(6, tt)),
              incs=[(A_, 1)])
            a_sq = A_.n
            a2sq[tt] = a_sq
            E("vector", lambda e, tt=tt: e.tensor_scalar(stc(7, tt), stc(5, tt), 1.0 / D,
                                                         None, ALU.mult),
              waits=[(A_, a_sq), (V, V.n)], incs=[(V, 1)])
            E("vector", lambda e, tt=tt: e.tensor_scalar(stc(9, tt), stc(6, tt), 1.0 / D,
                                                         None, ALU.mult),
              waits=[(V, V.n)], incs=[(V, 1)])
            E("vector", lambda e, tt=tt: e.tensor_tensor(stc(10, tt), stc(7, tt), stc(7, tt),
                                                         ALU.mult),
              waits=[(V, V.n)], incs=[(V, 1)])
            E("vector", lambda e, tt=tt: e.tensor_tensor(stc(11, tt), stc(9, tt), stc(10, tt),
                                                         ALU.subtract),
              waits=[(V, V.n)], incs=[(V, 1)])
            v_var = V.n
            E("scalar", lambda e, tt=tt: e.activation(stc(9, tt), stc(11, tt), AF.Ln,
                                                      bias=ceps[:]),
              waits=[(V, v_var)], incs=[(A_, 1)])
            E("scalar", lambda e, tt=tt: e.activation(stc(8, tt), stc(9, tt), AF.Exp,
                                                      scale=cm05[:]),
              incs=[(A_, 1)])
            a_rstd2 = A_.n
            E("vector",
              lambda e, tt=tt: e.tensor_scalar(ln_bf[:], ln_f32[:], stc(7, tt),
                                               stc(8, tt), ALU.subtract, ALU.mult),
              waits=[(A_, a_rstd2), (V, V.n), (M, m_lastT2)], incs=[(V, 1)])
            v_lnb = V.n
            a2c_lnb[tt] = v_lnb
            for dcg in range(NDC // 2):
                pslot = dcg % 2
                for q in range(2):
                    dc = dcg * 2 + q
                    w = [(V, v_lnb)] + (ps_wait(("T", pslot)) if q == 0 else [])
                    E("tensor",
                      lambda e, dc=dc, q=q, pslot=pslot: e.transpose(
                          PT[pslot][:, q * 128:(q + 1) * 128],
                          ln_bf[:, dc * 128:(dc + 1) * 128], idb[:]),
                      waits=w, incs=([(M, 1)] if q == 1 else []))
                mt = M.n
                wv = [(M, mt)] + ps_wait(("t2", pslot))
                if ST2[pslot].n > 0:
                    wv.append((ST2[pslot], ST2[pslot].n))
                E("vector",
                  lambda e, pslot=pslot: e.tensor_copy(t2t[pslot][:], PT[pslot][:, 0:256]),
                  waits=wv, incs=[(V, 1)])
                psfree[("T", pslot)] = (V, V.n)
                vev = V.n
                E("scalar",
                  lambda e, dcg=dcg, tt=tt, pslot=pslot: e.dma_start(
                      l2t_d[dcg // 2, tt, :, (dcg % 2) * 256:(dcg % 2 + 1) * 256],
                      t2t[pslot][:]),
                  waits=[(V, vev)], incs=[(ST2[pslot], 16)])
                for q in range(2):
                    dc = dcg * 2 + q
                    wmm = [(V, vev)]
                    if dc == 0:
                        wmm += [(SCG, consts_g)] + ps_wait(5)
                    E("tensor",
                      lambda e, dc=dc, q=q, pslot=pslot: e.matmul(
                          psS, t2t[pslot][:, q * 128:(q + 1) * 128],
                          gwT[:, dc * 8:(dc + 1) * 8],
                          start=(dc == 0), stop=(dc == NDC - 1)),
                      waits=(wmm if q == 0 else []),
                      incs=([(M, 1)] if dc == NDC - 1 else []))
                psfree[("t2", pslot)] = (M, M.n)
            m_lastT2 = M.n
            E("vector",
              lambda e, tt=tt: e.tensor_copy(scores_sb[:, tt * 8:(tt + 1) * 8], psS),
              waits=[(M, m_lastT2)], incs=[(V, 1)])
            psfree[5] = (V, V.n)
            if tt == 7:
                scq_l2t_half0 = [ST2[0].n, ST2[1].n]
        scq_x1 = SCQ.n
        scq_l2t = [ST2[0].n, ST2[1].n]
        scq_l2t_h0 = list(scq_l2t_half0)

        def g3(t):
            a = t if isinstance(t, bass.AP) else t.ap()
            return a.rearrange("p (g k) -> p g k", k=8)

        def bc16(t16):
            return t16.unsqueeze(2).broadcast_to([128, 16, 8])

        v_scores = V.n

        def gv(fn, waits=()):
            E("vector", fn, waits=[(V, V.n)] + list(waits), incs=[(V, 1)])
            return V.n

        def ga(fn, wv):
            E("scalar", fn, waits=[(V, wv)], incs=[(A_, 1)])
            return A_.n

        def gw(fn, wa):
            E("vector", fn, waits=[(A_, wa), (V, V.n)], incs=[(V, 1)])
            return V.n

        BIG = 1.0e30
        v_nv1 = gv(lambda e: e.tensor_reduce(gss(0), g3(scores_sb), AX.X, ALU.max, negate=True))
        gv(lambda e: e.tensor_tensor(g3(gT1), g3(scores_sb), bc16(gss(0)), ALU.add),
           waits=[(V, v_nv1)])
        gv(lambda e: e.tensor_scalar(gS2[:], gT1[:], -BIG, None, ALU.mult))
        gv(lambda e: e.tensor_scalar(gCM[:], gS2[:], 1.0, None, ALU.min))
        gv(lambda e: e.tensor_scalar(gE1[:], gCM[:], -1.0, 1.0, ALU.mult, ALU.add))
        ax = ga(lambda e: e.activation(gAB[:], scores_sb[:], AF.Abs), v_scores)
        v_nm1 = gw(lambda e: e.tensor_scalar(gss(1), gss(0), -1.0, None, ALU.mult), ax)
        gv(lambda e: e.tensor_tensor(g3(gU1), g3(gAB), bc16(gss(1)), ALU.max),
           waits=[(V, v_nm1)])
        gv(lambda e: e.tensor_scalar(gCM[:], gU1[:], 0.02, None, ALU.mult))
        gv(lambda e: e.tensor_tensor(gS2[:], gT1[:], gCM[:], ALU.add))
        gv(lambda e: e.tensor_scalar(gCM[:], gS2[:], -BIG, None, ALU.mult))
        gv(lambda e: e.tensor_scalar(gU1[:], gCM[:], 1.0, None, ALU.min))
        gv(lambda e: e.tensor_scalar(gCM[:], gU1[:], 0.0, None, ALU.max))
        gv(lambda e: e.tensor_scalar(gS2[:], gCM[:], NEG, None, ALU.mult))
        v_sm1 = gv(lambda e: e.tensor_tensor(gU1[:], gT1[:], gS2[:], ALU.add))
        a_e1 = ga(lambda e: e.activation(gTO[:], gU1[:], AF.Exp), v_sm1)
        v_d1 = gw(lambda e: e.tensor_reduce(gss(2), g3(gTO), AX.X, ALU.add), a_e1)
        a_l1 = ga(lambda e: e.activation(gss(9), gss(2), AF.Ln), v_d1)
        E("scalar", lambda e: e.activation(gss(3), gss(9), AF.Exp, scale=cneg1[:]),
          incs=[(A_, 1)])
        a_r1 = A_.n
        gv(lambda e: e.tensor_tensor(gU1[:], gTO[:], gE1[:], ALU.mult))
        gv(lambda e: e.tensor_reduce(gss(4), g3(gU1), AX.X, ALU.add))
        v_m1v = gw(lambda e: e.tensor_tensor(gss(5), gss(4), gss(3), ALU.mult), a_r1)
        gv(lambda e: e.tensor_scalar(gS2[:], gE1[:], NEG, None, ALU.mult))
        gv(lambda e: e.tensor_tensor(gU1[:], scores_sb[:], gS2[:], ALU.add))
        v_nv2 = gv(lambda e: e.tensor_reduce(gss(6), g3(gU1), AX.X, ALU.max, negate=True))
        gv(lambda e: e.tensor_tensor(g3(gTO), g3(scores_sb), bc16(gss(6)), ALU.add),
           waits=[(V, v_nv2)])
        gv(lambda e: e.tensor_tensor(g3(gS2), g3(gU1), bc16(gss(6)), ALU.add))
        gv(lambda e: e.tensor_scalar(gCM[:], gS2[:], -BIG, None, ALU.mult))
        gv(lambda e: e.tensor_scalar(gU1[:], gCM[:], 1.0, None, ALU.min))
        gv(lambda e: e.tensor_scalar(gE2[:], gU1[:], -1.0, 1.0, ALU.mult, ALU.add))
        v_nm2 = gv(lambda e: e.tensor_scalar(gss(7), gss(6), -1.0, None, ALU.mult))
        gv(lambda e: e.tensor_tensor(g3(gS2), g3(gAB), bc16(gss(7)), ALU.max),
           waits=[(V, v_nm2)])
        gv(lambda e: e.tensor_scalar(gU1[:], gS2[:], 0.02, None, ALU.mult))
        gv(lambda e: e.tensor_tensor(gS2[:], gTO[:], gU1[:], ALU.add))
        gv(lambda e: e.tensor_scalar(gCM[:], gS2[:], -BIG, None, ALU.mult))
        gv(lambda e: e.tensor_scalar(gU1[:], gCM[:], 1.0, None, ALU.min))
        gv(lambda e: e.tensor_scalar(gCM[:], gU1[:], 0.0, None, ALU.max))
        gv(lambda e: e.tensor_scalar(gU1[:], gCM[:], NEG, None, ALU.mult))
        gv(lambda e: e.tensor_scalar(gCM[:], gE1[:], NEG, None, ALU.mult))
        gv(lambda e: e.tensor_tensor(gS2[:], gU1[:], gCM[:], ALU.add))
        v_sm2 = gv(lambda e: e.tensor_tensor(gU1[:], gTO[:], gS2[:], ALU.add))
        a_e2 = ga(lambda e: e.activation(gCM[:], gU1[:], AF.Exp), v_sm2)
        v_d2 = gw(lambda e: e.tensor_reduce(gss(8), g3(gCM), AX.X, ALU.add), a_e2)
        a_l2 = ga(lambda e: e.activation(gss(10), gss(8), AF.Ln), v_d2)
        E("scalar", lambda e: e.activation(gss(9), gss(10), AF.Exp, scale=cneg1[:]),
          incs=[(A_, 1)])
        a_r2 = A_.n
        gv(lambda e: e.tensor_tensor(gU1[:], gCM[:], gE2[:], ALU.mult))
        gv(lambda e: e.tensor_reduce(gss(10), g3(gU1), AX.X, ALU.add))
        v_m2v = gw(lambda e: e.tensor_tensor(gss(11), gss(10), gss(9), ALU.mult), a_r2)
        gv(lambda e: e.tensor_tensor(g3(gT1), g3(gE1), bc16(gss(5)), ALU.mult),
           waits=[(V, v_m1v)])
        gv(lambda e: e.tensor_tensor(g3(gS2), g3(gE2), bc16(gss(11)), ALU.mult),
           waits=[(V, v_m2v)])
        v_coef = gv(lambda e: e.tensor_tensor(gU1[:], gT1[:], gS2[:], ALU.add))
        cfb = ln_bf[:, 0:128]
        v_cfb = gv(lambda e: e.tensor_copy(cfb, gU1[:]), waits=[(V, v_coef)])
        psCb = PT[0][0:8, 512:640]
        for tt in range(NTT):
            w = [(V, v_cfb), (SCS, consts_s)] + ps_wait(("cf",))
            E("tensor",
              lambda e, tt=tt: e.transpose(psCb, ln_bf[:, tt * 8: tt * 8 + 8], idb[:]),
              waits=w, incs=[(M, 1)])
            mt = M.n
            E("vector",
              lambda e, tt=tt: e.tensor_copy(coefT[0:8, tt * 128:(tt + 1) * 128], psCb),
              waits=[(M, mt)], incs=[(V, 1)])
            psfree[("cf",)] = (V, V.n)
        v_coefT = V.n
        m_endA = M.n
        v_endA = V.n
        a_endA = A_.n

        SDB = sem("sdb")
        if dbg:
            E("sync", lambda e: e.dma_start(dbgt["afin"], A_fin[:]),
              waits=[(V, v_afin)], incs=[(SDB, 16)])
            E("sync", lambda e: e.dma_start(dbgt["stats"], stats[:]),
              waits=[(A_, a_rstd2)], incs=[(SDB, 16)])
            E("sync", lambda e: e.dma_start(dbgt["scores"], scores_sb[:]),
              waits=[(V, v_scores)], incs=[(SDB, 16)])
            E("sync", lambda e: e.dma_start(dbgt["coef"], gU1[:]),
              waits=[(V, v_coef)], incs=[(SDB, 16)])
            E("sync", lambda e: e.dma_start(dbgt["coefT"], coefT[:]),
              waits=[(V, v_coefT)], incs=[(SDB, 16)])
            E("sync", lambda e: e.dma_start(dbgt["ln1T"], ln1T[:]),
              waits=[(V, v_p2_done)], incs=[(SDB, 16)])

        sA.close()

        sB = ExitStack()
        enB = sB.enter_context
        hP = enB(nc.sbuf_tensor("hP", [128, NFL * 1024], BF16))
        ln2T = enB(nc.sbuf_tensor("ln2T", [128, NDC * 1024], BF16))
        yac = [enB(nc.sbuf_tensor("yac0", [128, NHC * 1024], BF16)),
               enB(nc.sbuf_tensor("yac1", [128, NHC * 1024], BF16))]
        w1b = [enB(nc.sbuf_tensor("w1b0", [128, NDC * 128], BF16)),
               enB(nc.sbuf_tensor("w1b1", [128, NDC * 128], BF16))]
        w3b = [enB(nc.sbuf_tensor("w3b0", [128, NDC * 128], BF16)),
               enB(nc.sbuf_tensor("w3b1", [128, NDC * 128], BF16))]
        w2b = [enB(nc.sbuf_tensor("w2b0", [128, NFL * 128], BF16)),
               enB(nc.sbuf_tensor("w2b1", [128, NFL * 128], BF16))]
        c_b = enB(nc.sbuf_tensor("c_b", [128, 1024], F32))
        s_tmp = [enB(nc.sbuf_tensor("stmp0", [128, 512], BF16)),
                 enB(nc.sbuf_tensor("stmp1", [128, 512], BF16))]
        ye_t = [enB(nc.sbuf_tensor("ye0", [128, 512], BF16)),
                enB(nc.sbuf_tensor("ye1", [128, 512], BF16))]
        out_t = [enB(nc.sbuf_tensor("out_t0", [128, D], F32)),
                 enB(nc.sbuf_tensor("out_t1", [128, D], F32))]

        m_h_free = {}
        m_w2_free = {}
        a_silu = {}
        v_hprime = {}
        v_yev = {}
        m_h_last = 0
        m_endhalf = 0
        sst_slot = {0: 0, 1: 0}
        eh_out = {}
        dbg_cbw = None
        dbg_h0w = None

        for half in range(2):
            if half == 0:
                wl = [(M, m_endKV), (ST2[0], scq_l2t_h0[0]), (ST2[1], scq_l2t_h0[1])]
            else:
                wl = [(M, m_h_last), (ST2[0], scq_l2t[0]), (ST2[1], scq_l2t[1])]
            wl += ([(SDB, SDB.n)] if (dbg and half == 0) else []) + \
                  ([dbg_h0w] if (dbg and half == 1 and dbg_h0w) else [])
            for dc in range(NDC):
                E("sync",
                  lambda e, dc=dc, half=half: e.dma_start(
                      ln2T.ap()[:, dc * 1024:(dc + 1) * 1024].rearrange(
                          "p (tl t) -> p tl t", t=128),
                      l2t_d[dc // 4, half * 8:(half + 1) * 8, :,
                            (dc % 4) * 128:(dc % 4 + 1) * 128].transpose([1, 0, 2])),
                  waits=(wl if dc == 0 else []), incs=[(SLN, 16)])
            sln_n = SLN.n
            E("vector", lambda e: e.memset(yac[0][:], 0.0),
              waits=(([(V, v_endA)] if half == 0 else [(M, m_endhalf)]) +
                     ([dbg_h0w] if (dbg and half == 1 and dbg_h0w) else [])),
              incs=[(V, 1)])

            for ee in range(NE):
                for i in range(2):
                    w = [(V, v_coefT)] + ps_wait(4 + i)
                    E("tensor",
                      lambda e, ee=ee, half=half, i=i: e.matmul(
                          P[4 + i][:], oh[0:8, ee * 128:(ee + 1) * 128],
                          coefT[0:8, half * 1024 + i * 512: half * 1024 + (i + 1) * 512],
                          start=True, stop=True),
                      waits=w, incs=[(M, 1)])
                    mi = M.n
                    E("vector",
                      lambda e, i=i: e.tensor_copy(c_b[:, i * 512:(i + 1) * 512], P[4 + i][:]),
                      waits=[(M, mi)] + ([dbg_cbw] if dbg_cbw else []), incs=[(V, 1)])
                    dbg_cbw = None
                    psfree[4 + i] = (V, V.n)
                v_cb_e = V.n
                if dbg and half == 0 and ee == 0:
                    E("sync", lambda e: e.dma_start(dbgt["cb0"], c_b[:]),
                      waits=[(V, v_cb_e)], incs=[(SDB, 16)])
                    dbg_cbw = (SDB, SDB.n)

                for fh in range(2):
                    step = ee * 2 + fh
                    ysrc, ydst = yac[step % 2], yac[(step + 1) % 2]
                    for fl in range(NFL):
                        fc = fh * NFL + fl
                        sl = fl % 2
                        w1w = [(SCG, consts_g)]
                        if half == 0 and ee == 0 and fc == 0:
                            w1w += [(V, v_endA), (A_, a_endA), (SCQ, SCQ.n)]
                        if ("w1", sl) in m_h_free:
                            w1w.append((M, m_h_free[("w1", sl)]))
                        E("gpsimd",
                          lambda e, ee=ee, fc=fc, sl=sl: e.dma_start(w1b[sl][:], w1_in[ee, fc]),
                          waits=w1w, incs=[(SW1[sl], 16)])
                        sw1_n = SW1[sl].n
                        w3w = []
                        if ("w3", sl) in m_h_free:
                            w3w.append((M, m_h_free[("w3", sl)]))
                        E("gpsimd",
                          lambda e, ee=ee, fc=fc, sl=sl: e.dma_start(w3b[sl][:], w3_in[ee, fc]),
                          waits=w3w, incs=[(SW3[sl], 16)])
                        sw3_n = SW3[sl].n
                        for tau in range(2):
                            PA, PB = P[2 * tau], P[2 * tau + 1]
                            wh = [(SW1[sl], sw1_n), (SLN, sln_n)]
                            if tau in a_silu:
                                wh.append((A_, a_silu[tau]))
                            else:
                                wh += ps_wait(2 * tau)
                            for dc in range(NDC):
                                E("tensor",
                                  lambda e, tau=tau, dc=dc, sl=sl, PA=PA: e.matmul(
                                      PA[:], w1b[sl][:, dc * 128:(dc + 1) * 128],
                                      ln2T[:, dc * 1024 + tau * 512:
                                           dc * 1024 + (tau + 1) * 512],
                                      start=(dc == 0), stop=(dc == NDC - 1)),
                                  waits=(wh if dc == 0 else []),
                                  incs=([(M, 1)] if dc == NDC - 1 else []))
                            m_h1 = M.n
                            if tau == 1:
                                m_h_free[("w1", sl)] = m_h1
                            ws = [(M, m_h1)]
                            if tau in v_hprime:
                                ws.append((V, v_hprime[tau]))
                            E("scalar",
                              lambda e, tau=tau, PA=PA: e.activation(s_tmp[tau][:], PA[:],
                                                                     AF.Silu),
                              waits=ws, incs=[(A_, 1)])
                            a_silu[tau] = A_.n
                            wh3 = [(SW3[sl], sw3_n)]
                            if tau in v_hprime:
                                wh3.append((V, v_hprime[tau]))
                            else:
                                wh3 += ps_wait(2 * tau + 1)
                            for dc in range(NDC):
                                E("tensor",
                                  lambda e, tau=tau, dc=dc, sl=sl, PB=PB: e.matmul(
                                      PB[:], w3b[sl][:, dc * 128:(dc + 1) * 128],
                                      ln2T[:, dc * 1024 + tau * 512:
                                           dc * 1024 + (tau + 1) * 512],
                                      start=(dc == 0), stop=(dc == NDC - 1)),
                                  waits=(wh3 if dc == 0 else []),
                                  incs=([(M, 1)] if dc == NDC - 1 else []))
                            m_h3 = M.n
                            if tau == 1:
                                m_h_free[("w3", sl)] = m_h3
                            E("vector",
                              lambda e, fl=fl, tau=tau, PB=PB: e.tensor_tensor(
                                  hP[:, fl * 1024 + tau * 512: fl * 1024 + (tau + 1) * 512],
                                  s_tmp[tau][:], PB[:], ALU.mult),
                              waits=[(A_, a_silu[tau]), (M, m_h3)], incs=[(V, 1)])
                            v_hprime[tau] = V.n
                    m_h_last = M.n
                    v_hp_all = V.n

                    for hc in range(NHC):
                        sl = hc % 2
                        w2w = []
                        if ("w2", sl) in m_w2_free:
                            w2w.append((M, m_w2_free[("w2", sl)]))
                        E("gpsimd",
                          lambda e, ee=ee, fh=fh, hc=hc, sl=sl: e.dma_start(
                              w2b[sl][:], w2_in[ee, fh, hc]),
                          waits=w2w, incs=[(SW2[sl], 16)])
                        sw2_n = SW2[sl].n
                        for tau in range(2):
                            PY = P[4 + tau]
                            wy = [(SW2[sl], sw2_n)]
                            if hc == 0:
                                wy.append((V, v_hp_all))
                            if tau in v_yev:
                                wy.append((V, v_yev[tau]))
                            else:
                                wy += ps_wait(4 + tau)
                            for fl in range(NFL):
                                E("tensor",
                                  lambda e, tau=tau, fl=fl, sl=sl, PY=PY: e.matmul(
                                      PY[:], w2b[sl][:, fl * 128:(fl + 1) * 128],
                                      hP[:, fl * 1024 + tau * 512:
                                         fl * 1024 + (tau + 1) * 512],
                                      start=(fl == 0), stop=(fl == NFL - 1)),
                                  waits=(wy if fl == 0 else []),
                                  incs=([(M, 1)] if fl == NFL - 1 else []))
                            m_y = M.n
                            if tau == 1:
                                m_w2_free[("w2", sl)] = m_y
                            E("vector",
                              lambda e, tau=tau, PY=PY: e.tensor_tensor(
                                  ye_t[tau][:], PY[:], c_b[:, tau * 512:(tau + 1) * 512],
                                  ALU.mult),
                              waits=[(M, m_y)], incs=[(V, 1)])
                            E("vector",
                              lambda e, hc=hc, tau=tau, ysrc=ysrc, ydst=ydst: e.tensor_tensor(
                                  ydst[:, hc * 1024 + tau * 512: hc * 1024 + (tau + 1) * 512],
                                  ysrc[:, hc * 1024 + tau * 512: hc * 1024 + (tau + 1) * 512],
                                  ye_t[tau][:], ALU.add),
                              incs=[(V, 1)])
                            v_yev[tau] = V.n
                    psfree[4] = (V, v_yev[0])
                    psfree[5] = (V, v_yev[1])
            v_yacc_done = V.n
            if dbg and half == 0:
                E("sync", lambda e: e.dma_start(dbgt["yach0"], yac[0][:]),
                  waits=[(V, v_yacc_done)], incs=[(SDB, 16)])
                E("sync", lambda e: e.dma_start(dbgt["hp"], hP[:]),
                  waits=[(V, v_yacc_done)], incs=[(SDB, 16)])
                dbg_h0w = (SDB, SDB.n)

            for tl in range(8):
                tt = half * 8 + tl
                sl = tl % 2
                if tl < 2:
                    if half == 0:
                        wx = [(V, a2c_lnb[NTT - 2 + tl])]
                    else:
                        wx = [(V, eh_out[(0, 6 + tl)])]
                else:
                    wx = [(V, eh_out[(half, tl - 2)])]
                E("sync",
                  lambda e, tt=tt, sl=sl: e.dma_start(xsl[sl][:], x1_d[tt]),
                  waits=wx, incs=[(SX[sl], 16)])
                sx_n = SX[sl].n
                wout = ([(SST, sst_slot[sl])] if sst_slot[sl] else [])
                for hg in range(NHC // 4):
                    pslot = hg % 2
                    for q in range(4):
                        hc = hg * 4 + q
                        w = [(V, v_yacc_done)] + (ps_wait(("T", pslot)) if q == 0 else [])
                        E("tensor",
                          lambda e, tl=tl, hc=hc, q=q, pslot=pslot: e.transpose(
                              PT[pslot][:, q * 128:(q + 1) * 128],
                              yac[0][:, hc * 1024 + tl * 128: hc * 1024 + (tl + 1) * 128],
                              idb[:]),
                          waits=w, incs=([(M, 1)] if q == 3 else []))
                    mt = M.n
                    E("vector",
                      lambda e, hg=hg, sl=sl, pslot=pslot: e.tensor_tensor(
                          out_t[sl][:, hg * 512:(hg + 1) * 512],
                          xsl[sl][:, hg * 512:(hg + 1) * 512],
                          PT[pslot][:, 0:512], ALU.add),
                      waits=[(M, mt), (SX[sl], sx_n)] + (wout if hg == 0 else []),
                      incs=[(V, 1)])
                    psfree[("T", pslot)] = (V, V.n)
                v_out = V.n
                eh_out[(half, tl)] = v_out
                E("scalar",
                  lambda e, tt=tt, sl=sl: e.dma_start(
                      out_d[tt * 128:(tt + 1) * 128, :], out_t[sl][:]),
                  waits=[(V, v_out)], incs=[(SST, 16)])
                sst_slot[sl] = SST.n
            m_endhalf = M.n

        if dbg:
            E("sync", lambda e: e.dma_start(dbgt["x1"], x1_d), waits=[(SCQ, scq_x1)],
              incs=[(SDB, 16)])
            E("sync", lambda e: e.dma_start(dbgt["l2t"], l2t_d),
              waits=[(ST2[0], scq_l2t[0]), (ST2[1], scq_l2t[1])], incs=[(SDB, 16)])
            E("sync", lambda e: e.dma_start(dbgt["yac"], yac[0][:]),
              waits=[(V, v_yacc_done)], incs=[(SDB, 16)])
        E("sync", lambda e: None, waits=[(SST, SST.n), (SDB, SDB.n)])
        sB.close()

        block = stk.enter_context(nc.Block())
        sch.build(block)

    if os.environ.get("KDBG"):
        import sys
        print("engine op counts:", sch.counts(), file=sys.stderr)
    return nc


_CACHE = {}


def _prep(W, b, gate_w, w1, w2, w3):
    W = np.asarray(W, dtype=np.float32)
    b = np.asarray(b, dtype=np.float32)
    gate_w = np.asarray(gate_w, dtype=np.float32)
    w1 = np.asarray(w1, dtype=np.float32)
    w2 = np.asarray(w2, dtype=np.float32)
    w3 = np.asarray(w3, dtype=np.float32)
    wp = np.stack([np.concatenate([W[:, i * 512:(i + 1) * 512],
                                   W[:, 2048 + i * 512: 2048 + (i + 1) * 512]], axis=1)
                   for i in range(4)])
    wp = np.ascontiguousarray(
        wp.reshape(4, 16, 128, 1024).transpose(0, 2, 1, 3).reshape(4, 128, 16 * 1024))
    bp = np.concatenate([np.concatenate([b[i * 512:(i + 1) * 512],
                                         b[2048 + i * 512: 2048 + (i + 1) * 512]])
                         for i in range(4)]).reshape(1, 4096)
    gwT = np.ascontiguousarray(
        gate_w.T.reshape(16, 128, 8).transpose(1, 0, 2).reshape(128, 128))

    def prep_w13(w):
        t = w.transpose(0, 2, 1)
        t = t.reshape(NE, 16, 128, 32, 128).transpose(0, 3, 2, 1, 4)
        return np.ascontiguousarray(t.reshape(NE, 32, 128, 2048))
    w1p = prep_w13(w1)
    w3p = prep_w13(w3)
    t = w2.transpose(0, 2, 1)
    t = t.reshape(NE, 2, 16, 128, 16, 128)
    t = t.transpose(0, 1, 4, 3, 2, 5)
    w2p = np.ascontiguousarray(t.reshape(NE, 2, NHC, 128, 2048))
    pos = np.arange(16384, dtype=np.float32)
    inv_freq = (1.0 / (10000.0 ** (np.arange(0, 1024, 2, dtype=np.float32) / 1024.0))
                ).astype(np.float32)
    freqs = (pos[:, None] * inv_freq[None, :]).astype(np.float32)
    cosf = np.cos(freqs).astype(np.float32)
    sinf = np.sin(freqs).astype(np.float32)
    return wp, bp, gwT, w1p, w2p, w3p, cosf, sinf


def kernel(L0_attn_X, L0_attn_W, L0_attn_b, L0_ffn_gate_w, L0_ffn_w1, L0_ffn_w2, L0_ffn_w3):
    X = np.ascontiguousarray(np.asarray(L0_attn_X, dtype=np.float32).reshape(16384, 2048))
    wp, bp, gwT, w1p, w2p, w3p, cosf, sinf = _prep(
        L0_attn_W, L0_attn_b, L0_ffn_gate_w, L0_ffn_w1, L0_ffn_w2, L0_ffn_w3)

    idb = np.eye(128).astype(ml_dtypes.bfloat16)
    idf = np.eye(128, dtype=np.float32)
    onc = np.ones((128, 1), dtype=np.float32)
    onr = np.ones((1, 128), dtype=np.float32)
    onb = np.ones((1, 128), dtype=ml_dtypes.bfloat16)
    oh = np.zeros((8, 8 * 128), dtype=np.float32)
    for e in range(8):
        oh[e, e * 128:(e + 1) * 128] = 1.0
    oh = oh.astype(ml_dtypes.bfloat16)

    if "nc" not in _CACHE:
        _CACHE["nc"] = build_nc()
    nc = _CACHE["nc"]

    in_maps = []
    for c in range(NCORES):
        rows = slice(c * T, (c + 1) * T)
        cs_c = np.ascontiguousarray(
            cosf[rows].reshape(NTT, 128, 2, 256).transpose(2, 0, 1, 3))
        sn_c = np.ascontiguousarray(
            sinf[rows].reshape(NTT, 128, 2, 256).transpose(2, 0, 1, 3))
        in_maps.append({
            "x0_in": np.ascontiguousarray(X[rows]), "wp_in": wp, "b_in": bp, "gw_in": gwT,
            "w1_in": w1p, "w3_in": w3p, "w2_in": w2p,
            "cs_in": cs_c, "sn_in": sn_c,
            "idb_in": idb, "idf_in": idf, "onc_in": onc, "onr_in": onr,
            "onb_in": onb, "oh_in": oh,
        })
    res = run_bass_kernel_spmd(nc, in_maps, core_ids=list(range(NCORES)))
    _CACHE["res"] = res
    out = np.concatenate([np.asarray(res.results[c]["out"]) for c in range(NCORES)], axis=0)
    return out.reshape(1, 16384, 2048).astype(np.float32)
